# version 08
# baseline (speedup 1.0000x reference)
import numpy as np
import ml_dtypes
from contextlib import ExitStack

import concourse.bass as bass
import concourse.mybir as mybir
from concourse.bass_utils import run_bass_kernel_spmd

F32 = mybir.dt.float32
BF16 = mybir.dt.bfloat16
AF = mybir.ActivationFunctionType
ALU = mybir.AluOpType
AX = mybir.AxisListType

ENGINES = ("tensor", "vector", "scalar", "gpsimd", "sync")



def _rk(key):
    if isinstance(key, tuple):
        t, tag = key
        return (t if isinstance(t, str) else t.name, tag)
    return (key if isinstance(key, str) else key.name, "")


class _Op:
    __slots__ = ("engine", "fn", "pos", "signal", "rank", "waits", "know",
                 "dma_sem", "dma_val", "tag")

    def __init__(self, engine, fn, pos, tag=""):
        self.engine = engine
        self.fn = fn
        self.pos = pos
        self.signal = False
        self.rank = None
        self.waits = []
        self.know = None
        self.dma_sem = None
        self.dma_val = None
        self.tag = tag


class Prog:

    def __init__(self, nc):
        self.nc = nc
        self.ops = {e: [] for e in ENGINES}
        self.know = {e: {} for e in ENGINES}
        self.dma_counts = {}
        self.regions = {}
        self.dma_sem_names = []
        self.last_dma = {}
        self._emitted = False


    def _tok_know(self, op):
        k = dict(self.know[op.engine])
        k[("e", op.engine)] = op.pos
        if op.dma_sem is not None:
            k[("d", op.dma_sem)] = op.dma_val
        return k

    def _covered(self, engine, dep):
        know = self.know[engine]
        if isinstance(dep, _Op):
            if dep.dma_sem is not None:
                return know.get(("d", dep.dma_sem), -1) >= dep.dma_val
            return know.get(("e", dep.engine), -1) >= dep.pos
        raise TypeError(dep)

    def _absorb(self, engine, dep):
        know = self.know[engine]
        for k, v in dep.know.items():
            if know.get(k, -1) < v:
                know[k] = v

    def op(self, engine, fn, deps=(), reads=(), writes=(), dma_sem=None,
           tag=""):
        assert not self._emitted
        reads = [_rk(r) for r in reads]
        writes = [_rk(w) for w in writes]
        alldeps = list(deps)
        for r in reads:
            reg = self.regions.get(r)
            if reg and reg[0] is not None:
                alldeps.append(reg[0])
        for w in writes:
            reg = self.regions.get(w)
            if reg:
                if reg[0] is not None:
                    alldeps.append(reg[0])
                alldeps.extend(reg[1])

        pos = len(self.ops[engine])
        o = _Op(engine, fn, pos, tag)
        if dma_sem is not None:
            self.dma_counts.setdefault(dma_sem, 0)
            if dma_sem not in self.dma_sem_names:
                self.dma_sem_names.append(dma_sem)
            self.dma_counts[dma_sem] += 16
            o.dma_sem = dma_sem
            o.dma_val = self.dma_counts[dma_sem]
            self.last_dma[dma_sem] = o

        for dep in alldeps:
            if dep is o:
                continue
            assert isinstance(dep, _Op)
            if dep.dma_sem is None and dep.engine == engine \
                    and engine == "tensor":
                self._absorb(engine, dep)
                continue
            if self._covered(engine, dep):
                continue
            if dep.dma_sem is None:
                dep.signal = True
            o.waits.append(dep)
            self._absorb(engine, dep)

        o.know = self._tok_know(o)
        self.ops[engine].append(o)

        for r in reads:
            reg = self.regions.setdefault(r, [None, []])
            reg[1].append(o)
        for w in writes:
            self.regions[w] = [o, []]
        return o

    def dma(self, engine, fn, sem, deps=(), reads=(), writes=()):
        return self.op(engine, fn, deps=deps, reads=reads, writes=writes,
                       dma_sem=sem)

    def finalize(self):
        deps = [self.last_dma[s] for s in self.dma_sem_names]
        for e in ENGINES:
            if self.ops[e]:
                deps.append(self.ops[e][-1])
        self.op("sync", None, deps=deps)

    def barrier_all(self, engines=ENGINES):
        toks = []
        for e in engines:
            if self.ops[e]:
                toks.append(self.ops[e][-1])
        toks += [self.last_dma[s] for s in self.dma_sem_names]
        outs = []
        for e in engines:
            o = self.op(e, None, deps=toks)
            outs.append(o)
        return outs

    def emit(self, extra_final=None):
        assert not self._emitted
        self._emitted = True
        nc = self.nc
        es = ExitStack()
        with es:
            esems = {}
            for e in ENGINES:
                if any(o.signal for o in self.ops[e]):
                    esems[e] = es.enter_context(nc.semaphore(f"se_{e}"))
            dsems = {}
            for s in self.dma_sem_names:
                dsems[s] = es.enter_context(nc.semaphore(f"sd_{s}"))
            rank_of = {}
            for e in ENGINES:
                r = 0
                for o in self.ops[e]:
                    if o.signal:
                        r += 1
                        o.rank = r
                    rank_of[(e, o.pos)] = r
            block = es.enter_context(nc.Block())

            def emit_stream(eng_obj, ename):
                for o in self.ops[ename]:
                    for dep in o.waits:
                        if dep.dma_sem is not None:
                            eng_obj.wait_ge(dsems[dep.dma_sem], dep.dma_val)
                        else:
                            v = rank_of[(dep.engine, dep.pos)]
                            assert v > 0
                            eng_obj.wait_ge(esems[dep.engine], v)
                    ins = o.fn(eng_obj) if o.fn is not None else None
                    if o.dma_sem is not None:
                        assert ins is not None
                        ins.then_inc(dsems[o.dma_sem], 16)
                    elif o.signal:
                        if ins is None:
                            ins = eng_obj.nop()
                        ins.then_inc(esems[ename], 1)

            @block.tensor
            def _(t):
                emit_stream(t, "tensor")

            @block.vector
            def _(v):
                emit_stream(v, "vector")

            @block.scalar
            def _(s):
                emit_stream(s, "scalar")

            @block.gpsimd
            def _(g):
                emit_stream(g, "gpsimd")

            @block.sync
            def _(sy):
                emit_stream(sy, "sync")
                if extra_final is not None:
                    extra_final(sy)


def _ceil_div(a, b):
    return -(-a // b)


def host_tile_w(w):
    din, dout = w.shape
    nk, nm = _ceil_div(din, 128), _ceil_div(dout, 128)
    out = np.zeros((nm, nk, 128, 128), np.float32)
    for m in range(nm):
        for k in range(nk):
            blk = w[k * 128:(k + 1) * 128, m * 128:(m + 1) * 128]
            out[m, k, :blk.shape[0], :blk.shape[1]] = blk
    return out


def host_bias_cols(b):
    nm = _ceil_div(b.shape[0], 128)
    out = np.zeros((nm, 128), np.float32)
    out.reshape(-1)[:b.shape[0]] = b
    return np.ascontiguousarray(out.T)


def host_rhs(w_eff):
    din, dout = w_eff.shape
    return np.ascontiguousarray(
        w_eff.reshape(din // 128, 128, dout).transpose(1, 0, 2))


def host_fm(x):
    T, D = x.shape
    nc_ = D // 128
    return np.ascontiguousarray(x.T.reshape(nc_, 128, T).transpose(1, 0, 2))


def host_unfm(x_fm):
    p, nch, T = x_fm.shape
    return np.ascontiguousarray(x_fm.transpose(1, 0, 2).reshape(nch * 128, T).T)



class KB:

    def __init__(self, T=2048, name="k"):
        self.nc = bass.Bass("TRN2", target_bir_lowering=False)
        self.p = Prog(self.nc)
        self.es = ExitStack()
        self.T = T
        self.inputs = {}
        self.sb_handles = {}
        self._dma_rr = 0
        self._psum_used = 0
        self.prefix = ""

    from contextlib import contextmanager as _cm

    @_cm
    def scope(self, prefix):
        old_prefix, old_es = self.prefix, self.es
        self.prefix = old_prefix + prefix + "_"
        es = ExitStack()
        self.es = es
        try:
            with es:
                yield
        finally:
            self.es = old_es
            self.prefix = old_prefix

    def dram_in(self, name, shape, dtype=F32):
        return self.nc.dram_tensor(name, list(shape), dtype,
                                   kind="ExternalInput").ap()

    def dram_out(self, name, shape, dtype=F32):
        return self.nc.dram_tensor(name, list(shape), dtype,
                                   kind="ExternalOutput").ap()

    def sb(self, name, shape, dtype):
        name = self.prefix + name
        h = self.es.enter_context(self.nc.sbuf_tensor(name, list(shape), dtype))
        self.sb_handles[name] = h
        return h

    def ps(self, name, shape, dtype=F32):
        return self.es.enter_context(
            self.nc.psum_tensor(self.prefix + name, list(shape), dtype))

    def setup_common(self, npb=6, stats=True):
        nc, p = self.nc, self.p
        self.ones_col_f = self.sb("ones_col_f", [128, 1], F32)
        self.ones_row_f = self.sb("ones_row_f", [1, 128], F32)
        self.ident_b = self.sb("ident_b", [128, 128], BF16)
        self.ident_f = self.sb("ident_f", [128, 128], F32)
        t1 = p.op("gpsimd", lambda g: g.memset(self.ones_col_f[:], 1.0),
                  writes=[(self.ones_col_f, "")])
        t2 = p.op("gpsimd", lambda g: g.memset(self.ones_row_f[:], 1.0),
                  writes=[(self.ones_row_f, "")])
        for ident in (self.ident_b, self.ident_f):
            tz = p.op("gpsimd", lambda g, i=ident: g.memset(i[:], 0.0),
                      writes=[(ident, "")])
            p.op("gpsimd",
                 lambda g, i=ident: g.affine_select(
                     out=i[:], in_=i[:], compare_op=ALU.not_equal, fill=1.0,
                     base=0, pattern=[[-1, 128]], channel_multiplier=1),
                 reads=[(ident, "")], writes=[(ident, "")])
        self.pb = [self.ps(f"pb{i}", [128, 512], F32) for i in range(npb)]
        if stats:
            self.pstat1 = self.ps("pstat1", [1, 512], F32)
            self.pstat2 = self.ps("pstat2", [1, 512], F32)


    def load_const(self, name, dram_ap, shape, dtype, sem="dconst"):
        t = self.sb(name, shape, dtype)
        tok = self.p.dma("sync", lambda e: e.dma_start(out=t[:], in_=dram_ap),
                         sem=sem, writes=[(t, "")])
        return t, tok

    def wstream_load(self, wbuf, w_dram_ap, m, nk, sem):
        src = w_dram_ap[m].rearrange("k p m -> p k m")
        dst = wbuf[:, :nk * 128].rearrange("p (k m) -> p k m", m=128)
        return self.p.dma("gpsimd",
                          lambda e: e.dma_start(out=dst, in_=src),
                          sem=sem, writes=[(wbuf, "")])

    def matmul_acc(self, psum_ap, pairs, psum_region, extra_reads=(),
                   deps=()):
        p = self.p
        toks = []
        n = len(pairs)
        for i, (lhsT, rhs, reads) in enumerate(pairs):
            tok = p.op("tensor",
                       lambda t, l=lhsT, r=rhs, s=(i == 0), e=(i == n - 1):
                       t.matmul(psum_ap, l, r, start=s, stop=e),
                       deps=deps if i == 0 else (),
                       reads=list(reads) + list(extra_reads),
                       writes=[psum_region] if i == 0 else [],
                       )
            toks.append(tok)
        p.regions[_rk(psum_region)] = [toks[-1], []]
        return toks[-1]

    def emit_ln_fm(self, x_sb, out_sb, nch, Tex, x_dtype,
                   gamma_beta=None, eps=1e-5, D=None, sq_sb=None,
                   stat_sb=None, ones_col_b=None, pbase=0, offs=None,
                   x_stride=None, out_stride=None, out_off=0):
        p = self.p
        offs = offs or [0] * nch
        xs = x_stride or Tex
        os_ = out_stride or Tex
        D = D or nch * 128
        tc = 512
        sq = sq_sb
        mu, rstd, st3 = stat_sb
        ones_f = self.ones_col_f
        for t0 in range(0, Tex, tc):
            n = min(tc, Tex - t0)
            s1 = self.pstat1
            s2 = self.pstat2
            pairs1 = []
            oc = ones_f if x_dtype == F32 else ones_col_b
            for c in range(nch):
                xap = x_sb[:, c * xs + offs[c] + t0: c * xs + offs[c] + t0 + n]
                pairs1.append((oc[:, 0:1], xap, [(x_sb, ""), (oc, "")]))
            tok_s1 = self.matmul_acc(s1[:, :n], pairs1, (s1, ""))
            for c in range(nch):
                xap = x_sb[:, c * xs + offs[c] + t0: c * xs + offs[c] + t0 + n]
                p.op("scalar",
                     lambda s, xap=xap, c=c, n=n:
                     s.activation(sq[:, (c % 2) * tc: (c % 2) * tc + n],
                                  xap, AF.Square),
                     reads=[(x_sb, "")], writes=[(sq, c % 2)])
                sqap = sq[:, (c % 2) * tc: (c % 2) * tc + n]
                tok_s2 = p.op("tensor",
                              lambda t, sqap=sqap, s=(c == 0), e=(c == nch - 1),
                              n=n:
                              t.matmul(s2[:, :n], ones_f[:, 0:1], sqap,
                                       start=s, stop=e),
                              reads=[(sq, c % 2), (ones_f, "")],
                              writes=[(s2, "")] if c == 0 else [])
            p.regions[_rk((s2, ""))] = [tok_s2, []]
            t_mu = p.op("vector",
                        lambda v, n=n: v.tensor_scalar_mul(mu[:, :n], s1[:, :n],
                                                           1.0 / D),
                        deps=[tok_s1], reads=[(s1, "")], writes=[(mu, "")])
            t_e2 = p.op("vector",
                        lambda v, n=n: v.tensor_scalar_mul(st3[:, :n],
                                                           s2[:, :n], 1.0 / D),
                        deps=[tok_s2], reads=[(s2, "")], writes=[(st3, "")])
            t_m2 = p.op("vector",
                        lambda v, n=n: v.tensor_mul(rstd[:, :n], mu[:, :n],
                                                    mu[:, :n]),
                        reads=[(mu, "")], writes=[(rstd, "")])
            t_v = p.op("vector",
                       lambda v, n=n: v.scalar_tensor_tensor(
                           rstd[:, :n], st3[:, :n], float(eps), rstd[:, :n],
                           ALU.add, ALU.subtract),
                       reads=[(st3, ""), (rstd, "")], writes=[(rstd, "")])
            t_sd = p.op("scalar",
                        lambda s, n=n: s.activation(rstd[:, :n], rstd[:, :n],
                                                    AF.Sqrt),
                        reads=[(rstd, "")], writes=[(rstd, "")])
            t_r = p.op("vector",
                       lambda v, n=n: v.reciprocal(rstd[:, :n], rstd[:, :n]),
                       reads=[(rstd, "")], writes=[(rstd, "")])
            bmu = self.pb[pbase]
            brs = self.pb[pbase + 1]
            tok_bmu = self.matmul_acc(bmu[:, :n],
                                      [(self.ones_row_f[0:1, :],
                                        mu[:, :n],
                                        [(mu, ""), (self.ones_row_f, "")])],
                                      (bmu, ""))
            tok_brs = self.matmul_acc(brs[:, :n],
                                      [(self.ones_row_f[0:1, :],
                                        rstd[:, :n],
                                        [(rstd, ""), (self.ones_row_f, "")])],
                                      (brs, ""))
            for c in range(nch):
                xap = x_sb[:, c * xs + offs[c] + t0: c * xs + offs[c] + t0 + n]
                oap = out_sb[:, c * os_ + out_off + t0:
                             c * os_ + out_off + t0 + n]
                tsub = p.op("vector",
                            lambda v, xap=xap, c=c, n=n:
                            v.tensor_sub(sq[:, (c % 2) * tc: (c % 2) * tc + n],
                                         xap, bmu[:, :n]),
                            deps=[tok_bmu], reads=[(x_sb, ""), (bmu, "")],
                            writes=[(sq, c % 2)])
                if gamma_beta is None:
                    p.op("vector",
                         lambda v, oap=oap, c=c, n=n:
                         v.tensor_mul(oap,
                                      sq[:, (c % 2) * tc: (c % 2) * tc + n],
                                      brs[:, :n]),
                         deps=[tok_brs], reads=[(sq, c % 2), (brs, "")],
                         writes=[(out_sb, "")])
                else:
                    gb_sb, goff = gamma_beta
                    p.op("vector",
                         lambda v, c=c, n=n:
                         v.tensor_mul(sq[:, (c % 2) * tc: (c % 2) * tc + n],
                                      sq[:, (c % 2) * tc: (c % 2) * tc + n],
                                      brs[:, :n]),
                         deps=[tok_brs], reads=[(sq, c % 2), (brs, "")],
                         writes=[(sq, c % 2)])
                    p.op("vector",
                         lambda v, oap=oap, c=c, n=n, gb=gb_sb, go=goff:
                         v.tensor_scalar(oap,
                                         sq[:, (c % 2) * tc: (c % 2) * tc + n],
                                         gb[:, go + c: go + c + 1],
                                         gb[:, go + nch + c: go + nch + c + 1],
                                         ALU.mult, ALU.add),
                         reads=[(sq, c % 2), (gb_sb, "")],
                         writes=[(out_sb, "")])

    def emit_project_fm(self, out_sb, x_sb, w_dram, nm, nk, Tex, evac,
                        wbufs, wsem, tc=512, pbase=2, npb=2):
        p = self.p
        for m in range(nm):
            wb = wbufs[m % len(wbufs)]
            self.wstream_load(wb, w_dram, m, nk, f"{wsem}{m % len(wbufs)}")
            for ti, t0 in enumerate(range(0, Tex, tc)):
                n = min(tc, Tex - t0)
                pbi = self.pb[pbase + (m * ((Tex + tc - 1) // tc) + ti) % npb]
                pairs = []
                for k in range(nk):
                    pairs.append((wb[:, k * 128:(k + 1) * 128],
                                  x_sb[:, k * Tex + t0: k * Tex + t0 + n],
                                  [(wb, ""), (x_sb, "")]))
                self.matmul_acc(pbi[:, :n], pairs, (pbi, ""))
                evac(m, t0, n, pbi)

    def mk_evac_store(self, obufs, out_dram, Tex, func=AF.Copy, scale=1.0,
                      bias_sb=None, bias_off=0, osem="dout", engine="scalar",
                      nm_total=None):
        p = self.p

        def evac(m, t0, n, psum):
            ob = obufs[m % len(obufs)]
            if engine == "scalar":
                bias = 0.0 if bias_sb is None else \
                    bias_sb[:, bias_off + m: bias_off + m + 1]
                rd = [(psum, "")] + ([] if bias_sb is None else [(bias_sb, "")])
                p.op("scalar",
                     lambda s: s.activation(ob[:, t0:t0 + n], psum[:, :n],
                                            func, bias=bias, scale=scale),
                     reads=rd, writes=[(ob, "")])
            else:
                assert bias_sb is None and func == AF.Copy
                p.op("vector",
                     lambda v: v.tensor_scalar_mul(ob[:, t0:t0 + n],
                                                   psum[:, :n], scale),
                     reads=[(psum, "")], writes=[(ob, "")])
            if t0 + n >= Tex and out_dram is not None:
                p.dma("sync",
                      lambda e: e.dma_start(out=out_dram[:, m, :],
                                            in_=ob[:, :Tex]),
                      sem=f"{osem}{m % len(obufs)}", reads=[(ob, "")])
        return evac



def build_pre0(T):
    K = KB(T=T)
    io = {}
    io["x"] = K.dram_in("x", [128, 8 * T], F32)
    io["wq"] = K.dram_in("wq", [8, 8, 128, 128], F32)
    io["wk"] = K.dram_in("wk", [8, 8, 128, 128], F32)
    io["wv"] = K.dram_in("wv", [8, 8, 128, 128], F32)
    io["we"] = K.dram_in("we", [1, 8, 128, 128], F32)
    io["gb"] = K.dram_in("gb", [128, 16], F32)
    io["q"] = K.dram_out("q", [128, 8, T], BF16)
    io["k"] = K.dram_out("k", [128, 8, T], BF16)
    io["v"] = K.dram_out("v", [128, 8, T], BF16)
    io["ctx"] = K.dram_out("ctx", [128, 1, T], BF16)
    io["h"] = K.dram_out("h", [128, 8, T], BF16)
    with K.es:
        K.setup_common(npb=5, stats=False)
        emit_pre0(K, io)
        K.p.finalize()
        K.p.emit(None)
    return K


def emit_pre0(K, io):
    T = K.T
    p = K.p
    x_d, wq_d, wk_d, wv_d, we_d, gb_d = (
        io["x"], io["wq"], io["wk"], io["wv"], io["we"], io["gb"])
    q_o, k_o, v_o, c_o, h_o = io["q"], io["k"], io["v"], io["ctx"], io["h"]
    with K.scope("p0"):
        K.pstat1 = K.ps("pstat1", [1, 512], F32)
        K.pstat2 = K.ps("pstat2", [1, 512], F32)
        xt = K.sb("xt", [128, 8 * T], F32)
        xn = K.sb("xn", [128, 8 * T], BF16)
        hh = K.sb("hh", [128, 8 * T], BF16)
        sq = K.sb("sq", [128, 1024], F32)
        st = [K.sb(f"st{i}", [1, 512], F32) for i in range(3)]
        ocb = K.sb("ocb", [128, 1], BF16)
        gb = K.sb("gbs", [128, 16], F32)
        wbufs = [K.sb(f"wbuf{i}", [128, 8 * 128], BF16) for i in range(2)]
        obufs = [K.sb(f"obuf{i}", [128, T], BF16) for i in range(2)]

        p.op("gpsimd", lambda g: g.memset(ocb[:], 1.0), writes=[(ocb, "")])
        p.dma("sync", lambda e: e.dma_start(out=gb[:], in_=gb_d),
              sem="dconst", writes=[(gb, "")])
        p.dma("sync", lambda e: e.dma_start(out=xt[:], in_=x_d),
              sem="dx", writes=[(xt, "")])

        K.emit_ln_fm(xt, xt, 8, T, F32, sq_sb=sq, stat_sb=st,
                     ones_col_b=ocb, pbase=0)
        p.op("scalar", lambda s: s.activation(xn[:], xt[:], AF.Copy),
             reads=[(xt, "")], writes=[(xn, "")])
        K.emit_ln_fm(xt, hh, 8, T, F32, gamma_beta=(gb, 0), sq_sb=sq,
                     stat_sb=st, ones_col_b=ocb, pbase=0)
        p.dma("sync", lambda e: e.dma_start(out=h_o[:, :, :].rearrange(
            "p m t -> p (m t)"), in_=hh[:]), sem="douth", reads=[(hh, "")])

        for (wd, od, scale) in ((wq_d, q_o, 0.125), (wk_d, k_o, 1.0),
                                (wv_d, v_o, 1.0)):
            evac = K.mk_evac_store(obufs, od, T, func=AF.Copy, scale=scale)
            K.emit_project_fm(None, xn, wd, 8, 8, T, evac,
                              wbufs, "w", pbase=2, npb=2)
        evac = K.mk_evac_store(obufs, c_o, T, func=AF.Copy, scale=1.0)
        K.emit_project_fm(None, xn, we_d, 1, 8, T, evac, wbufs, "w",
                          pbase=2, npb=2)
    p.barrier_all()


def build_attn0(T, S):
    K = KB(T=T)
    io = {}
    io["q"] = K.dram_in("q", [128, 8, T], BF16)
    io["kf"] = K.dram_in("kf", [128, 8, S], BF16)
    io["vf"] = K.dram_in("vf", [128, 8, S], BF16)
    io["cf"] = K.dram_in("cf", [128, 1, S], BF16)
    io["cq"] = K.dram_in("cq", [128, 1, T], BF16)
    io["wo"] = K.dram_in("wo", [8, 8, 128, 128], F32)
    io["x"] = K.dram_in("x", [128, 8 * T], F32)
    io["xa"] = K.dram_out("xa", [128, 8 * T], F32)
    with K.es:
        K.setup_common(npb=5, stats=False)
        emit_attn(K, io, S, influence=True, masked=False, scope="a0")
        K.p.finalize()
        K.p.emit()
    return K


def build_attn1(T, S):
    K = KB(T=T)
    io = {}
    io["q"] = K.dram_in("q", [128, 8, T], BF16)
    io["kf"] = K.dram_in("kf", [128, 8, S], BF16)
    io["vf"] = K.dram_in("vf", [128, 8, S], BF16)
    io["cmask"] = K.dram_in("cmask", [T // 512, S // 128, 128, 512], BF16)
    io["wo"] = K.dram_in("wo", [8, 8, 128, 128], F32)
    io["x"] = K.dram_in("x", [128, 8 * T], F32)
    io["xa"] = K.dram_out("xa", [128, 8 * T], F32)
    with K.es:
        K.setup_common(npb=5, stats=False)
        emit_attn(K, io, S, influence=False, masked=True, scope="a1")
        K.p.finalize()
        K.p.emit()
    return K


def emit_attn(K, io, S, influence, masked, scope, NH=16, debug_stop=None,
              gathered=False):
    T = K.T
    p = K.p
    q_d, kf_d, vf_d = io["q"], io["kf"], io["vf"]
    if influence:
        cf_d, cq_d = io["cf"], io["cq"]
    if masked:
        cm_d = io["cmask"]
    wo_d, x_d, xa_o = io["wo"], io["x"], io["xa"]
    NKC = S // 512
    NKT = S // 128
    NQT = T // 128
    NQB = T // 512

    def kv_slab(d, r0, c0):
        if not gathered:
            return [(d[r0:r0 + 64, c0, :], 0, S)]
        return [(d[r * 128 + r0: r * 128 + r0 + 64,
                   c0 * T:(c0 + 1) * T], r * T, T) for r in range(4)]

    with K.scope(scope):
        pb5 = K.ps("pb5", [128, 512], F32)
        q_sb = K.sb("q_sb", [128, 8 * T], BF16)
        xbv = [K.sb(f"xbv{i}", [128, T], F32) for i in range(2)]
        hb = K.sb("hb", [128, 8 * T], BF16)
        if influence:
            ck = K.sb("ck", [128, S], BF16)
            cqs = K.sb("cqs", [128, T], BF16)
        if masked:
            mbuf = [K.sb(f"mbuf{i}", [128, 512], BF16) for i in range(2)]
        kt_b = [K.sb(f"kt{i}", [65, S], BF16) for i in range(2)]
        vt_b = [K.sb(f"vt{i}", [128, NKT * 65], BF16) for i in range(2)]
        vtmp = [K.sb(f"vtmp{i}", [64, S], BF16) for i in range(2)]
        qt_b = [K.sb(f"qt{i}", [65, T], BF16) for i in range(2)]
        pbuf = [K.sb(f"pbuf{i}", [128, 512], BF16) for i in range(2)]
        am = K.sb("am", [128, NQT + 1], F32)
        mrow = K.sb("mrow", [1, 512], F32)
        rsum = K.sb("rsum", [1, 512], F32)
        rb = K.sb("rb", [64, 512], F32)
        wbufs = [K.sb(f"wbuf{i}", [128, 8 * 128], BF16) for i in range(2)]
        ptr = K.ps("ptr", [128, 512], BF16)
        psm = K.ps("psm", [1, 512], F32)

        p.dma("sync", lambda e: e.dma_start(
            out=q_sb[:], in_=q_d.rearrange("p m t -> p (m t)")), sem="din",
            writes=[(q_sb, "")])
        if influence:
            if not gathered:
                p.dma("sync", lambda e: e.dma_start(
                    out=ck[:], in_=cf_d.rearrange("p m t -> p (m t)")),
                    sem="dinc", writes=[(ck, "")])
            else:
                for r in range(4):
                    p.dma("sync", lambda e, r=r: e.dma_start(
                        out=ck[:, r * T:(r + 1) * T],
                        in_=cf_d[r * 128:(r + 1) * 128, 0:T]),
                        sem="dinc", writes=[(ck, "")])
            tcq = p.dma("sync", lambda e: e.dma_start(
                out=cqs[:], in_=cq_d.rearrange("p m t -> p (m t)")),
                sem="dinc2", writes=[(cqs, "")])
            p.op("vector", lambda v: v.tensor_scalar_mul(cqs[:], cqs[:], 0.1),
                 deps=[tcq], reads=[(cqs, "")], writes=[(cqs, "")])
        for i in range(2):
            p.op("gpsimd", lambda g, i=i: g.memset(kt_b[i][64:65, :], 1.0),
                 writes=[(kt_b[i], "ones")])
            vap = vt_b[i].ap().rearrange("p (t c) -> p t c", c=65)
            p.op("gpsimd", lambda g, i=i, vap=vap: g.memset(vap[:, :, 64:65],
                                                            1.0),
                 writes=[(vt_b[i], "ones")])

        def load_head(h):
            par = h % 2
            kt, vt, vq, qt = kt_b[par], vt_b[par], vtmp[par], qt_b[par]
            r0 = (h % 2) * 64
            c0 = h // 2
            for (srcap, off, ln) in kv_slab(kf_d, r0, c0):
                p.dma("gpsimd", lambda e, srcap=srcap, off=off, ln=ln:
                      e.dma_start(out=kt[0:64, off:off + ln], in_=srcap),
                      sem=f"dk{par}", writes=[(kt, "")])
            if debug_stop == "kt":
                return
            for (srcap, off, ln) in kv_slab(vf_d, r0, c0):
                p.dma("gpsimd", lambda e, srcap=srcap, off=off, ln=ln:
                      e.dma_start(out=vq[0:64, off:off + ln], in_=srcap),
                      sem=f"dv{par}", writes=[(vq, "")])
            if debug_stop == "vq":
                return
            for g4 in range(0 if debug_stop == "vt0" else NKT // 4):
                for u in range(4):
                    kt_i = g4 * 4 + u
                    p.op("tensor",
                         lambda t, kt_i=kt_i, u=u:
                         t.transpose(ptr[:, u * 128: u * 128 + 64],
                                     vq[0:64, kt_i * 128:(kt_i + 1) * 128],
                                     K.ident_b[0:64, 0:64]),
                         reads=[(vq, ""), (K.ident_b, "")],
                         writes=[(ptr, "")])
                src4 = ptr.ap().rearrange("p (q c) -> p q c", c=128)[:, :, 0:64]
                dst4 = vt.ap().rearrange("p (t c) -> p t c", c=65)[
                    :, g4 * 4:(g4 + 1) * 4, 0:64]
                p.op("vector",
                     lambda v, src4=src4, dst4=dst4:
                     v.tensor_copy(dst4, src4),
                     reads=[(ptr, "")], writes=[(vt, "")])
            if debug_stop == "vt":
                return
            p.dma("sync", lambda e: e.dma_start(
                out=qt[0:64, :],
                in_=q_sb[r0:r0 + 64, c0 * T:(c0 + 1) * T]),
                sem=f"dq{par}", reads=[(q_sb, "")], writes=[(qt, "lo")])

        def pass_a(h):
            par = h % 2
            kt = kt_b[par]
            qt = qt_b[par]
            for qt_i in range(NQT):
                lq = qt[0:64, qt_i * 128: qt_i * 128 + 128]
                lc = cqs[:, qt_i * 128: qt_i * 128 + 128] if influence else None
                for kc in range(NKC):
                    ps = K.pb[kc % 2]
                    rk = kt[0:64, kc * 512:(kc + 1) * 512]
                    mtok = p.op("tensor", lambda t, lq=lq, rk=rk, ps=ps,
                                e=not influence:
                                t.matmul(ps[:, :512], lq, rk, start=True,
                                         stop=e),
                                reads=[(qt, "lo"), (kt, "")], writes=[(ps, "")])
                    if influence:
                        rc = ck[:, kc * 512:(kc + 1) * 512]
                        mtok = p.op("tensor", lambda t, lc=lc, rc=rc, ps=ps:
                                    t.matmul(ps[:, :512], lc, rc, start=False,
                                             stop=True),
                                    reads=[(cqs, ""), (ck, "")])
                    p.regions[_rk((ps, ""))] = [mtok, []]
                    if kc == 0:
                        p.op("vector", lambda v, ps=ps, qt_i=qt_i:
                             v.reduce_max(am[:, qt_i:qt_i + 1], ps[:, :512],
                                          axis=AX.X),
                             reads=[(ps, "")], writes=[(am, qt_i)])
                    else:
                        t1 = p.op("vector", lambda v, ps=ps:
                                  v.reduce_max(am[:, NQT:NQT + 1],
                                               ps[:, :512], axis=AX.X),
                                  reads=[(ps, "")], writes=[(am, "tmp")])
                        p.op("vector", lambda v, qt_i=qt_i:
                             v.tensor_max(am[:, qt_i:qt_i + 1],
                                          am[:, qt_i:qt_i + 1],
                                          am[:, NQT:NQT + 1]),
                             deps=[t1], reads=[(am, "tmp"), (am, qt_i)],
                             writes=[(am, qt_i)])

        def neg_m_to_qt(h):
            par = h % 2
            qt = qt_b[par]
            for j in range(NQB):
                for u in range(4):
                    qi = j * 4 + u
                    p.op("tensor",
                         lambda t, qi=qi, u=u:
                         t.transpose(psm[0:1, u * 128:(u + 1) * 128],
                                     am[:, qi:qi + 1], K.ident_f[:, :]),
                         reads=[(am, qi), (K.ident_f, "")], writes=[(psm, u)])
                p.op("vector",
                     lambda v, j=j:
                     v.tensor_scalar_mul(qt[64:65, j * 512:(j + 1) * 512],
                                         psm[0:1, :512], -1.0),
                     reads=[(psm, 0), (psm, 1), (psm, 2), (psm, 3)],
                     writes=[(qt, "bias")])

        def pass_b(h):
            par = h % 2
            kt, vt, qt = kt_b[par], vt_b[par], qt_b[par]
            r0, c0 = (h % 2) * 64, h // 2
            for j in range(NQB):
                pv = K.pb[2]
                rq = qt[0:65, j * 512:(j + 1) * 512]
                rc = cqs[:, j * 512:(j + 1) * 512] if influence else None
                pv_tok = None
                for kc in range(NKT):
                    ps = K.pb[3 + kc % 2]
                    lk = kt[0:65, kc * 128:(kc + 1) * 128]
                    mtok = p.op("tensor", lambda t, lk=lk, rq=rq, ps=ps,
                                e=not influence:
                                t.matmul(ps[:, :512], lk, rq, start=True,
                                         stop=e),
                                reads=[(kt, ""), (kt, "ones"), (qt, "lo"),
                                       (qt, "bias")],
                                writes=[(ps, "")])
                    if influence:
                        lc = ck[:, kc * 128:(kc + 1) * 128]
                        mtok = p.op("tensor", lambda t, lc=lc, rc=rc, ps=ps:
                                    t.matmul(ps[:, :512], lc, rc, start=False,
                                             stop=True),
                                    reads=[(ck, ""), (cqs, "")])
                    p.regions[_rk((ps, ""))] = [mtok, []]
                    pb_ = pbuf[kc % 2]
                    if masked:
                        mb = mbuf[kc % 2]
                        p.dma("gpsimd", lambda e, j=j, kc=kc, mb=mb:
                              e.dma_start(out=mb[:], in_=cm_d[j, kc]),
                              sem=f"dm{kc % 2}", writes=[(mb, "")])
                    etok = p.op("scalar", lambda s, ps=ps, pb_=pb_:
                                s.activation(pb_[:, :512], ps[:, :512], AF.Exp),
                                reads=[(ps, "")], writes=[(pb_, "")])
                    if masked:
                        p.op("gpsimd", lambda g, pb_=pb_, mb=mb:
                             g.tensor_mul(pb_[:, :512], pb_[:, :512],
                                          mb[:, :512]),
                             reads=[(pb_, ""), (mb, "")], writes=[(pb_, "")])
                    lv = vt[:, kc * 65:(kc + 1) * 65]
                    pv_tok = p.op("tensor",
                                  lambda t, lv=lv, pb_=pb_, pv=pv,
                                  s=(kc == 0), e=(kc == NKT - 1):
                                  t.matmul(pv[0:65, :512], lv, pb_[:, :512],
                                           start=s, stop=e),
                                  reads=[(vt, ""), (vt, "ones"), (pb_, "")],
                                  writes=[(pv, "")] if kc == 0 else [])
                p.regions[_rk((pv, ""))] = [pv_tok, []]
                t_r = p.op("vector", lambda v, pv=pv:
                           v.reciprocal(rsum[0:1, :512], pv[64:65, :512]),
                           reads=[(pv, "")], writes=[(rsum, "")])
                t_bc = K.matmul_acc(
                    pb5[0:64, :512],
                    [(K.ones_row_f[0:1, 0:64], rsum[0:1, :512],
                      [(rsum, ""), (K.ones_row_f, "")])], (pb5, ""))
                t_cp = p.op("vector", lambda v:
                            v.tensor_copy(rb[0:64, :512], pb5[0:64, :512]),
                            deps=[t_bc], reads=[(pb5, "")],
                            writes=[(rb, "")])
                dst = hb[r0:r0 + 64, c0 * T + j * 512: c0 * T + (j + 1) * 512]
                p.op("vector", lambda v, dst=dst, pv=pv:
                     v.tensor_mul(dst, pv[0:64, :512], rb[0:64, :512]),
                     deps=[t_cp], reads=[(pv, ""), (rb, "")],
                     writes=[(hb, "")])

        load_head(0)
        for h in range(NH):
            if h + 1 < NH:
                load_head(h + 1)
            if debug_stop == "loads":
                continue
            pass_a(h)
            if debug_stop == "passa":
                continue
            neg_m_to_qt(h)
            if debug_stop == "negm":
                continue
            pass_b(h)

        if debug_stop is not None and debug_stop != "wo":
            return
        def evac_addx(m, t0, n, psum):
            xb = xbv[m % 2]
            if t0 == 0:
                p.dma("sync", lambda e, m=m, xb=xb: e.dma_start(
                    out=xb[:, :T], in_=x_d[:, m * T:(m + 1) * T]),
                    sem=f"dxin{m % 2}", writes=[(xb, "")])
            dst = xb[:, t0:t0 + n]
            p.op("vector", lambda v, dst=dst, psum=psum, n=n:
                 v.tensor_add(dst, dst, psum[:, :n]),
                 reads=[(psum, ""), (xb, "")], writes=[(xb, "")])
            if t0 + n >= T:
                p.dma("sync", lambda e, m=m, xb=xb: e.dma_start(
                    out=xa_o[:, m * T:(m + 1) * T], in_=xb[:, :T]),
                    sem=f"dox{m % 2}", reads=[(xb, "")])
        K.emit_project_fm(None, hb, wo_d, 8, 8, T, evac_addx, wbufs, "w",
                          pbase=0, npb=2)
    p.barrier_all()


def build_moe0(T):
    K = KB(T=T)
    io = {}
    io["h"] = K.dram_in("h", [128, 8, T], BF16)
    io["xa"] = K.dram_in("xa", [128, 8 * T], F32)
    io["w1n"] = K.dram_in("w1n", [9, 16, 128, 1024], F32)
    io["w2n"] = K.dram_in("w2n", [9, 8, 128, 2048], F32)
    io["wr"] = K.dram_in("wr", [1, 8, 128, 128], F32)
    io["x1"] = K.dram_out("x1", [128, 8 * T], F32)
    with K.es:
        K.setup_common(npb=5, stats=False)
        emit_moe0(K, io)
        K.p.finalize()
        K.p.emit()
    return K


def emit_moe0(K, io):
    T = K.T
    p = K.p
    h_d, xa_d, w1n_d, w2n_d, wr_d, x1_o = (
        io["h"], io["xa"], io["w1n"], io["w2n"], io["wr"], io["x1"])
    NT = T // 128
    NC = T // 512
    with K.scope("mo"):
        ptr = K.ps("ptr", [128, 512], BF16)
        psm = K.ps("psm", [1, 512], F32)
        pg = K.ps("pg", [128, 8], F32)
        h_sb = K.sb("h_sb", [128, 8 * T], BF16)
        F_sb = K.sb("F_sb", [128, 8 * T], F32)
        w1T = K.sb("w1T", [128, 16 * 8 * 128], BF16)
        w2T = K.sb("w2T", [128, 8 * 16 * 128], BF16)
        h1 = K.sb("h1", [128, 16 * 512], BF16)
        wn = [K.sb("wn0", [128, 2048], F32)] * 2
        wq = [K.sb("wq0", [128, 2048], BF16)] * 2
        qs = [K.sb(f"qs{i}", [128, 4], F32) for i in range(2)]
        lg = K.sb("lg", [8, T], F32)
        ltm = K.sb("ltm", [128, 8], F32)
        etm = K.sb("etm", [128, 8], F32)
        gstat = K.sb("gstat", [128, 4], F32)
        gtm = K.sb("gtm", [128, 8 * NT], F32)
        grow = K.sb("grow", [1, 512], F32)
        gbc = K.sb("gbc", [128, 512], F32)
        ytmp = K.sb("ytmp", [128, 512], BF16)
        wrb = [wq[0]]

        tload = p.dma("sync", lambda e: e.dma_start(
            out=h_sb[:], in_=h_d.rearrange("p m t -> p (m t)")), sem="dh",
            writes=[(h_sb, "")])

        def evac_lg(m, t0, n, psum):
            p.op("vector", lambda v, t0=t0, n=n:
                 v.tensor_copy(lg[0:8, t0:t0 + n], psum[0:8, :n]),
                 reads=[(psum, "")], writes=[(lg, "")])
        K.emit_project_fm(None, h_sb, wr_d, 1, 8, T, evac_lg, wrb, "wr",
                          pbase=0, npb=2)
        for t in range(NT):
            ttok = p.op("tensor", lambda tt, t=t:
                        tt.transpose(pg[:, 0:8], lg[0:8, t * 128:(t + 1) * 128],
                                     K.ident_f[0:8, 0:8]),
                        reads=[(lg, ""), (K.ident_f, "")], writes=[(pg, "")])
            c1 = p.op("vector", lambda v: v.tensor_copy(ltm[:], pg[:, 0:8]),
                      deps=[ttok], reads=[(pg, "")], writes=[(ltm, "")])
            c2 = p.op("vector", lambda v:
                      v.reduce_max(gstat[:, 0:1], ltm[:], axis=AX.X,
                                   negate=True),
                      reads=[(ltm, "")], writes=[(gstat, "nml")])
            c3 = p.op("scalar", lambda s:
                      s.activation(etm[:], ltm[:], AF.Exp,
                                   bias=gstat[:, 0:1],
                                   accum_out=gstat[:, 1:2]),
                      reads=[(ltm, ""), (gstat, "nml")],
                      writes=[(etm, ""), (gstat, "se")])
            c4 = p.op("vector", lambda v:
                      v.reciprocal(gstat[:, 2:3], gstat[:, 1:2]),
                      reads=[(gstat, "se")], writes=[(gstat, "tp")])
            c5 = p.op("vector", lambda v:
                      v.tensor_scalar_mul(gstat[:, 3:4], gstat[:, 0:1], -1.0),
                      reads=[(gstat, "nml")], writes=[(gstat, "ml")])
            c6 = p.op("vector", lambda v, t=t:
                      v.tensor_single_scalar(gtm[:, t * 8:(t + 1) * 8], ltm[:],
                                             gstat[:, 3:4], ALU.is_equal),
                      reads=[(ltm, ""), (gstat, "ml")], writes=[(gtm, t)])
            p.op("vector", lambda v, t=t:
                 v.tensor_scalar_mul(gtm[:, t * 8:(t + 1) * 8],
                                     gtm[:, t * 8:(t + 1) * 8],
                                     gstat[:, 2:3]),
                 reads=[(gtm, t), (gstat, "tp")], writes=[(gtm, t)])

        def quant_chunk(src_ap, par, din, tag):
            w = wn[par]
            o = wq[par]
            s = qs[par]
            t0 = p.dma("gpsimd", lambda e: e.dma_start(out=w[:, :din],
                                                       in_=src_ap),
                       sem=f"dwn{par}", writes=[(w, "")])
            p.op("vector", lambda v:
                 v.tensor_reduce(s[:, 0:1], w[:, :din], axis=AX.X, op=ALU.add,
                                 apply_absolute_value=True),
                 reads=[(w, "")], writes=[(s, "s")])
            p.op("vector", lambda v:
                 v.tensor_scalar(s[:, 0:1], s[:, 0:1], 1.0 / din, 1e-5,
                                 ALU.mult, ALU.max),
                 reads=[(s, "s")], writes=[(s, "s")])
            p.op("vector", lambda v:
                 v.tensor_scalar_mul(s[:, 1:2], s[:, 0:1], 0.5),
                 reads=[(s, "s")], writes=[(s, "h")])
            p.op("vector", lambda v:
                 v.tensor_scalar_mul(s[:, 2:3], s[:, 0:1], -0.5),
                 reads=[(s, "s")], writes=[(s, "nh")])
            p.op("vector", lambda v:
                 v.tensor_single_scalar(o[:, :din], w[:, :din], s[:, 1:2],
                                        ALU.is_ge),
                 reads=[(w, ""), (s, "h")], writes=[(o, "")])
            p.op("vector", lambda v:
                 v.tensor_single_scalar(sgn[:, :din], w[:, :din], s[:, 2:3],
                                        ALU.is_le),
                 reads=[(w, ""), (s, "nh")], writes=[(sgn, "")])
            p.op("vector", lambda v:
                 v.tensor_sub(o[:, :din], o[:, :din], sgn[:, :din]),
                 reads=[(o, ""), (sgn, "")], writes=[(o, "")])
            p.op("vector", lambda v:
                 v.tensor_scalar_mul(o[:, :din], o[:, :din], s[:, 0:1]),
                 reads=[(o, ""), (s, "s")], writes=[(o, "")])
            return o

        sgn = K.sb("sgn", [128, 2048], BF16)

        def quant_matrix(nat_d, mat, nm, din, dstT):
            nk = din // 128
            for m in range(nm):
                par = m % 2
                o = quant_chunk(nat_d[mat, m], par, din, f"m{m}")
                for g4 in range(nk // 4):
                    for u in range(4):
                        k = g4 * 4 + u
                        p.op("tensor", lambda t, o=o, k=k, u=u:
                             t.transpose(ptr[:, u * 128:(u + 1) * 128],
                                         o[:, k * 128:(k + 1) * 128],
                                         K.ident_b[:, :]),
                             reads=[(o, ""), (K.ident_b, "")],
                             writes=[(ptr, "")])
                    dst4 = dstT[:, (m * nk + g4 * 4) * 128:
                                (m * nk + g4 * 4 + 4) * 128]
                    p.op("vector", lambda v, dst4=dst4:
                         v.tensor_copy(dst4, ptr[:, :512]),
                         reads=[(ptr, "")], writes=[(dstT, m)])

        def expert_pass(mat, gated):
            quant_matrix(w1n_d, mat, 16, 1024, w1T)
            quant_matrix(w2n_d, mat, 8, 2048, w2T)
            for tc in range(NC):
                t0 = tc * 512
                for m in range(16):
                    ps = K.pb[m % 2]
                    pairs = [(w1T[:, (m * 8 + k) * 128:(m * 8 + k + 1) * 128],
                              h_sb[:, k * T + t0:k * T + t0 + 512],
                              [(w1T, m), (h_sb, "")]) for k in range(8)]
                    K.matmul_acc(ps[:, :512], pairs, (ps, ""))
                    p.op("scalar", lambda s, ps=ps, m=m:
                         s.activation(h1[:, m * 512:(m + 1) * 512],
                                      ps[:, :512], AF.Silu),
                         reads=[(ps, "")], writes=[(h1, m)])
                if gated is not None:
                    e = gated
                    for u in range(4):
                        t = tc * 4 + u
                        p.op("tensor", lambda tt, t=t, u=u, e=e:
                             tt.transpose(psm[0:1, u * 128:(u + 1) * 128],
                                          gtm[:, t * 8 + e:t * 8 + e + 1],
                                          K.ident_f[:, :]),
                             reads=[(gtm, t), (K.ident_f, "")],
                             writes=[(psm, u)])
                    p.op("vector", lambda v:
                         v.tensor_copy(grow[0:1, :512], psm[0:1, :512]),
                         reads=[(psm, 0), (psm, 1), (psm, 2), (psm, 3)],
                         writes=[(grow, "")])
                    K.matmul_acc(K.pb[4][:, :512],
                                 [(K.ones_row_f[0:1, :], grow[0:1, :512],
                                   [(grow, ""), (K.ones_row_f, "")])],
                                 (K.pb[4], ""))
                    p.op("vector", lambda v:
                         v.tensor_copy(gbc[:, :512], K.pb[4][:, :512]),
                         reads=[(K.pb[4], "")], writes=[(gbc, "")])
                for m in range(8):
                    ps = K.pb[2 + m % 2]
                    pairs = [(w2T[:, (m * 16 + k) * 128:(m * 16 + k + 1) * 128],
                              h1[:, k * 512:(k + 1) * 512],
                              [(w2T, m), (h1, k)]) for k in range(16)]
                    K.matmul_acc(ps[:, :512], pairs, (ps, ""))
                    dst = F_sb[:, m * T + t0: m * T + t0 + 512]
                    if gated is None:
                        if mat == 0:
                            p.op("vector", lambda v, dst=dst, ps=ps:
                                 v.tensor_copy(dst, ps[:, :512]),
                                 reads=[(ps, "")], writes=[(F_sb, "")])
                        else:
                            p.op("vector", lambda v, dst=dst, ps=ps:
                                 v.tensor_add(dst, dst, ps[:, :512]),
                                 reads=[(ps, ""), (F_sb, "")],
                                 writes=[(F_sb, "")])
                    else:
                        t1 = p.op("vector", lambda v, ps=ps:
                                  v.tensor_mul(ytmp[:, :512], ps[:, :512],
                                               gbc[:, :512]),
                                  reads=[(ps, ""), (gbc, "")],
                                  writes=[(ytmp, "")])
                        p.op("vector", lambda v, dst=dst:
                             v.tensor_add(dst, dst, ytmp[:, :512]),
                             deps=[t1], reads=[(ytmp, ""), (F_sb, "")],
                             writes=[(F_sb, "")])

        expert_pass(0, None)
        for e in range(8):
            expert_pass(1 + e, e)

        xbuf = wn[0]
        for m in range(8):
            tl = p.dma("sync", lambda e, m=m: e.dma_start(
                out=xbuf[:, :T], in_=xa_d[:, m * T:(m + 1) * T]),
                sem="dxa", writes=[(xbuf, "")])
            ta = p.op("vector", lambda v, m=m:
                      v.tensor_add(F_sb[:, m * T:(m + 1) * T],
                                   F_sb[:, m * T:(m + 1) * T], xbuf[:, :T]),
                      deps=[tl], reads=[(xbuf, ""), (F_sb, "")],
                      writes=[(F_sb, "")])
            p.dma("sync", lambda e, m=m: e.dma_start(
                out=x1_o[:, m * T:(m + 1) * T],
                in_=F_sb[:, m * T:(m + 1) * T]),
                sem="dox", deps=[ta], reads=[(F_sb, "")])
    p.barrier_all()


def build_pre1(T):
    K = KB(T=T)
    io = {}
    io["x"] = K.dram_in("x", [128, 8 * T], F32)
    io["wq"] = K.dram_in("wq", [128, 8, 1024], F32)
    io["wk"] = K.dram_in("wk", [128, 8, 1024], F32)
    io["wv"] = K.dram_in("wv", [8, 8, 128, 128], F32)
    io["cosE"] = K.dram_in("cosE", [T, 1024], F32)
    io["sinE"] = K.dram_in("sinE", [T, 1024], F32)
    io["xn"] = K.dram_out("xn", [128, 8, T], BF16)
    io["q"] = K.dram_out("q", [128, 8, T], BF16)
    io["k"] = K.dram_out("k", [128, 8, T], BF16)
    io["v"] = K.dram_out("v", [128, 8, T], BF16)
    with K.es:
        K.setup_common(npb=5, stats=False)
        emit_pre1(K, io)
        K.p.finalize()
        K.p.emit()
    return K


def emit_pre1(K, io):
    T = K.T
    p = K.p
    x_d, wq_d, wk_d, wv_d, cos_d, sin_d = (
        io["x"], io["wq"], io["wk"], io["wv"], io["cosE"], io["sinE"])
    xn_o, q_o, k_o, v_o = io["xn"], io["q"], io["k"], io["v"]
    NT = T // 128
    xnf_o = io.get("xnf")
    with K.scope("p1"):
        K.pstat1 = K.ps("pstat1", [1, 512], F32)
        K.pstat2 = K.ps("pstat2", [1, 512], F32)
        ptr = K.ps("ptr", [128, 512], BF16)
        xt = K.sb("xt", [128, 8 * T], F32)
        xn = K.sb("xns", [128, 8 * T], BF16)
        sq = K.sb("sq", [128, 1024], F32)
        st = [K.sb(f"st{i}", [1, 512], F32) for i in range(3)]
        wbufs = [K.sb(f"wbuf{i}", [128, 8 * 128], BF16) for i in range(2)]
        wtm = [K.sb(f"wtm{i}", [128, 8 * 512], BF16) for i in range(2)]
        obufs = [K.sb(f"obuf{i}", [128, T], BF16) for i in range(2)]
        qtm = K.sb("qtm", [128, 1024], F32)
        qrm = [K.sb(f"qrm{i}", [128, 1024], BF16) for i in range(2)]
        ce = K.sb("ce", [128, 1024], F32)
        se = K.sb("se_t", [128, 1024], F32)
        rsc = [K.sb(f"rsc{i}", [128, 512], F32) for i in range(2)]
        qf = K.sb("qf", [128, 8 * T], BF16)

        p.dma("sync", lambda e: e.dma_start(out=xt[:], in_=x_d),
              sem="dx", writes=[(xt, "")])
        K.emit_ln_fm(xt, xt, 8, T, F32, sq_sb=sq, stat_sb=st,
                     ones_col_b=None, pbase=0)
        p.op("scalar", lambda s: s.activation(xn[:], xt[:], AF.Copy),
             reads=[(xt, "")], writes=[(xn, "")])
        p.dma("sync", lambda e: e.dma_start(out=xn_o.rearrange(
            "p m t -> p (m t)"), in_=xn[:]), sem="doxn", reads=[(xn, "")])
        if xnf_o is not None:
            p.dma("sync", lambda e: e.dma_start(out=xnf_o, in_=xt[:]),
                  sem="doxnf", reads=[(xt, "")])

        evac = K.mk_evac_store(obufs, v_o, T, func=AF.Copy, scale=1.0,
                               osem="dov")
        K.emit_project_fm(None, xn, wv_d, 8, 8, T, evac, wbufs, "wv",
                          pbase=2, npb=2)

        def qk_roped(w_d, out_d, scale, wsem):
            for t in range(NT):
                tct = p.dma("gpsimd", lambda e, t=t: e.dma_start(
                    out=ce[:], in_=cos_d[t * 128:(t + 1) * 128, :]),
                    sem="dce", writes=[(ce, "")])
                tst = p.dma("gpsimd", lambda e, t=t: e.dma_start(
                    out=se[:], in_=sin_d[t * 128:(t + 1) * 128, :]),
                    sem="dse", writes=[(se, "")])
                for half in range(2):
                    ps = K.pb[2 + half]
                    pairs = [(xn[:, kk * T + t * 128: kk * T + (t + 1) * 128],
                              wtm[half][:, kk * 512:(kk + 1) * 512],
                              [(xn, ""), (wtm[half], "")])
                             for kk in range(8)]
                    K.matmul_acc(ps[:, :512], pairs, (ps, ""))
                    p.op("scalar", lambda s, ps=ps, half=half:
                         s.activation(qtm[:, half * 512:(half + 1) * 512],
                                      ps[:, :512], AF.Copy, scale=scale),
                         reads=[(ps, "")], writes=[(qtm, half)])
                qr = qrm[t % 2]
                xe = qtm.ap().rearrange("p (j two) -> p j two", two=2)[:, :, 0:1]
                xo = qtm.ap().rearrange("p (j two) -> p j two", two=2)[:, :, 1:2]
                cee = ce.ap().rearrange("p (j two) -> p j two", two=2)[:, :, 0:1]
                soo = se.ap().rearrange("p (j two) -> p j two", two=2)[:, :, 0:1]
                oe = qr.ap().rearrange("p (j two) -> p j two", two=2)[:, :, 0:1]
                oo = qr.ap().rearrange("p (j two) -> p j two", two=2)[:, :, 1:2]
                r0 = rsc[0]
                r1 = rsc[1]
                r0a = r0[:, :512].rearrange("p (j o) -> p j o", o=1)
                r1a = r1[:, :512].rearrange("p (j o) -> p j o", o=1)
                t1 = p.op("vector", lambda v, r0a=r0a, xe=xe, cee=cee:
                          v.tensor_mul(r0a, xe, cee),
                          deps=[tct], reads=[(qtm, 0), (qtm, 1), (ce, "")],
                          writes=[(rsc[0], "")])
                t2 = p.op("vector", lambda v, r1a=r1a, xo=xo, soo=soo:
                          v.tensor_mul(r1a, xo, soo),
                          deps=[tst], reads=[(qtm, 0), (qtm, 1), (se, "")],
                          writes=[(rsc[1], "")])
                p.op("vector", lambda v, oe=oe, r0a=r0a, r1a=r1a:
                     v.tensor_sub(oe, r0a, r1a),
                     deps=[t1, t2], reads=[(rsc[0], ""), (rsc[1], "")],
                     writes=[(qr, "e")])
                t3 = p.op("vector", lambda v, r0a=r0a, xe=xe, soo=soo:
                          v.tensor_mul(r0a, xe, soo),
                          reads=[(qtm, 0), (qtm, 1), (se, ""), (rsc[0], "")],
                          writes=[(rsc[0], "")])
                t4 = p.op("vector", lambda v, r1a=r1a, xo=xo, cee=cee:
                          v.tensor_mul(r1a, xo, cee),
                          reads=[(qtm, 0), (qtm, 1), (ce, ""), (rsc[1], "")],
                          writes=[(rsc[1], "")])
                p.op("vector", lambda v, oo=oo, r0a=r0a, r1a=r1a:
                     v.tensor_add(oo, r0a, r1a),
                     deps=[t3, t4], reads=[(rsc[0], ""), (rsc[1], "")],
                     writes=[(qr, "o")])
                for g4 in range(2):
                    for u in range(4):
                        c = g4 * 4 + u
                        p.op("tensor", lambda tt, c=c, u=u, qr=qr:
                             tt.transpose(ptr[:, u * 128:(u + 1) * 128],
                                          qr[:, c * 128:(c + 1) * 128],
                                          K.ident_b[:, :]),
                             reads=[(qr, "e"), (qr, "o"), (K.ident_b, "")],
                             writes=[(ptr, "")])
                    src4 = ptr.ap().rearrange("p (q c) -> p q c", c=128)
                    dst4 = qf.ap().rearrange("p (c t) -> p c t", t=T)[
                        :, g4 * 4:(g4 + 1) * 4, t * 128:(t + 1) * 128]
                    p.op("vector", lambda v, src4=src4, dst4=dst4:
                         v.tensor_copy(dst4, src4),
                         reads=[(ptr, "")], writes=[(qf, "")])
            p.dma("sync", lambda e: e.dma_start(out=out_d.rearrange(
                "p m t -> p (m t)"), in_=qf[:]), sem="doq", reads=[(qf, "")])

        def load_w_half2(w_d, half, wsem):
            src = w_d[:, :, half * 512:(half + 1) * 512]
            dst = wtm[half].ap().rearrange("p (k c) -> p k c", c=512)
            return p.dma("gpsimd", lambda e: e.dma_start(out=dst, in_=src),
                         sem=wsem, writes=[(wtm[half], "")])

        for (w_d, out_d, scale, nm) in ((wq_d, q_o, 0.125, "wq2"),
                                        (wk_d, k_o, 1.0, "wk2")):
            load_w_half2(w_d, 0, nm + "a")
            load_w_half2(w_d, 1, nm + "b")
            qk_roped(w_d, out_d, scale, nm)
    p.barrier_all()


def build_loc_attn(T, debug_dump_step=None):
    HLO = 257
    TE = T + HLO
    NS = T // 512
    K = KB(T=T)
    io = {}
    io["y"] = K.dram_in("y", [128, 8 * TE], F32)
    io["wqkv"] = K.dram_in("wqkv", [128, 8, 3072], F32)
    io["wo"] = K.dram_in("wo", [8, 8, 128, 128], F32)
    io["qa"] = K.dram_in("qa", [TE, 1024], F32)
    io["qb"] = K.dram_in("qb", [TE, 1024], F32)
    io["ka"] = K.dram_in("ka", [TE, 1024], F32)
    io["kb"] = K.dram_in("kb", [TE, 1024], F32)
    io["qscl"] = K.dram_in("qscl", [128, 1024], F32)
    io["kscl"] = K.dram_in("kscl", [128, 1024], F32)
    io["maskA"] = K.dram_in("maskA", [NS * 4, 128, 384], BF16)
    io["maskB"] = K.dram_in("maskB", [NS * 6, 128, 512], BF16)
    io["yo"] = K.dram_out("yo", [128, 8 * T], F32)
    with K.es:
        K.setup_common(npb=5, stats=False)
        emit_loc_attn(K, io, scope="la", debug_dump_step=debug_dump_step)
        K.p.finalize()
        K.p.emit()
    return K


def emit_loc_attn(K, io, scope, debug_dump_step=None):
    T = K.T
    p = K.p
    HLO = 257
    TE = T + HLO
    NS = T // 512
    y_d, wqkv_d, wo_d = io["y"], io["wqkv"], io["wo"]
    qa_d, qb_d, ka_d, kb_d = io["qa"], io["qb"], io["ka"], io["kb"]
    ma_d, mb_d, yo_o = io["maskA"], io["maskB"], io["yo"]
    with K.scope(scope):
        qscl = K.sb("qscl_t", [128, 1024], BF16)
        kscl = K.sb("kscl_t", [128, 1024], BF16)
        p.dma("gpsimd", lambda e: e.dma_start(out=qscl[:], in_=io["qscl"]),
              sem="dqscl", writes=[(qscl, "")])
        p.dma("gpsimd", lambda e: e.dma_start(out=kscl[:], in_=io["kscl"]),
              sem="dkscl", writes=[(kscl, "")])
        pb5 = K.ps("pb5", [128, 512], F32)
        ptr = K.ps("ptr", [128, 512], BF16)
        psm = K.ps("psm", [1, 512], F32)
        y_sb = K.sb("y_sb", [128, 8 * TE], F32)
        wtm = [K.sb(f"wtm{i}", [128, 8 * 512], BF16) for i in range(2)]
        xta = K.sb("xta", [128, 6 * 1024], BF16)
        qkt = K.sb("qkt", [128, 6 * 2048], BF16)
        sqv = K.sb("sqv", [128, 1024], F32)
        nrm = K.sb("nrm", [128, 32], F32)
        tabs = [K.sb(f"tab{i}", [128, 1024], F32) for i in range(2)]
        qfl = K.sb("qfl", [128, 8 * 768], BF16)
        kfl = K.sb("kfl", [128, 8 * 768], BF16)
        vta = K.sb("vta", [128, 6 * 16 * 65], BF16)
        hbl = K.sb("hbl", [128, 8 * 512], BF16)
        qkr = hbl
        ktl = K.sb("ktl", [65, 768], BF16)
        qtl = K.sb("qtl", [65, 512], BF16)
        maska = K.sb("maska", [128, 4 * 384], BF16)
        maskb = [K.sb(f"maskb{i}", [128, 512], BF16) for i in range(2)]
        pbuf = [K.sb(f"pbuf{i}", [128, 512], BF16) for i in range(2)]
        am = K.sb("am_l", [128, 4], F32)
        thirty = K.sb("thirty", [128, 1], F32)
        rsum = K.sb("rsum_l", [1, 512], F32)
        rb = K.sb("rb_l", [64, 512], F32)
        oby = [K.sb(f"oby{i}", [128, 512], F32) for i in range(2)]
        wbufs = [K.sb(f"wbuf{i}", [128, 8 * 128], BF16) for i in range(2)]

        p.dma("sync", lambda e: e.dma_start(out=y_sb[:], in_=y_d),
              sem="dy", writes=[(y_sb, "")])
        p.op("gpsimd", lambda g: g.memset(ktl[64:65, :], 1.0),
             writes=[(ktl, "ones")])
        p.op("gpsimd", lambda g: g.memset(thirty[:], 30.0),
             writes=[(thirty, "")])
        vap = vta.ap().rearrange("p (t c) -> p t c", c=65)
        p.op("gpsimd", lambda g: g.memset(vap[:, :, 64:65], 1.0),
             writes=[(vta, "ones")])

        for s in range(NS):
            ps0 = 1 + s * 512
            for ti in range(6):
                c0 = ps0 + ti * 128
                for kk in range(8):
                    off = c0 if kk < 4 else c0 - 1
                    p.op("scalar", lambda sc, kk=kk, off=off, ti=ti:
                         sc.activation(xta[:, ti * 1024 + kk * 128:
                                           ti * 1024 + (kk + 1) * 128],
                                       y_sb[:, kk * TE + off:
                                            kk * TE + off + 128],
                                       AF.Copy),
                         reads=[(y_sb, "")], writes=[(xta, (ti, kk))])
            for g in range(6):
                p.dma("gpsimd", lambda e, g=g: e.dma_start(
                    out=wtm[g % 2].ap().rearrange(
                        "p (k c) -> p k c", c=512),
                    in_=wqkv_d[:, :, g * 512:(g + 1) * 512]),
                    sem=f"dwg{g % 2}", writes=[(wtm[g % 2], "")])
                for ti in range(6):
                    ps_ = K.pb[ti % 2 + (0 if g % 2 == 0 else 2)]
                    pairs = [(xta[:, ti * 1024 + kk * 128:
                                  ti * 1024 + (kk + 1) * 128],
                              wtm[g % 2][:, kk * 512:(kk + 1) * 512],
                              [(xta, (ti, kk)), (wtm[g % 2], "")])
                             for kk in range(8)]
                    K.matmul_acc(ps_[:, :512], pairs, (ps_, ""))
                    if g < 4:
                        p.op("scalar", lambda sc, ps_=ps_, g=g, ti=ti:
                             sc.activation(qkt[:, ti * 2048 + g * 512:
                                               ti * 2048 + (g + 1) * 512],
                                           ps_[:, :512], AF.Copy),
                             reads=[(ps_, "")], writes=[(qkt, (ti, g))])
                    else:
                        dst = vta.ap().rearrange("p (t h c) -> p t h c",
                                                 h=16, c=65)[
                            :, ti, (g - 4) * 8:(g - 4) * 8 + 8, 0:64]
                        src = ps_[:, :512].rearrange("p (h c) -> p h c", c=64)
                        p.op("scalar", lambda sc, dst=dst, src=src, ps_=ps_:
                             sc.activation(dst, src, AF.Copy),
                             reads=[(ps_, "")], writes=[(vta, ti)])
            for ti in range(6):
                c0 = ps0 + ti * 128
                qk_reads = [(qkt, (ti, g)) for g in range(4)]
                for half in range(2):
                    p.op("scalar", lambda sc, ti=ti, half=half:
                         sc.activation(sqv[:],
                                       qkt[:, ti * 2048 + half * 1024:
                                           ti * 2048 + (half + 1) * 1024],
                                       AF.Square),
                         reads=qk_reads, writes=[(sqv, "")])
                    p.op("vector", lambda v, half=half: v.tensor_reduce(
                        nrm[:, half * 16:(half + 1) * 16].rearrange(
                            "p (h o) -> p h o", o=1),
                        sqv[:].rearrange("p (h d) -> p h d", d=64),
                        axis=AX.X, op=ALU.add),
                        reads=[(sqv, "")], writes=[(nrm, half)])
                    p.op("scalar", lambda sc, half=half: sc.activation(
                        nrm[:, half * 16:(half + 1) * 16],
                        nrm[:, half * 16:(half + 1) * 16], AF.Sqrt),
                        reads=[(nrm, half)], writes=[(nrm, half)])
                    p.op("vector", lambda v, half=half: v.tensor_scalar_max(
                        nrm[:, half * 16:(half + 1) * 16],
                        nrm[:, half * 16:(half + 1) * 16], 1e-12),
                        reads=[(nrm, half)], writes=[(nrm, half)])
                    p.op("vector", lambda v, half=half: v.reciprocal(
                        nrm[:, half * 16:(half + 1) * 16],
                        nrm[:, half * 16:(half + 1) * 16]),
                        reads=[(nrm, half)], writes=[(nrm, half)])
                for qk in range(2):
                    scl_t = qscl if qk == 0 else kscl
                    p.op("vector", lambda v, ti=ti, qk=qk, scl_t=scl_t:
                         v.tensor_mul(
                             qkt[:, ti * 2048 + qk * 1024:
                                 ti * 2048 + (qk + 1) * 1024],
                             qkt[:, ti * 2048 + qk * 1024:
                                 ti * 2048 + (qk + 1) * 1024],
                             scl_t[:]),
                         reads=[(qkt, (ti, 2 * qk)), (qkt, (ti, 2 * qk + 1)),
                                (scl_t, "")],
                         writes=[(qkt, (ti, 2 * qk)), (qkt, (ti, 2 * qk + 1))])
                for qk in range(2):
                    A, B = tabs[0], tabs[1]
                    tda = (qa_d, qb_d) if qk == 0 else (ka_d, kb_d)
                    ta_tok = p.dma("gpsimd", lambda e, td=tda[0], c0=c0:
                                   e.dma_start(out=A[:],
                                               in_=td[c0:c0 + 128, :]),
                                   sem="dta", writes=[(tabs[0], "")])
                    tb_tok = p.dma("gpsimd", lambda e, td=tda[1], c0=c0:
                                   e.dma_start(out=B[:],
                                               in_=td[c0:c0 + 128, :]),
                                   sem="dtb", writes=[(tabs[1], "")])
                    base = ti * 2048 + qk * 1024
                    src3 = qkt.ap()[:, base:base + 1024].rearrange(
                        "p (h v d) -> p h v d", v=2, d=32)
                    out3 = qkr.ap()[:, qk * 1024:(qk + 1) * 1024].rearrange(
                        "p (h v d) -> p h v d", v=2, d=32)
                    A3 = A.ap().rearrange("p (h v d) -> p h v d", v=2, d=32)
                    B3 = B.ap().rearrange("p (h v d) -> p h v d", v=2, d=32)
                    r03 = sqv.ap()[:, 0:512].rearrange(
                        "p (h d) -> p h d", d=32)
                    r13 = sqv.ap()[:, 512:1024].rearrange(
                        "p (h d) -> p h d", d=32)
                    q1, q2 = src3[:, :, 0], src3[:, :, 1]
                    t1 = p.op("vector", lambda v, q1=q1, A3=A3, r03=r03:
                              v.tensor_mul(r03, q1, A3[:, :, 0]),
                              deps=[ta_tok], reads=qk_reads + [(tabs[0], "")],
                              writes=[(sqv, "")])
                    t2 = p.op("vector", lambda v, q2=q2, B3=B3, r13=r13:
                              v.tensor_mul(r13, q2, B3[:, :, 0]),
                              deps=[tb_tok], reads=qk_reads + [(tabs[1], "")],
                              writes=[(sqv, "")])
                    o1 = p.op("vector", lambda v, out3=out3, r03=r03, r13=r13:
                              v.tensor_sub(out3[:, :, 0], r03, r13),
                              deps=[t1, t2],
                              reads=[(sqv, ""), (sqv, "")],
                              writes=[(hbl, "")])
                    t3 = p.op("vector", lambda v, q2=q2, A3=A3, r03=r03:
                              v.tensor_mul(r03, q2, A3[:, :, 1]),
                              reads=qk_reads + [(tabs[0], ""), (sqv, "")],
                              writes=[(sqv, "")])
                    t4 = p.op("vector", lambda v, q1=q1, B3=B3, r13=r13:
                              v.tensor_mul(r13, q1, B3[:, :, 1]),
                              reads=qk_reads + [(tabs[1], ""), (sqv, "")],
                              writes=[(sqv, "")])
                    o2 = p.op("vector", lambda v, out3=out3, r03=r03, r13=r13:
                              v.tensor_add(out3[:, :, 1], r03, r13),
                              deps=[t3, t4],
                              reads=[(sqv, ""), (sqv, "")],
                              writes=[(hbl, "")])
                    for h in range(16):
                        sl = slice(qk * 1024 + h * 64, qk * 1024 + (h + 1) * 64)
                        p.op("vector", lambda v, h=h, qk=qk, sl=sl:
                             v.tensor_scalar_mul(
                                 qkr[:, sl], qkr[:, sl],
                                 nrm[:, qk * 16 + h: qk * 16 + h + 1]),
                             deps=[o1, o2],
                             reads=[(hbl, ""), (hbl, ""),
                                    (nrm, qk)],
                             writes=[(hbl, ""), (hbl, "")])
                for g4 in range(4):
                    dst = qfl if g4 < 2 else kfl
                    for u in range(4):
                        c = g4 * 4 + u
                        p.op("tensor", lambda tt, c=c, u=u:
                             tt.transpose(ptr[:, u * 128:(u + 1) * 128],
                                          qkr[:, c * 128:(c + 1) * 128],
                                          K.ident_b[:, :]),
                             reads=[(hbl, ""), (hbl, ""), (hbl, ""), (hbl, ""),
                                    (K.ident_b, "")],
                             writes=[(ptr, "")])
                    src4 = ptr.ap().rearrange("p (q c) -> p q c", c=128)
                    dst4 = dst.ap().rearrange("p (c t) -> p c t", t=768)[
                        :, (g4 % 2) * 4:(g4 % 2) * 4 + 4,
                        ti * 128:(ti + 1) * 128]
                    p.op("vector", lambda v, src4=src4, dst4=dst4:
                         v.tensor_copy(dst4, src4),
                         reads=[(ptr, "")], writes=[(dst, "")])

            p.dma("sync", lambda e, s=s: e.dma_start(
                out=maska[:].rearrange("p (q c) -> p q c", c=384),
                in_=ma_d[s * 4:(s + 1) * 4].rearrange("q p c -> p q c")),
                sem="dma_a", writes=[(maska, "")])
            if debug_dump_step == s:
                qkt_o = K.dram_out("qkt_dbg", [128, 6 * 2048], BF16)
                p.dma("sync", lambda e: e.dma_start(out=qkt_o, in_=qkt[:]),
                      sem="ddbg0", reads=[(qkt, (ti_, g_)) for ti_ in range(6)
                                         for g_ in range(4)])
                qfl_o = K.dram_out("qfl_dbg", [128, 8 * 768], BF16)
                kfl_o = K.dram_out("kfl_dbg", [128, 8 * 768], BF16)
                vta_o = K.dram_out("vta_dbg", [128, 6 * 16 * 65], BF16)
                p.dma("sync", lambda e: e.dma_start(out=qfl_o, in_=qfl[:]),
                      sem="ddbg1", reads=[(qfl, "")])
                p.dma("sync", lambda e: e.dma_start(out=kfl_o, in_=kfl[:]),
                      sem="ddbg2", reads=[(kfl, "")])
                p.dma("sync", lambda e: e.dma_start(out=vta_o, in_=vta[:]),
                      sem="ddbg3", reads=[(vta, ""), (vta, "ones")] +
                      [(vta, z) for z in range(6)])

            for h in range(16):
                r0 = (h % 2) * 64
                c0h = h // 2
                p.dma("sync", lambda e, r0=r0, c0h=c0h: e.dma_start(
                    out=ktl[0:64, :],
                    in_=kfl[r0:r0 + 64, c0h * 768:(c0h + 1) * 768]),
                    sem="dktl", reads=[(kfl, "")], writes=[(ktl, "")])
                for qt in range(4):
                    ps_ = K.pb[qt % 2]
                    lq = qfl[r0:r0 + 64,
                             c0h * 768 + 256 + qt * 128:
                             c0h * 768 + 256 + (qt + 1) * 128]
                    rk = kfl[r0:r0 + 64,
                             c0h * 768 + qt * 128: c0h * 768 + qt * 128 + 384]
                    mtok = p.op("tensor", lambda t, lq=lq, rk=rk, ps_=ps_:
                                t.matmul(ps_[:, :384], lq, rk, start=True,
                                         stop=True),
                                reads=[(qfl, ""), (kfl, "")],
                                writes=[(ps_, "")])
                    p.regions[_rk((ps_, ""))] = [mtok, []]
                    ta = p.op("vector", lambda v, ps_=ps_, qt=qt:
                              v.tensor_add(sqv[:, :384], ps_[:, :384],
                                           maska[:, qt * 384:(qt + 1) * 384]),
                              reads=[(ps_, ""), (maska, ""), (sqv, "")],
                              writes=[(sqv, "")])
                    p.op("vector", lambda v, qt=qt:
                         v.reduce_max(am[:, qt:qt + 1], sqv[:, :384],
                                      axis=AX.X),
                         deps=[ta], reads=[(sqv, "")], writes=[(am, qt)])
                p.dma("sync", lambda e, r0=r0, c0h=c0h: e.dma_start(
                    out=qtl[0:64, :],
                    in_=qfl[r0:r0 + 64, c0h * 768 + 256: c0h * 768 + 768]),
                    sem="dqtl", reads=[(qfl, "")], writes=[(qtl, "lo")])
                for u in range(4):
                    p.op("tensor", lambda t, u=u:
                         t.transpose(psm[0:1, u * 128:(u + 1) * 128],
                                     am[:, u:u + 1], K.ident_f[:, :]),
                         reads=[(am, u), (K.ident_f, "")], writes=[(psm, u)])
                p.op("vector", lambda v:
                     v.tensor_scalar_mul(qtl[64:65, 0:512], psm[0:1, :512],
                                         -1.0),
                     reads=[(psm, 0), (psm, 1), (psm, 2), (psm, 3)],
                     writes=[(qtl, "bias")])
                pv = K.pb[2]
                pv_tok = None
                for kc in range(6):
                    ps_ = K.pb[3 + kc % 2]
                    lk = ktl[0:65, kc * 128:(kc + 1) * 128]
                    mtok = p.op("tensor", lambda t, lk=lk, ps_=ps_:
                                t.matmul(ps_[:, :512], lk, qtl[0:65, :512],
                                         start=True, stop=True),
                                reads=[(ktl, ""), (ktl, "ones"), (qtl, "lo"),
                                       (qtl, "bias")],
                                writes=[(ps_, "")])
                    p.regions[_rk((ps_, ""))] = [mtok, []]
                    mb = maskb[kc % 2]
                    p.dma("gpsimd", lambda e, s=s, kc=kc, mb=mb: e.dma_start(
                        out=mb[:], in_=mb_d[s * 6 + kc]),
                        sem=f"dmb{kc % 2}", writes=[(mb, "")])
                    pb_ = pbuf[kc % 2]
                    p.op("scalar", lambda sc, ps_=ps_:
                         sc.activation(ps_[:, :512], ps_[:, :512], AF.Relu,
                                       bias=thirty[:, 0:1], scale=-1.0),
                         reads=[(ps_, ""), (thirty, "")], writes=[(ps_, "")])
                    p.op("scalar", lambda sc, ps_=ps_, pb_=pb_:
                         sc.activation(pb_[:, :512], ps_[:, :512], AF.Exp,
                                       bias=thirty[:, 0:1], scale=-1.0),
                         reads=[(ps_, ""), (thirty, "")], writes=[(pb_, "")])
                    p.op("gpsimd", lambda g, pb_=pb_, mb=mb:
                         g.tensor_mul(pb_[:, :512], pb_[:, :512],
                                      mb[:, :512]),
                         reads=[(pb_, ""), (mb, "")], writes=[(pb_, "")])
                    lv = vta[:, (kc * 16 + h) * 65:(kc * 16 + h) * 65 + 65]
                    pv_tok = p.op("tensor",
                                  lambda t, lv=lv, pb_=pb_, pv=pv,
                                  st=(kc == 0), en=(kc == 5):
                                  t.matmul(pv[0:65, :512], lv, pb_[:, :512],
                                           start=st, stop=en),
                                  reads=[(vta, kc), (vta, "ones"), (pb_, "")],
                                  writes=[(pv, "")] if kc == 0 else [])
                p.regions[_rk((pv, ""))] = [pv_tok, []]
                t_r = p.op("vector", lambda v, pv=pv:
                           v.reciprocal(rsum[0:1, :512], pv[64:65, :512]),
                           reads=[(pv, "")], writes=[(rsum, "")])
                t_bc = K.matmul_acc(
                    pb5[0:64, :512],
                    [(K.ones_row_f[0:1, 0:64], rsum[0:1, :512],
                      [(rsum, ""), (K.ones_row_f, "")])], (pb5, ""))
                t_cp = p.op("vector", lambda v:
                            v.tensor_copy(rb[0:64, :512], pb5[0:64, :512]),
                            deps=[t_bc], reads=[(pb5, "")],
                            writes=[(rb, "")])
                dst = hbl[r0:r0 + 64, c0h * 512:(c0h + 1) * 512]
                p.op("vector", lambda v, dst=dst, pv=pv:
                     v.tensor_mul(dst, pv[0:64, :512], rb[0:64, :512]),
                     deps=[t_cp], reads=[(pv, ""), (rb, "")],
                     writes=[(hbl, "")])

            for m in range(8):
                K.wstream_load(wbufs[m % 2], wo_d, m, 8, f"dwo{m % 2}")
                ps_ = K.pb[m % 2]
                pairs = [(wbufs[m % 2][:, k * 128:(k + 1) * 128],
                          hbl[:, k * 512:(k + 1) * 512],
                          [(wbufs[m % 2], ""), (hbl, "")]) for k in range(8)]
                K.matmul_acc(ps_[:, :512], pairs, (ps_, ""))
                ob = oby[m % 2]
                ysl = y_sb[:, m * TE + HLO + s * 512:
                           m * TE + HLO + (s + 1) * 512]
                ta = p.op("vector", lambda v, ob=ob, ps_=ps_, ysl=ysl:
                          v.tensor_add(ob[:, :512], ps_[:, :512], ysl),
                          reads=[(ps_, ""), (y_sb, "")], writes=[(ob, "")])
                p.dma("sync", lambda e, m=m, s=s, ob=ob: e.dma_start(
                    out=yo_o[:, m * T + s * 512: m * T + (s + 1) * 512],
                    in_=ob[:, :512]),
                    sem=f"doy{m % 2}", deps=[ta], reads=[(ob, "")])
    p.barrier_all()



def host_loc_tables(cols_glob, q_scale=None, k_scale=None):
    DIM_HEAD, WINDOW, N = 64, 256, 8192
    half = np.arange(0, DIM_HEAD, 2, dtype=np.float64)
    inv_freq = 1.0 / 10000.0 ** (half / DIM_HEAD)
    t = cols_glob.astype(np.float64)
    fr = np.outer(t, inv_freq)
    freqs = np.concatenate([fr, fr], -1)
    base_scale = (half + 0.4 * DIM_HEAD) / (1.4 * DIM_HEAD)
    power = (t - N // 2) / (WINDOW // 2)
    scale = base_scale[None, :] ** power[:, None]
    scale = np.concatenate([scale, scale], -1)
    cos, sin = np.cos(freqs), np.sin(freqs)
    qa = cos * scale
    qb = sin * scale
    ka = cos / scale
    kb = sin / scale
    tile16 = lambda a: np.tile(a, (1, 16)).astype(np.float32)
    return tile16(qa), tile16(qb), tile16(ka), tile16(kb)



def host_loc_masks(T, t_off, batch_start):
    NS = T // 512
    ma = np.zeros((NS * 4, 128, 384), np.float32)
    mb = np.zeros((NS * 6, 128, 512), np.float32)
    for s in range(NS):
        for qt in range(4):
            q = t_off + s * 512 + qt * 128 + np.arange(128)[:, None]
            k = t_off - 256 + s * 512 + qt * 128 + np.arange(384)[None, :]
            ok = (k >= q - 256) & (k <= q) & (k >= batch_start)
            ma[s * 4 + qt] = np.where(ok, 0.0, -3e38)
        for kc in range(6):
            k = t_off - 256 + s * 512 + kc * 128 + np.arange(128)[:, None]
            q = t_off + s * 512 + np.arange(512)[None, :]
            ok = (k >= q - 256) & (k <= q) & (k >= batch_start)
            mb[s * 6 + kc] = np.where(ok, 1.0, 0.0)
    return ma, mb


def build_loc_ffn(T):
    TE = T + 1
    K = KB(T=T)
    io = {}
    io["y"] = K.dram_in("y", [128, 8 * TE], F32)
    io["gb"] = K.dram_in("gb", [128, 16], F32)
    io["w1a"] = K.dram_in("w1a", [22, 8, 128, 128], F32)
    io["w1g"] = K.dram_in("w1g", [22, 8, 128, 128], F32)
    io["b1a"] = K.dram_in("b1a", [128, 22], F32)
    io["b1g"] = K.dram_in("b1g", [128, 22], F32)
    io["w2"] = K.dram_in("w2", [8, 22, 128, 128], F32)
    io["b2"] = K.dram_in("b2", [128, 8], F32)
    io["yo"] = K.dram_out("yo", [128, 8 * T], F32)
    with K.es:
        K.setup_common(npb=5, stats=False)
        emit_loc_ffn(K, io, scope="lf")
        K.p.finalize()
        K.p.emit()
    return K


def emit_loc_ffn(K, io, scope):
    T = K.T
    p = K.p
    TE = T + 1
    NC = T // 512
    y_d, gb_d = io["y"], io["gb"]
    w1a_d, w1g_d, b1a_d, b1g_d = io["w1a"], io["w1g"], io["b1a"], io["b1g"]
    w2_d, b2_d, yo_o = io["w2"], io["b2"], io["yo"]
    with K.scope(scope):
        K.pstat1 = K.ps("pstat1", [1, 512], F32)
        K.pstat2 = K.ps("pstat2", [1, 512], F32)
        y_sb = K.sb("y_sb", [128, 8 * TE], F32)
        xnb = K.sb("xnb", [128, 8 * 512], BF16)
        ab = K.sb("ab", [128, 22 * 512], BF16)
        gel = K.sb("gel", [128, 512], BF16)
        oby = [K.sb(f"oby{i}", [128, 512], F32) for i in range(2)]
        sq = K.sb("sq", [128, 1024], F32)
        st = [K.sb(f"st{i}", [1, 512], F32) for i in range(3)]
        gb = K.sb("gbs", [128, 16], F32)
        b1a = K.sb("b1as", [128, 22], F32)
        b1g = K.sb("b1gs", [128, 22], F32)
        b2 = K.sb("b2s", [128, 8], F32)
        wb = [K.sb(f"wb{i}", [128, 22 * 128], BF16) for i in range(2)]

        p.dma("sync", lambda e: e.dma_start(out=y_sb[:], in_=y_d),
              sem="dy", writes=[(y_sb, "")])
        for ci, (t_, d_) in enumerate(((gb, gb_d), (b1a, b1a_d),
                                       (b1g, b1g_d), (b2, b2_d))):
            p.dma("sync", lambda e, t_=t_, d_=d_: e.dma_start(out=t_[:],
                                                              in_=d_),
                  sem=f"dconst{ci}", writes=[(t_, "")])

        offs = [1, 1, 1, 1, 0, 0, 0, 0]
        for tc in range(NC):
            t0 = tc * 512
            K.emit_ln_fm(y_sb, xnb, 8, 512, F32, gamma_beta=(gb, 0),
                         sq_sb=sq, stat_sb=st, pbase=0,
                         offs=[o + 1 + t0 - 1 for o in offs], x_stride=TE,
                         out_stride=512, out_off=0)
            for m in range(22):
                K.wstream_load(wb[m % 2], w1a_d, m, 8, f"dwa{m % 2}")
                ps_ = K.pb[2 + m % 2]
                pairs = [(wb[m % 2][:, k * 128:(k + 1) * 128],
                          xnb[:, k * 512:(k + 1) * 512],
                          [(wb[m % 2], ""), (xnb, "")]) for k in range(8)]
                K.matmul_acc(ps_[:, :512], pairs, (ps_, ""))
                p.op("scalar", lambda s, ps_=ps_, m=m:
                     s.activation(ab[:, m * 512:(m + 1) * 512], ps_[:, :512],
                                  AF.Identity, bias=b1a[:, m:m + 1]),
                     reads=[(ps_, ""), (b1a, "")], writes=[(ab, m)])
            for m in range(22):
                K.wstream_load(wb[m % 2], w1g_d, m, 8, f"dwa{m % 2}")
                ps_ = K.pb[2 + m % 2]
                pairs = [(wb[m % 2][:, k * 128:(k + 1) * 128],
                          xnb[:, k * 512:(k + 1) * 512],
                          [(wb[m % 2], ""), (xnb, "")]) for k in range(8)]
                K.matmul_acc(ps_[:, :512], pairs, (ps_, ""))
                tg = p.op("scalar", lambda s, ps_=ps_, m=m:
                          s.activation(gel[:, :512], ps_[:, :512],
                                       AF.Gelu, bias=b1g[:, m:m + 1]),
                          reads=[(ps_, ""), (b1g, "")], writes=[(gel, "")])
                p.op("vector", lambda v, m=m:
                     v.tensor_mul(ab[:, m * 512:(m + 1) * 512],
                                  ab[:, m * 512:(m + 1) * 512], gel[:, :512]),
                     deps=[tg], reads=[(ab, m), (gel, "")], writes=[(ab, m)])
            for m in range(8):
                K.wstream_load(wb[m % 2], w2_d, m, 22, f"dwa{m % 2}")
                ps_ = K.pb[2 + m % 2]
                pairs = [(wb[m % 2][:, k * 128:(k + 1) * 128],
                          ab[:, k * 512:(k + 1) * 512],
                          [(wb[m % 2], ""), (ab, k)]) for k in range(22)]
                K.matmul_acc(ps_[:, :512], pairs, (ps_, ""))
                ob = oby[m % 2]
                ysl = y_sb[:, m * TE + 1 + t0: m * TE + 1 + t0 + 512]
                ta = p.op("vector", lambda v, ob=ob, ps_=ps_, ysl=ysl, m=m:
                          v.scalar_tensor_tensor(ob[:, :512], ps_[:, :512],
                                                 b2[:, m:m + 1], ysl,
                                                 ALU.add, ALU.add),
                          reads=[(ps_, ""), (b2, ""), (y_sb, "")],
                          writes=[(ob, "")])
                p.dma("sync", lambda e, m=m, tc=tc, ob=ob: e.dma_start(
                    out=yo_o[:, m * T + tc * 512: m * T + (tc + 1) * 512],
                    in_=ob[:, :512]),
                    sem=f"doy{m % 2}", deps=[ta], reads=[(ob, "")])
    p.barrier_all()


def build_final(T):
    K = KB(T=T)
    io = {}
    io["a"] = K.dram_in("a", [128, 8 * T], F32)
    io["b"] = K.dram_in("b", [128, 8 * T], F32)
    io["o"] = K.dram_out("o", [128, 8 * T], F32)
    with K.es:
        emit_final(K, io)
        K.p.finalize()
        K.p.emit()
    return K


def emit_final(K, io):
    T = K.T
    p = K.p
    a_d, b_d, o_d = io["a"], io["b"], io["o"]
    with K.scope("fz"):
        at = K.sb("at", [128, 8 * T], F32)
        bt = K.sb("bt", [128, 8 * T], F32)
        p.dma("sync", lambda e: e.dma_start(out=at[:], in_=a_d), sem="da",
              writes=[(at, "")])
        p.dma("sync", lambda e: e.dma_start(out=bt[:], in_=b_d), sem="db",
              writes=[(bt, "")])
        for m in range(8):
            ta = p.op("vector", lambda v, m=m: v.tensor_add(
                at[:, m * T:(m + 1) * T], at[:, m * T:(m + 1) * T],
                bt[:, m * T:(m + 1) * T]),
                reads=[(at, ""), (bt, "")], writes=[(at, "")])
            p.dma("sync", lambda e, m=m: e.dma_start(
                out=o_d[:, m * T:(m + 1) * T], in_=at[:, m * T:(m + 1) * T]),
                sem=f"do{m % 2}", deps=[ta], reads=[(at, "")])
    p.barrier_all()



NCORES = 8
TLOC = 2048
SFULL = 8192

_build_cache = {}


def _get(name, builder, *args):
    if name not in _build_cache:
        _build_cache[name] = builder(*args)
    return _build_cache[name]


def _run(K, in_maps):
    res = run_bass_kernel_spmd(K.nc, in_maps, core_ids=list(range(NCORES)))
    return res.results


def kernel(**inputs):
    if FUSED:
        return kernel_fused(inputs)
    return kernel_multi(**inputs)


def kernel_multi(L0_attn_X, L0_attn_Wq, L0_attn_Wk, L0_attn_Wv, L0_attn_Wo,
           L0_attn_We, L0_ffn_shared_w1, L0_ffn_shared_w2, L0_ffn_expert_w1,
           L0_ffn_expert_w2, L0_ffn_router_w, L0_ffn_ln_gamma, L0_ffn_ln_beta,
           L1_attn_Wq, L1_attn_Wk, L1_attn_Wv, L1_attn_Wo, L1_ffn_Wqkv,
           L1_ffn_q_scale, L1_ffn_k_scale, L1_ffn_Wo, L1_ffn_ln_g,
           L1_ffn_ln_b, L1_ffn_W1, L1_ffn_b1, L1_ffn_W2, L1_ffn_b2):
    bf16 = ml_dtypes.bfloat16
    f32 = np.float32
    T, S = TLOC, SFULL
    X = np.asarray(L0_attn_X, f32)

    def gb_cols(g, b):
        out = np.zeros((128, 16), f32)
        out[:, 0:8] = np.asarray(g, f32).reshape(8, 128).T
        out[:, 8:16] = np.asarray(b, f32).reshape(8, 128).T
        return out

    x_sh = []
    for c in range(NCORES):
        b, t0 = c // 4, (c % 4) * T
        x_sh.append(host_fm(X[b, t0:t0 + T]).reshape(128, 8 * T))

    K1 = _get("pre0", build_pre0, T)
    wq0 = host_tile_w(np.asarray(L0_attn_Wq, f32).T)
    wk0 = host_tile_w(np.asarray(L0_attn_Wk, f32).T)
    wv0 = host_tile_w(np.asarray(L0_attn_Wv, f32).T)
    we0 = host_tile_w(np.asarray(L0_attn_We, f32).T)
    gb0 = gb_cols(L0_ffn_ln_gamma, L0_ffn_ln_beta)
    r1 = _run(K1, [dict(x=x_sh[c], wq=wq0, wk=wk0, wv=wv0, we=we0, gb=gb0)
                   for c in range(NCORES)])

    K2 = _get("attn0", build_attn0, T, S)
    wo0 = host_tile_w(np.asarray(L0_attn_Wo, f32).T)
    kf = [np.concatenate([r1[g * 4 + i]["k"] for i in range(4)], axis=2)
          for g in range(2)]
    vf = [np.concatenate([r1[g * 4 + i]["v"] for i in range(4)], axis=2)
          for g in range(2)]
    cf = [np.concatenate([r1[g * 4 + i]["ctx"] for i in range(4)], axis=2)
          for g in range(2)]
    r2 = _run(K2, [dict(q=r1[c]["q"], kf=kf[c // 4], vf=vf[c // 4],
                        cf=cf[c // 4], cq=r1[c]["ctx"], wo=wo0, x=x_sh[c])
                   for c in range(NCORES)])

    K3 = _get("moe0", build_moe0, T)
    w1n = np.concatenate([np.asarray(L0_ffn_shared_w1, f32)[None],
                          np.asarray(L0_ffn_expert_w1, f32)]
                         ).reshape(9, 16, 128, 1024)
    w2n = np.concatenate([np.asarray(L0_ffn_shared_w2, f32)[None],
                          np.asarray(L0_ffn_expert_w2, f32)]
                         ).reshape(9, 8, 128, 2048)
    wr = host_tile_w(np.asarray(L0_ffn_router_w, f32).T)
    r3 = _run(K3, [dict(h=r1[c]["h"], xa=r2[c]["xa"], w1n=w1n, w2n=w2n, wr=wr)
                   for c in range(NCORES)])
    x1 = [r3[c]["x1"] for c in range(NCORES)]

    K4 = _get("pre1", build_pre1, T)
    wq1 = host_rhs(np.asarray(L1_attn_Wq, f32).T)
    wk1 = host_rhs(np.asarray(L1_attn_Wk, f32).T)
    wv1 = host_tile_w(np.asarray(L1_attn_Wv, f32).T)
    DK = 64
    inv = 1.0 / 10000.0 ** (np.arange(0, DK, 2, dtype=np.float64) / DK)
    tt = np.arange(S, dtype=np.float64)
    fr = np.outer(tt, inv)
    cosS, sinS = np.cos(fr), np.sin(fr)
    idx = (np.arange(1024) % 64) // 2
    in4 = []
    for c in range(NCORES):
        t0 = (c % 4) * T
        in4.append(dict(x=x1[c], wq=wq1, wk=wk1, wv=wv1,
                        cosE=cosS[t0:t0 + T][:, idx].astype(f32),
                        sinE=sinS[t0:t0 + T][:, idx].astype(f32)))
    r4 = _run(K4, in4)

    K5 = _get("attn1", build_attn1, T, S)
    wo1 = host_tile_w(np.asarray(L1_attn_Wo, f32).T)
    kf1 = [np.concatenate([r4[g * 4 + i]["k"] for i in range(4)], axis=2)
           for g in range(2)]
    vf1 = [np.concatenate([r4[g * 4 + i]["v"] for i in range(4)], axis=2)
           for g in range(2)]
    in5 = []
    for c in range(NCORES):
        t0 = (c % 4) * T
        kpos = np.arange(S).reshape(64, 128)[:, :, None]
        qpos = (t0 + np.arange(T).reshape(4, 512))[:, None, None, :]
        cm = (kpos[None] <= qpos).astype(bf16)
        in5.append(dict(q=r4[c]["q"], kf=kf1[c // 4], vf=vf1[c // 4],
                        cmask=cm, wo=wo1, x=x1[c]))
    r5 = _run(K5, in5)
    xacc = [r5[c]["xa"] for c in range(NCORES)]

    KA = _get("loc_attn", build_loc_attn, T)
    KF = _get("loc_ffn", build_loc_ffn, T)
    y = [np.ascontiguousarray(
        np.asarray(r4[c]["xn"], bf16).astype(f32).reshape(128, 8 * T))
        for c in range(NCORES)]
    HLO = 257
    qsl = np.asarray(L1_ffn_q_scale, f32)
    ksl = np.asarray(L1_ffn_k_scale, f32)
    tabs_c = []
    masks_c = []
    for c in range(NCORES):
        t0 = (c % 4) * T
        cols_glob = np.arange(t0 - HLO, t0 + T)
        tabs_c.append(host_loc_tables(cols_glob, None, None))
        ma, mb = host_loc_masks(T, t0, 0)
        masks_c.append((ma.astype(bf16), mb.astype(bf16)))
    tile16 = lambda a: np.ascontiguousarray(
        np.tile(np.tile(a.astype(f32), 16)[None, :], (128, 1)))
    for i in range(6):
        wqkv = host_rhs(np.asarray(L1_ffn_Wqkv, f32)[i])
        woL = host_tile_w(np.asarray(L1_ffn_Wo, f32)[i])
        qscl = tile16(8.0 * qsl[i])
        kscl = tile16(ksl[i])
        inA = []
        for c in range(NCORES):
            yv = y[c].reshape(128, 8, T)
            if c % 4 == 0:
                halo = np.zeros((128, 8, HLO), f32)
            else:
                halo = y[c - 1].reshape(128, 8, T)[:, :, T - HLO:]
            y_ext = np.concatenate([halo, yv], axis=2).reshape(128, 8 * (T + HLO))
            qa, qb, ka, kb = tabs_c[c]
            ma, mb = masks_c[c]
            inA.append(dict(y=y_ext, wqkv=wqkv, wo=woL,
                            qa=qa, qb=qb, ka=ka, kb=kb,
                            qscl=qscl, kscl=kscl,
                            maskA=ma, maskB=mb))
        rA = _run(KA, inA)
        y = [np.ascontiguousarray(np.asarray(rA[c]["yo"], f32))
             for c in range(NCORES)]

        gbL = gb_cols(np.asarray(L1_ffn_ln_g, f32)[i],
                      np.asarray(L1_ffn_ln_b, f32)[i])
        w1a = host_tile_w(np.asarray(L1_ffn_W1, f32)[i][:, :2730])
        w1g = host_tile_w(np.asarray(L1_ffn_W1, f32)[i][:, 2730:])
        b1a = host_bias_cols(np.asarray(L1_ffn_b1, f32)[i][:2730])
        b1g = host_bias_cols(np.asarray(L1_ffn_b1, f32)[i][2730:])
        w2L = host_tile_w(np.asarray(L1_ffn_W2, f32)[i])
        b2L = host_bias_cols(np.asarray(L1_ffn_b2, f32)[i])
        inF = []
        for c in range(NCORES):
            yv = y[c].reshape(128, 8, T)
            if c % 4 == 0:
                halo = np.zeros((128, 8, 1), f32)
            else:
                halo = y[c - 1].reshape(128, 8, T)[:, :, T - 1:]
            y_ext = np.concatenate([halo, yv], axis=2).reshape(128, 8 * (T + 1))
            inF.append(dict(y=y_ext, gb=gbL, w1a=w1a, w1g=w1g, b1a=b1a,
                            b1g=b1g, w2=w2L, b2=b2L))
        rF = _run(KF, inF)
        y = [np.ascontiguousarray(np.asarray(rF[c]["yo"], f32))
             for c in range(NCORES)]

    KZ = _get("final", build_final, T)
    rZ = _run(KZ, [dict(a=xacc[c], b=y[c]) for c in range(NCORES)])

    out = np.zeros((2, S, 1024), f32)
    for c in range(NCORES):
        b, t0 = c // 4, (c % 4) * T
        out[b, t0:t0 + T] = host_unfm(
            np.asarray(rZ[c]["o"], f32).reshape(128, 8, T))
    return out



def build_fused(T, S):
    K = KB(T=T)
    nc = K.nc
    p = K.p
    HLO = 257
    TE = T + HLO

    def scr(name, shape, dtype):
        return nc.dram_tensor(name, list(shape), dtype, kind="Internal").ap()

    ein = {}

    def din(name, shape, dtype=F32):
        ein[name] = (shape, dtype)
        return K.dram_in(name, list(shape), dtype)

    x_d = din("z_x", [128, 8 * T])
    out_d = K.dram_out("z_out", [128, 8 * T], F32)
    io0 = dict(x=x_d,
               wq=din("p0_wq", [8, 8, 128, 128]),
               wk=din("p0_wk", [8, 8, 128, 128]),
               wv=din("p0_wv", [8, 8, 128, 128]),
               we=din("p0_we", [1, 8, 128, 128]),
               gb=din("p0_gb", [128, 16]),
               q=scr("s_q0", [128, 8, T], BF16),
               k=scr("s_k0", [128, 8, T], BF16),
               v=scr("s_v0", [128, 8, T], BF16),
               ctx=scr("s_c0", [128, 1, T], BF16),
               h=scr("s_h0", [128, 8, T], BF16))
    k0g = scr("s_k0g", [512, 8 * T], BF16)
    v0g = scr("s_v0g", [512, 8 * T], BF16)
    c0g = scr("s_c0g", [512, T], BF16)
    xa_s = scr("s_xa", [128, 8 * T], F32)
    io2 = dict(q=io0["q"], kf=k0g, vf=v0g, cf=c0g, cq=io0["ctx"],
               wo=din("a0_wo", [8, 8, 128, 128]), x=x_d, xa=xa_s)
    x1_s = scr("s_x1", [128, 8 * T], F32)
    io3 = dict(h=io0["h"], xa=xa_s,
               w1n=din("mo_w1n", [9, 16, 128, 1024]),
               w2n=din("mo_w2n", [9, 8, 128, 2048]),
               wr=din("mo_wr", [1, 8, 128, 128]),
               x1=x1_s)
    xnf_s = scr("s_xnf", [128, 8 * T], F32)
    io4 = dict(x=x1_s,
               wq=din("p1_wq", [128, 8, 1024]),
               wk=din("p1_wk", [128, 8, 1024]),
               wv=din("p1_wv", [8, 8, 128, 128]),
               cosE=din("p1_cos", [T, 1024]),
               sinE=din("p1_sin", [T, 1024]),
               xn=scr("s_xn1", [128, 8, T], BF16),
               q=scr("s_q1", [128, 8, T], BF16),
               k=scr("s_k1", [128, 8, T], BF16),
               v=scr("s_v1", [128, 8, T], BF16),
               xnf=xnf_s)
    k1g = scr("s_k1g", [512, 8 * T], BF16)
    v1g = scr("s_v1g", [512, 8 * T], BF16)
    xacc_s = scr("s_xacc", [128, 8 * T], F32)
    io5 = dict(q=io4["q"], kf=k1g, vf=v1g,
               cmask=din("a1_cm", [T // 512, S // 128, 128, 512], BF16),
               wo=din("a1_wo", [8, 8, 128, 128]), x=x1_s, xa=xacc_s)
    lt = dict(qa=din("lt_qa", [TE, 1024]), qb=din("lt_qb", [TE, 1024]),
              ka=din("lt_ka", [TE, 1024]), kb=din("lt_kb", [TE, 1024]),
              maskA=din("lt_ma", [(T // 512) * 4, 128, 384], BF16),
              maskB=din("lt_mb", [(T // 512) * 6, 128, 512], BF16))
    hsel_d = din("lt_hsel", [128, 4])
    locA = []
    locF = []
    for i in range(6):
        locA.append(dict(wqkv=din(f"l{i}_wqkv", [128, 8, 3072]),
                         wo=din(f"l{i}_wo", [8, 8, 128, 128]),
                         qscl=din(f"l{i}_qscl", [128, 1024]),
                         kscl=din(f"l{i}_kscl", [128, 1024])))
        locF.append(dict(gb=din(f"l{i}_gb", [128, 16]),
                         w1a=din(f"l{i}_w1a", [22, 8, 128, 128]),
                         w1g=din(f"l{i}_w1g", [22, 8, 128, 128]),
                         b1a=din(f"l{i}_b1a", [128, 22]),
                         b1g=din(f"l{i}_b1g", [128, 22]),
                         w2=din(f"l{i}_w2", [8, 22, 128, 128]),
                         b2=din(f"l{i}_b2", [128, 8])))
    y_a = scr("s_ya", [128, 8 * T], F32)
    y_b = scr("s_yb", [128, 8 * T], F32)
    yxA = scr("s_yxA", [128, 8 * TE], F32)
    yxF = scr("s_yxF", [128, 8 * (T + 1)], F32)
    ccinE = scr("s_ccinE", [128, 8 * HLO], F32)
    ccoutE = scr("s_ccoutE", [512, 8 * HLO], F32)
    ccinF = scr("s_ccinF", [128, 8], F32)
    ccoutF = scr("s_ccoutF", [512, 8], F32)
    groups = [[0, 1, 2, 3], [4, 5, 6, 7]]

    def cc_gather(cin2d, cout, nm):
        p.op("gpsimd", lambda g: g.collective_compute(
            "AllGather", ALU.bypass, replica_groups=groups,
            ins=[cin2d], outs=[cout]), tag=nm)
        p.barrier_all()

    def halo_exchange(y_cur, y_ext, H, scope_nm):
        cin = ccinE if H > 1 else ccinF
        cout = ccoutE if H > 1 else ccoutF
        with K.scope(scope_nm):
            hbuf = K.sb("hbuf", [128, 8 * H], F32)
            hacc = K.sb("hacc", [128, 8 * H], F32)
            hsel = K.sb("hsel", [128, 4], F32)
            p.dma("sync", lambda e: e.dma_start(out=hsel[:], in_=hsel_d),
                  sem="dhsel", writes=[(hsel, "")])
            p.dma("sync", lambda e: e.dma_start(
                out=y_ext.rearrange("p (c t) -> p c t", t=T + H)[:, :, H:],
                in_=y_cur.rearrange("p (c t) -> p c t", t=T)), sem="dyx")
            with K.nc.allow_non_contiguous_dma(reason="tiny halo edge"):
                p.dma("sync", lambda e: e.dma_start(
                    out=cin.rearrange("p (c t) -> p c t", t=H),
                    in_=y_cur.rearrange("p (c t) -> p c t", t=T)[:, :, T - H:],
                    allow_slow_non_contiguous=(H == 1)),
                    sem="dyE")
            p.barrier_all()
            p.op("gpsimd", lambda g: g.collective_compute(
                "AllGather", ALU.bypass, replica_groups=groups,
                ins=[cin], outs=[cout]))
            p.barrier_all()
            tz = p.op("gpsimd", lambda g: g.memset(hacc[:, :8 * H], 0.0),
                      writes=[(hacc, "")])
            for r in range(4):
                tl = p.dma("sync", lambda e, r=r: e.dma_start(
                    out=hbuf[:, :8 * H],
                    in_=cout[r * 128:(r + 1) * 128, :]),
                    sem="dhb", writes=[(hbuf, "")])
                p.op("vector", lambda v, r=r:
                     v.scalar_tensor_tensor(hacc[:, :8 * H], hbuf[:, :8 * H],
                                            hsel[:, r:r + 1], hacc[:, :8 * H],
                                            ALU.mult, ALU.add),
                     deps=[tl], reads=[(hbuf, ""), (hsel, ""), (hacc, "")],
                     writes=[(hacc, "")])
            with K.nc.allow_non_contiguous_dma(reason="tiny halo cols"):
                td = p.dma("sync", lambda e: e.dma_start(
                    out=y_ext.rearrange("p (c t) -> p c t", t=T + H)[:, :, 0:H],
                    in_=hacc[:].rearrange("p (c t) -> p c t", t=H),
                    allow_slow_non_contiguous=(H == 1)),
                    sem="dyH", reads=[(hacc, "")])
        p.barrier_all()

    with K.es:
        K.setup_common(npb=5, stats=False)
        emit_pre0(K, io0)
        cc_gather(io0["k"].rearrange("p m t -> p (m t)"), k0g, "cck0")
        cc_gather(io0["v"].rearrange("p m t -> p (m t)"), v0g, "ccv0")
        cc_gather(io0["ctx"].rearrange("p m t -> p (m t)"), c0g, "ccc0")
        emit_attn(K, io2, S, influence=True, masked=False, scope="a0",
                  gathered=True)
        emit_moe0(K, io3)
        emit_pre1(K, io4)
        cc_gather(io4["k"].rearrange("p m t -> p (m t)"), k1g, "cck1")
        cc_gather(io4["v"].rearrange("p m t -> p (m t)"), v1g, "ccv1")
        emit_attn(K, io5, S, influence=False, masked=True, scope="a1",
                  gathered=True)
        y_cur, y_nxt = xnf_s, y_a
        for i in range(6):
            halo_exchange(y_cur, yxA, HLO, f"hxA{i}")
            ioA = dict(y=yxA, yo=y_nxt, **lt, **locA[i])
            emit_loc_attn(K, ioA, scope=f"la{i}")
            y_cur, y_nxt = y_nxt, (y_b if y_nxt is y_a else y_a)
            halo_exchange(y_cur, yxF, 1, f"hxF{i}")
            ioF = dict(y=yxF, yo=y_nxt, **locF[i])
            emit_loc_ffn(K, ioF, scope=f"lf{i}")
            y_cur, y_nxt = y_nxt, (y_b if y_nxt is y_a else y_a)
        emit_final(K, dict(a=xacc_s, b=y_cur, o=out_d))
        K.p.finalize()
        K.p.emit()
    return K


FUSED = True


def kernel_fused(ins):
    bf16 = ml_dtypes.bfloat16
    f32 = np.float32
    T, S, HLO = TLOC, SFULL, 257
    X = np.asarray(ins["L0_attn_X"], f32)

    def gb_cols(g, b):
        out = np.zeros((128, 16), f32)
        out[:, 0:8] = np.asarray(g, f32).reshape(8, 128).T
        out[:, 8:16] = np.asarray(b, f32).reshape(8, 128).T
        return out

    KF = _get("fused", build_fused, T, S)

    shared = {
        "p0_wq": host_tile_w(np.asarray(ins["L0_attn_Wq"], f32).T),
        "p0_wk": host_tile_w(np.asarray(ins["L0_attn_Wk"], f32).T),
        "p0_wv": host_tile_w(np.asarray(ins["L0_attn_Wv"], f32).T),
        "p0_we": host_tile_w(np.asarray(ins["L0_attn_We"], f32).T),
        "p0_gb": gb_cols(ins["L0_ffn_ln_gamma"], ins["L0_ffn_ln_beta"]),
        "a0_wo": host_tile_w(np.asarray(ins["L0_attn_Wo"], f32).T),
        "mo_w1n": np.concatenate(
            [np.asarray(ins["L0_ffn_shared_w1"], f32)[None],
             np.asarray(ins["L0_ffn_expert_w1"], f32)]).reshape(
                 9, 16, 128, 1024),
        "mo_w2n": np.concatenate(
            [np.asarray(ins["L0_ffn_shared_w2"], f32)[None],
             np.asarray(ins["L0_ffn_expert_w2"], f32)]).reshape(
                 9, 8, 128, 2048),
        "mo_wr": host_tile_w(np.asarray(ins["L0_ffn_router_w"], f32).T),
        "p1_wq": host_rhs(np.asarray(ins["L1_attn_Wq"], f32).T),
        "p1_wk": host_rhs(np.asarray(ins["L1_attn_Wk"], f32).T),
        "p1_wv": host_tile_w(np.asarray(ins["L1_attn_Wv"], f32).T),
        "a1_wo": host_tile_w(np.asarray(ins["L1_attn_Wo"], f32).T),
    }
    tile16 = lambda a: np.ascontiguousarray(
        np.tile(np.tile(a.astype(f32), 16)[None, :], (128, 1)))
    qsl = np.asarray(ins["L1_ffn_q_scale"], f32)
    ksl = np.asarray(ins["L1_ffn_k_scale"], f32)
    for i in range(6):
        shared[f"l{i}_wqkv"] = host_rhs(np.asarray(ins["L1_ffn_Wqkv"], f32)[i])
        shared[f"l{i}_wo"] = host_tile_w(np.asarray(ins["L1_ffn_Wo"], f32)[i])
        shared[f"l{i}_qscl"] = tile16(8.0 * qsl[i])
        shared[f"l{i}_kscl"] = tile16(ksl[i])
        shared[f"l{i}_gb"] = gb_cols(np.asarray(ins["L1_ffn_ln_g"], f32)[i],
                                     np.asarray(ins["L1_ffn_ln_b"], f32)[i])
        shared[f"l{i}_w1a"] = host_tile_w(
            np.asarray(ins["L1_ffn_W1"], f32)[i][:, :2730])
        shared[f"l{i}_w1g"] = host_tile_w(
            np.asarray(ins["L1_ffn_W1"], f32)[i][:, 2730:])
        shared[f"l{i}_b1a"] = host_bias_cols(
            np.asarray(ins["L1_ffn_b1"], f32)[i][:2730])
        shared[f"l{i}_b1g"] = host_bias_cols(
            np.asarray(ins["L1_ffn_b1"], f32)[i][2730:])
        shared[f"l{i}_w2"] = host_tile_w(np.asarray(ins["L1_ffn_W2"], f32)[i])
        shared[f"l{i}_b2"] = host_bias_cols(np.asarray(ins["L1_ffn_b2"], f32)[i])

    DK = 64
    inv = 1.0 / 10000.0 ** (np.arange(0, DK, 2, dtype=np.float64) / DK)
    tt = np.arange(S, dtype=np.float64)
    fr = np.outer(tt, inv)
    cosS, sinS = np.cos(fr), np.sin(fr)
    idx = (np.arange(1024) % 64) // 2

    in_maps = []
    for c in range(NCORES):
        b, t0 = c // 4, (c % 4) * T
        im = dict(shared)
        im["z_x"] = host_fm(X[b, t0:t0 + T]).reshape(128, 8 * T)
        im["p1_cos"] = cosS[t0:t0 + T][:, idx].astype(f32)
        im["p1_sin"] = sinS[t0:t0 + T][:, idx].astype(f32)
        kpos = np.arange(S).reshape(64, 128)[:, :, None]
        qpos = (t0 + np.arange(T).reshape(4, 512))[:, None, None, :]
        im["a1_cm"] = (kpos[None] <= qpos).astype(bf16)
        cols_glob = np.arange(t0 - HLO, t0 + T)
        qa, qb, ka, kb = host_loc_tables(cols_glob)
        im["lt_qa"], im["lt_qb"], im["lt_ka"], im["lt_kb"] = qa, qb, ka, kb
        ma, mb = host_loc_masks(T, t0, 0)
        im["lt_ma"] = ma.astype(bf16)
        im["lt_mb"] = mb.astype(bf16)
        hs = np.zeros((128, 4), f32)
        if c % 4 > 0:
            hs[:, (c % 4) - 1] = 1.0
        im["lt_hsel"] = hs
        in_maps.append(im)

    res = run_bass_kernel_spmd(KF.nc, in_maps, core_ids=list(range(NCORES)))
    out = np.zeros((2, S, 1024), f32)
    for c in range(NCORES):
        b, t0 = c // 4, (c % 4) * T
        out[b, t0:t0 + T] = host_unfm(
            np.asarray(res.results[c]["z_out"], f32).reshape(128, 8, T))
    return out


# version 09 (final)
# speedup vs baseline: 1.0915x; 1.0915x over previous
import numpy as np
import ml_dtypes
from contextlib import ExitStack

import concourse.bass as bass
import concourse.mybir as mybir
from concourse.bass_utils import run_bass_kernel_spmd

F32 = mybir.dt.float32
BF16 = mybir.dt.bfloat16
AF = mybir.ActivationFunctionType
ALU = mybir.AluOpType
AX = mybir.AxisListType

ENGINES = ("tensor", "vector", "scalar", "gpsimd", "sync")



def _rk(key):
    if isinstance(key, tuple):
        t, tag = key
        return (t if isinstance(t, str) else t.name, tag)
    return (key if isinstance(key, str) else key.name, "")


class _Op:
    __slots__ = ("engine", "fn", "pos", "signal", "rank", "waits", "know",
                 "dma_sem", "dma_val", "tag")

    def __init__(self, engine, fn, pos, tag=""):
        self.engine = engine
        self.fn = fn
        self.pos = pos
        self.signal = False
        self.rank = None
        self.waits = []
        self.know = None
        self.dma_sem = None
        self.dma_val = None
        self.tag = tag


class Prog:

    def __init__(self, nc):
        self.nc = nc
        self.ops = {e: [] for e in ENGINES}
        self.know = {e: {} for e in ENGINES}
        self.dma_counts = {}
        self.regions = {}
        self.dma_sem_names = []
        self.last_dma = {}
        self._emitted = False


    def _tok_know(self, op):
        k = dict(self.know[op.engine])
        k[("e", op.engine)] = op.pos
        if op.dma_sem is not None:
            k[("d", op.dma_sem)] = op.dma_val
        return k

    def _covered(self, engine, dep):
        know = self.know[engine]
        if isinstance(dep, _Op):
            if dep.dma_sem is not None:
                return know.get(("d", dep.dma_sem), -1) >= dep.dma_val
            return know.get(("e", dep.engine), -1) >= dep.pos
        raise TypeError(dep)

    def _absorb(self, engine, dep):
        know = self.know[engine]
        for k, v in dep.know.items():
            if know.get(k, -1) < v:
                know[k] = v

    def op(self, engine, fn, deps=(), reads=(), writes=(), dma_sem=None,
           tag=""):
        assert not self._emitted
        reads = [_rk(r) for r in reads]
        writes = [_rk(w) for w in writes]
        alldeps = list(deps)
        for r in reads:
            reg = self.regions.get(r)
            if reg and reg[0] is not None:
                alldeps.append(reg[0])
        for w in writes:
            reg = self.regions.get(w)
            if reg:
                if reg[0] is not None:
                    alldeps.append(reg[0])
                alldeps.extend(reg[1])

        pos = len(self.ops[engine])
        o = _Op(engine, fn, pos, tag)
        if dma_sem is not None:
            self.dma_counts.setdefault(dma_sem, 0)
            if dma_sem not in self.dma_sem_names:
                self.dma_sem_names.append(dma_sem)
            self.dma_counts[dma_sem] += 16
            o.dma_sem = dma_sem
            o.dma_val = self.dma_counts[dma_sem]
            self.last_dma[dma_sem] = o

        for dep in alldeps:
            if dep is o:
                continue
            assert isinstance(dep, _Op)
            if dep.dma_sem is None and dep.engine == engine \
                    and engine == "tensor":
                self._absorb(engine, dep)
                continue
            if self._covered(engine, dep):
                continue
            if dep.dma_sem is None:
                dep.signal = True
            o.waits.append(dep)
            self._absorb(engine, dep)

        o.know = self._tok_know(o)
        self.ops[engine].append(o)

        for r in reads:
            reg = self.regions.setdefault(r, [None, []])
            reg[1].append(o)
        for w in writes:
            self.regions[w] = [o, []]
        return o

    def dma(self, engine, fn, sem, deps=(), reads=(), writes=()):
        return self.op(engine, fn, deps=deps, reads=reads, writes=writes,
                       dma_sem=sem)

    def finalize(self):
        deps = [self.last_dma[s] for s in self.dma_sem_names]
        for e in ENGINES:
            if self.ops[e]:
                deps.append(self.ops[e][-1])
        self.op("sync", None, deps=deps)

    def barrier_all(self, engines=ENGINES):
        toks = []
        for e in engines:
            if self.ops[e]:
                toks.append(self.ops[e][-1])
        toks += [self.last_dma[s] for s in self.dma_sem_names]
        outs = []
        for e in engines:
            o = self.op(e, None, deps=toks)
            outs.append(o)
        return outs

    def emit(self, extra_final=None):
        assert not self._emitted
        self._emitted = True
        nc = self.nc
        es = ExitStack()
        with es:
            esems = {}
            for e in ENGINES:
                if any(o.signal for o in self.ops[e]):
                    esems[e] = es.enter_context(nc.semaphore(f"se_{e}"))
            dsems = {}
            for s in self.dma_sem_names:
                dsems[s] = es.enter_context(nc.semaphore(f"sd_{s}"))
            rank_of = {}
            for e in ENGINES:
                r = 0
                for o in self.ops[e]:
                    if o.signal:
                        r += 1
                        o.rank = r
                    rank_of[(e, o.pos)] = r
            block = es.enter_context(nc.Block())

            def emit_stream(eng_obj, ename):
                for o in self.ops[ename]:
                    for dep in o.waits:
                        if dep.dma_sem is not None:
                            eng_obj.wait_ge(dsems[dep.dma_sem], dep.dma_val)
                        else:
                            v = rank_of[(dep.engine, dep.pos)]
                            assert v > 0
                            eng_obj.wait_ge(esems[dep.engine], v)
                    ins = o.fn(eng_obj) if o.fn is not None else None
                    if o.dma_sem is not None:
                        assert ins is not None
                        ins.then_inc(dsems[o.dma_sem], 16)
                    elif o.signal:
                        if ins is None:
                            ins = eng_obj.nop()
                        ins.then_inc(esems[ename], 1)

            @block.tensor
            def _(t):
                emit_stream(t, "tensor")

            @block.vector
            def _(v):
                emit_stream(v, "vector")

            @block.scalar
            def _(s):
                emit_stream(s, "scalar")

            @block.gpsimd
            def _(g):
                emit_stream(g, "gpsimd")

            @block.sync
            def _(sy):
                emit_stream(sy, "sync")
                if extra_final is not None:
                    extra_final(sy)


def _ceil_div(a, b):
    return -(-a // b)


def host_tile_w(w):
    din, dout = w.shape
    nk, nm = _ceil_div(din, 128), _ceil_div(dout, 128)
    out = np.zeros((nm, nk, 128, 128), np.float32)
    for m in range(nm):
        for k in range(nk):
            blk = w[k * 128:(k + 1) * 128, m * 128:(m + 1) * 128]
            out[m, k, :blk.shape[0], :blk.shape[1]] = blk
    return out


def host_bias_cols(b):
    nm = _ceil_div(b.shape[0], 128)
    out = np.zeros((nm, 128), np.float32)
    out.reshape(-1)[:b.shape[0]] = b
    return np.ascontiguousarray(out.T)


def host_rhs(w_eff):
    din, dout = w_eff.shape
    return np.ascontiguousarray(
        w_eff.reshape(din // 128, 128, dout).transpose(1, 0, 2))


def host_fm(x):
    T, D = x.shape
    nc_ = D // 128
    return np.ascontiguousarray(x.T.reshape(nc_, 128, T).transpose(1, 0, 2))


def host_unfm(x_fm):
    p, nch, T = x_fm.shape
    return np.ascontiguousarray(x_fm.transpose(1, 0, 2).reshape(nch * 128, T).T)



class KB:

    def __init__(self, T=2048, name="k"):
        self.nc = bass.Bass("TRN2", target_bir_lowering=False)
        self.p = Prog(self.nc)
        self.es = ExitStack()
        self.T = T
        self.inputs = {}
        self.sb_handles = {}
        self._dma_rr = 0
        self._psum_used = 0
        self.prefix = ""

    from contextlib import contextmanager as _cm

    @_cm
    def scope(self, prefix):
        old_prefix, old_es = self.prefix, self.es
        self.prefix = old_prefix + prefix + "_"
        es = ExitStack()
        self.es = es
        try:
            with es:
                yield
        finally:
            self.es = old_es
            self.prefix = old_prefix

    def dram_in(self, name, shape, dtype=F32):
        return self.nc.dram_tensor(name, list(shape), dtype,
                                   kind="ExternalInput").ap()

    def dram_out(self, name, shape, dtype=F32):
        return self.nc.dram_tensor(name, list(shape), dtype,
                                   kind="ExternalOutput").ap()

    def sb(self, name, shape, dtype):
        name = self.prefix + name
        h = self.es.enter_context(self.nc.sbuf_tensor(name, list(shape), dtype))
        self.sb_handles[name] = h
        return h

    def ps(self, name, shape, dtype=F32):
        return self.es.enter_context(
            self.nc.psum_tensor(self.prefix + name, list(shape), dtype))

    def setup_common(self, npb=6, stats=True):
        nc, p = self.nc, self.p
        self.ones_col_f = self.sb("ones_col_f", [128, 1], F32)
        self.ones_row_f = self.sb("ones_row_f", [1, 128], F32)
        self.ident_b = self.sb("ident_b", [128, 128], BF16)
        self.ident_f = self.sb("ident_f", [128, 128], F32)
        t1 = p.op("gpsimd", lambda g: g.memset(self.ones_col_f[:], 1.0),
                  writes=[(self.ones_col_f, "")])
        t2 = p.op("gpsimd", lambda g: g.memset(self.ones_row_f[:], 1.0),
                  writes=[(self.ones_row_f, "")])
        for ident in (self.ident_b, self.ident_f):
            tz = p.op("gpsimd", lambda g, i=ident: g.memset(i[:], 0.0),
                      writes=[(ident, "")])
            p.op("gpsimd",
                 lambda g, i=ident: g.affine_select(
                     out=i[:], in_=i[:], compare_op=ALU.not_equal, fill=1.0,
                     base=0, pattern=[[-1, 128]], channel_multiplier=1),
                 reads=[(ident, "")], writes=[(ident, "")])
        self.pb = [self.ps(f"pb{i}", [128, 512], F32) for i in range(npb)]
        if stats:
            self.pstat1 = self.ps("pstat1", [1, 512], F32)
            self.pstat2 = self.ps("pstat2", [1, 512], F32)


    def load_const(self, name, dram_ap, shape, dtype, sem="dconst"):
        t = self.sb(name, shape, dtype)
        tok = self.p.dma("sync", lambda e: e.dma_start(out=t[:], in_=dram_ap),
                         sem=sem, writes=[(t, "")])
        return t, tok

    def wstream_load(self, wbuf, w_dram_ap, m, nk, sem):
        src = w_dram_ap[m].rearrange("k p m -> p k m")
        dst = wbuf[:, :nk * 128].rearrange("p (k m) -> p k m", m=128)
        return self.p.dma("gpsimd",
                          lambda e: e.dma_start(out=dst, in_=src),
                          sem=sem, writes=[(wbuf, "")])

    def matmul_acc(self, psum_ap, pairs, psum_region, extra_reads=(),
                   deps=()):
        p = self.p
        toks = []
        n = len(pairs)
        for i, (lhsT, rhs, reads) in enumerate(pairs):
            tok = p.op("tensor",
                       lambda t, l=lhsT, r=rhs, s=(i == 0), e=(i == n - 1):
                       t.matmul(psum_ap, l, r, start=s, stop=e),
                       deps=deps if i == 0 else (),
                       reads=list(reads) + list(extra_reads),
                       writes=[psum_region] if i == 0 else [],
                       )
            toks.append(tok)
        p.regions[_rk(psum_region)] = [toks[-1], []]
        return toks[-1]

    def emit_ln_fm(self, x_sb, out_sb, nch, Tex, x_dtype,
                   gamma_beta=None, eps=1e-5, D=None, sq_sb=None,
                   stat_sb=None, ones_col_b=None, pbase=0, offs=None,
                   x_stride=None, out_stride=None, out_off=0):
        p = self.p
        offs = offs or [0] * nch
        xs = x_stride or Tex
        os_ = out_stride or Tex
        D = D or nch * 128
        tc = 512
        sq = sq_sb
        mu, rstd, st3 = stat_sb
        ones_f = self.ones_col_f
        for t0 in range(0, Tex, tc):
            n = min(tc, Tex - t0)
            s1 = self.pstat1
            s2 = self.pstat2
            pairs1 = []
            oc = ones_f if x_dtype == F32 else ones_col_b
            for c in range(nch):
                xap = x_sb[:, c * xs + offs[c] + t0: c * xs + offs[c] + t0 + n]
                pairs1.append((oc[:, 0:1], xap, [(x_sb, ""), (oc, "")]))
            tok_s1 = self.matmul_acc(s1[:, :n], pairs1, (s1, ""))
            for c in range(nch):
                xap = x_sb[:, c * xs + offs[c] + t0: c * xs + offs[c] + t0 + n]
                p.op("scalar",
                     lambda s, xap=xap, c=c, n=n:
                     s.activation(sq[:, (c % 2) * tc: (c % 2) * tc + n],
                                  xap, AF.Square),
                     reads=[(x_sb, "")], writes=[(sq, c % 2)])
                sqap = sq[:, (c % 2) * tc: (c % 2) * tc + n]
                tok_s2 = p.op("tensor",
                              lambda t, sqap=sqap, s=(c == 0), e=(c == nch - 1),
                              n=n:
                              t.matmul(s2[:, :n], ones_f[:, 0:1], sqap,
                                       start=s, stop=e),
                              reads=[(sq, c % 2), (ones_f, "")],
                              writes=[(s2, "")] if c == 0 else [])
            p.regions[_rk((s2, ""))] = [tok_s2, []]
            t_mu = p.op("vector",
                        lambda v, n=n: v.tensor_scalar_mul(mu[:, :n], s1[:, :n],
                                                           1.0 / D),
                        deps=[tok_s1], reads=[(s1, "")], writes=[(mu, "")])
            t_e2 = p.op("vector",
                        lambda v, n=n: v.tensor_scalar_mul(st3[:, :n],
                                                           s2[:, :n], 1.0 / D),
                        deps=[tok_s2], reads=[(s2, "")], writes=[(st3, "")])
            t_m2 = p.op("vector",
                        lambda v, n=n: v.tensor_mul(rstd[:, :n], mu[:, :n],
                                                    mu[:, :n]),
                        reads=[(mu, "")], writes=[(rstd, "")])
            t_v = p.op("vector",
                       lambda v, n=n: v.scalar_tensor_tensor(
                           rstd[:, :n], st3[:, :n], float(eps), rstd[:, :n],
                           ALU.add, ALU.subtract),
                       reads=[(st3, ""), (rstd, "")], writes=[(rstd, "")])
            t_sd = p.op("scalar",
                        lambda s, n=n: s.activation(rstd[:, :n], rstd[:, :n],
                                                    AF.Sqrt),
                        reads=[(rstd, "")], writes=[(rstd, "")])
            t_r = p.op("vector",
                       lambda v, n=n: v.reciprocal(rstd[:, :n], rstd[:, :n]),
                       reads=[(rstd, "")], writes=[(rstd, "")])
            bmu = self.pb[pbase]
            brs = self.pb[pbase + 1]
            tok_bmu = self.matmul_acc(bmu[:, :n],
                                      [(self.ones_row_f[0:1, :],
                                        mu[:, :n],
                                        [(mu, ""), (self.ones_row_f, "")])],
                                      (bmu, ""))
            tok_brs = self.matmul_acc(brs[:, :n],
                                      [(self.ones_row_f[0:1, :],
                                        rstd[:, :n],
                                        [(rstd, ""), (self.ones_row_f, "")])],
                                      (brs, ""))
            for c in range(nch):
                xap = x_sb[:, c * xs + offs[c] + t0: c * xs + offs[c] + t0 + n]
                oap = out_sb[:, c * os_ + out_off + t0:
                             c * os_ + out_off + t0 + n]
                tsub = p.op("vector",
                            lambda v, xap=xap, c=c, n=n:
                            v.tensor_sub(sq[:, (c % 2) * tc: (c % 2) * tc + n],
                                         xap, bmu[:, :n]),
                            deps=[tok_bmu], reads=[(x_sb, ""), (bmu, "")],
                            writes=[(sq, c % 2)])
                if gamma_beta is None:
                    p.op("vector",
                         lambda v, oap=oap, c=c, n=n:
                         v.tensor_mul(oap,
                                      sq[:, (c % 2) * tc: (c % 2) * tc + n],
                                      brs[:, :n]),
                         deps=[tok_brs], reads=[(sq, c % 2), (brs, "")],
                         writes=[(out_sb, "")])
                else:
                    gb_sb, goff = gamma_beta
                    p.op("vector",
                         lambda v, c=c, n=n:
                         v.tensor_mul(sq[:, (c % 2) * tc: (c % 2) * tc + n],
                                      sq[:, (c % 2) * tc: (c % 2) * tc + n],
                                      brs[:, :n]),
                         deps=[tok_brs], reads=[(sq, c % 2), (brs, "")],
                         writes=[(sq, c % 2)])
                    p.op("vector",
                         lambda v, oap=oap, c=c, n=n, gb=gb_sb, go=goff:
                         v.tensor_scalar(oap,
                                         sq[:, (c % 2) * tc: (c % 2) * tc + n],
                                         gb[:, go + c: go + c + 1],
                                         gb[:, go + nch + c: go + nch + c + 1],
                                         ALU.mult, ALU.add),
                         reads=[(sq, c % 2), (gb_sb, "")],
                         writes=[(out_sb, "")])

    def emit_project_fm(self, out_sb, x_sb, w_dram, nm, nk, Tex, evac,
                        wbufs, wsem, tc=512, pbase=2, npb=2):
        p = self.p
        for m in range(nm):
            wb = wbufs[m % len(wbufs)]
            self.wstream_load(wb, w_dram, m, nk, f"{wsem}{m % len(wbufs)}")
            for ti, t0 in enumerate(range(0, Tex, tc)):
                n = min(tc, Tex - t0)
                pbi = self.pb[pbase + (m * ((Tex + tc - 1) // tc) + ti) % npb]
                pairs = []
                for k in range(nk):
                    pairs.append((wb[:, k * 128:(k + 1) * 128],
                                  x_sb[:, k * Tex + t0: k * Tex + t0 + n],
                                  [(wb, ""), (x_sb, "")]))
                self.matmul_acc(pbi[:, :n], pairs, (pbi, ""))
                evac(m, t0, n, pbi)

    def mk_evac_store(self, obufs, out_dram, Tex, func=AF.Copy, scale=1.0,
                      bias_sb=None, bias_off=0, osem="dout", engine="scalar",
                      nm_total=None):
        p = self.p

        def evac(m, t0, n, psum):
            ob = obufs[m % len(obufs)]
            if engine == "scalar":
                bias = 0.0 if bias_sb is None else \
                    bias_sb[:, bias_off + m: bias_off + m + 1]
                rd = [(psum, "")] + ([] if bias_sb is None else [(bias_sb, "")])
                p.op("scalar",
                     lambda s: s.activation(ob[:, t0:t0 + n], psum[:, :n],
                                            func, bias=bias, scale=scale),
                     reads=rd, writes=[(ob, "")])
            else:
                assert bias_sb is None and func == AF.Copy
                p.op("vector",
                     lambda v: v.tensor_scalar_mul(ob[:, t0:t0 + n],
                                                   psum[:, :n], scale),
                     reads=[(psum, "")], writes=[(ob, "")])
            if t0 + n >= Tex and out_dram is not None:
                p.dma("sync",
                      lambda e: e.dma_start(out=out_dram[:, m, :],
                                            in_=ob[:, :Tex]),
                      sem=f"{osem}{m % len(obufs)}", reads=[(ob, "")])
        return evac



def build_pre0(T):
    K = KB(T=T)
    io = {}
    io["x"] = K.dram_in("x", [128, 8 * T], F32)
    io["wq"] = K.dram_in("wq", [8, 8, 128, 128], F32)
    io["wk"] = K.dram_in("wk", [8, 8, 128, 128], F32)
    io["wv"] = K.dram_in("wv", [8, 8, 128, 128], F32)
    io["we"] = K.dram_in("we", [1, 8, 128, 128], F32)
    io["gb"] = K.dram_in("gb", [128, 16], F32)
    io["q"] = K.dram_out("q", [128, 8, T], BF16)
    io["k"] = K.dram_out("k", [128, 8, T], BF16)
    io["v"] = K.dram_out("v", [128, 8, T], BF16)
    io["ctx"] = K.dram_out("ctx", [128, 1, T], BF16)
    io["h"] = K.dram_out("h", [128, 8, T], BF16)
    with K.es:
        K.setup_common(npb=5, stats=False)
        emit_pre0(K, io)
        K.p.finalize()
        K.p.emit(None)
    return K


def emit_pre0(K, io):
    T = K.T
    p = K.p
    x_d, wq_d, wk_d, wv_d, we_d, gb_d = (
        io["x"], io["wq"], io["wk"], io["wv"], io["we"], io["gb"])
    q_o, k_o, v_o, c_o, h_o = io["q"], io["k"], io["v"], io["ctx"], io["h"]
    with K.scope("p0"):
        K.pstat1 = K.ps("pstat1", [1, 512], F32)
        K.pstat2 = K.ps("pstat2", [1, 512], F32)
        xt = K.sb("xt", [128, 8 * T], F32)
        xn = K.sb("xn", [128, 8 * T], BF16)
        hh = K.sb("hh", [128, 8 * T], BF16)
        sq = K.sb("sq", [128, 1024], F32)
        st = [K.sb(f"st{i}", [1, 512], F32) for i in range(3)]
        ocb = K.sb("ocb", [128, 1], BF16)
        gb = K.sb("gbs", [128, 16], F32)
        wbufs = [K.sb(f"wbuf{i}", [128, 8 * 128], BF16) for i in range(2)]
        obufs = [K.sb(f"obuf{i}", [128, T], BF16) for i in range(2)]

        p.op("gpsimd", lambda g: g.memset(ocb[:], 1.0), writes=[(ocb, "")])
        p.dma("sync", lambda e: e.dma_start(out=gb[:], in_=gb_d),
              sem="dconst", writes=[(gb, "")])
        p.dma("sync", lambda e: e.dma_start(out=xt[:], in_=x_d),
              sem="dx", writes=[(xt, "")])

        K.emit_ln_fm(xt, xt, 8, T, F32, sq_sb=sq, stat_sb=st,
                     ones_col_b=ocb, pbase=0)
        p.op("scalar", lambda s: s.activation(xn[:], xt[:], AF.Copy),
             reads=[(xt, "")], writes=[(xn, "")])
        K.emit_ln_fm(xt, hh, 8, T, F32, gamma_beta=(gb, 0), sq_sb=sq,
                     stat_sb=st, ones_col_b=ocb, pbase=0)
        p.dma("sync", lambda e: e.dma_start(out=h_o[:, :, :].rearrange(
            "p m t -> p (m t)"), in_=hh[:]), sem="douth", reads=[(hh, "")])

        for (wd, od, scale) in ((wq_d, q_o, 0.125), (wk_d, k_o, 1.0),
                                (wv_d, v_o, 1.0)):
            evac = K.mk_evac_store(obufs, od, T, func=AF.Copy, scale=scale)
            K.emit_project_fm(None, xn, wd, 8, 8, T, evac,
                              wbufs, "w", pbase=2, npb=2)
        evac = K.mk_evac_store(obufs, c_o, T, func=AF.Copy, scale=1.0)
        K.emit_project_fm(None, xn, we_d, 1, 8, T, evac, wbufs, "w",
                          pbase=2, npb=2)
    p.barrier_all()


def build_attn0(T, S):
    K = KB(T=T)
    io = {}
    io["q"] = K.dram_in("q", [128, 8, T], BF16)
    io["kf"] = K.dram_in("kf", [128, 8, S], BF16)
    io["vf"] = K.dram_in("vf", [128, 8, S], BF16)
    io["cf"] = K.dram_in("cf", [128, 1, S], BF16)
    io["cq"] = K.dram_in("cq", [128, 1, T], BF16)
    io["wo"] = K.dram_in("wo", [8, 8, 128, 128], F32)
    io["x"] = K.dram_in("x", [128, 8 * T], F32)
    io["xa"] = K.dram_out("xa", [128, 8 * T], F32)
    with K.es:
        K.setup_common(npb=5, stats=False)
        emit_attn(K, io, S, influence=True, masked=False, scope="a0")
        K.p.finalize()
        K.p.emit()
    return K


def build_attn1(T, S):
    K = KB(T=T)
    io = {}
    io["q"] = K.dram_in("q", [128, 8, T], BF16)
    io["kf"] = K.dram_in("kf", [128, 8, S], BF16)
    io["vf"] = K.dram_in("vf", [128, 8, S], BF16)
    io["cmask"] = K.dram_in("cmask", [T // 512, S // 128, 128, 512], BF16)
    io["wo"] = K.dram_in("wo", [8, 8, 128, 128], F32)
    io["x"] = K.dram_in("x", [128, 8 * T], F32)
    io["xa"] = K.dram_out("xa", [128, 8 * T], F32)
    with K.es:
        K.setup_common(npb=5, stats=False)
        emit_attn(K, io, S, influence=False, masked=True, scope="a1")
        K.p.finalize()
        K.p.emit()
    return K


def emit_attn(K, io, S, influence, masked, scope, NH=16, debug_stop=None,
              gathered=False):
    T = K.T
    p = K.p
    q_d, kf_d, vf_d = io["q"], io["kf"], io["vf"]
    if influence:
        cf_d, cq_d = io["cf"], io["cq"]
    if masked:
        cm_d = io["cmask"]
    wo_d, x_d, xa_o = io["wo"], io["x"], io["xa"]
    NKC = S // 512
    NKT = S // 128
    NQT = T // 128
    NQB = T // 512

    def kv_slab(d, r0, c0):
        if not gathered:
            return [(d[r0:r0 + 64, c0, :], 0, S)]
        return [(d[r * 128 + r0: r * 128 + r0 + 64,
                   c0 * T:(c0 + 1) * T], r * T, T) for r in range(4)]

    with K.scope(scope):
        pb5 = K.ps("pb5", [128, 512], F32)
        q_sb = K.sb("q_sb", [128, 8 * T], BF16)
        xbv = [K.sb(f"xbv{i}", [128, T], F32) for i in range(2)]
        hb = K.sb("hb", [128, 8 * T], BF16)
        if influence:
            ck = K.sb("ck", [128, S], BF16)
            cqs = K.sb("cqs", [128, T], BF16)
        if masked:
            mbuf = [K.sb(f"mbuf{i}", [128, 512], BF16) for i in range(2)]
        kt_b = [K.sb(f"kt{i}", [65, S], BF16) for i in range(2)]
        vt_b = [K.sb(f"vt{i}", [128, NKT * 65], BF16) for i in range(2)]
        vtmp = [K.sb(f"vtmp{i}", [64, S], BF16) for i in range(2)]
        qt_b = [K.sb(f"qt{i}", [65, T], BF16) for i in range(2)]
        pbuf = [K.sb(f"pbuf{i}", [128, 512], BF16) for i in range(4)]
        am = K.sb("am", [128, NQT + 1], F32)
        rsum = K.sb("rsum", [1, 512], F32)
        rb = K.sb("rb", [64, 512], F32)
        wbufs = [K.sb(f"wbuf{i}", [128, 8 * 128], BF16) for i in range(2)]
        ptr = K.ps("ptr", [128, 512], BF16)
        psm = K.ps("psm", [1, 512], F32)

        p.dma("sync", lambda e: e.dma_start(
            out=q_sb[:], in_=q_d.rearrange("p m t -> p (m t)")), sem="din",
            writes=[(q_sb, "")])
        if influence:
            if not gathered:
                p.dma("sync", lambda e: e.dma_start(
                    out=ck[:], in_=cf_d.rearrange("p m t -> p (m t)")),
                    sem="dinc", writes=[(ck, "")])
            else:
                for r in range(4):
                    p.dma("sync", lambda e, r=r: e.dma_start(
                        out=ck[:, r * T:(r + 1) * T],
                        in_=cf_d[r * 128:(r + 1) * 128, 0:T]),
                        sem="dinc", writes=[(ck, "")])
            tcq = p.dma("sync", lambda e: e.dma_start(
                out=cqs[:], in_=cq_d.rearrange("p m t -> p (m t)")),
                sem="dinc2", writes=[(cqs, "")])
            p.op("vector", lambda v: v.tensor_scalar_mul(cqs[:], cqs[:], 0.1),
                 deps=[tcq], reads=[(cqs, "")], writes=[(cqs, "")])
        for i in range(2):
            p.op("gpsimd", lambda g, i=i: g.memset(kt_b[i][64:65, :], 1.0),
                 writes=[(kt_b[i], "ones")])
            vap = vt_b[i].ap().rearrange("p (t c) -> p t c", c=65)
            p.op("gpsimd", lambda g, i=i, vap=vap: g.memset(vap[:, :, 64:65],
                                                            1.0),
                 writes=[(vt_b[i], "ones")])

        def load_head(h):
            par = h % 2
            kt, vt, vq, qt = kt_b[par], vt_b[par], vtmp[par], qt_b[par]
            r0 = (h % 2) * 64
            c0 = h // 2
            for (srcap, off, ln) in kv_slab(kf_d, r0, c0):
                p.dma("gpsimd", lambda e, srcap=srcap, off=off, ln=ln:
                      e.dma_start(out=kt[0:64, off:off + ln], in_=srcap),
                      sem=f"dk{par}", writes=[(kt, "")])
            if debug_stop == "kt":
                return
            for (srcap, off, ln) in kv_slab(vf_d, r0, c0):
                p.dma("gpsimd", lambda e, srcap=srcap, off=off, ln=ln:
                      e.dma_start(out=vq[0:64, off:off + ln], in_=srcap),
                      sem=f"dv{par}", writes=[(vq, "")])
            if debug_stop == "vq":
                return
            for g4 in range(0 if debug_stop == "vt0" else NKT // 4):
                for u in range(4):
                    kt_i = g4 * 4 + u
                    p.op("tensor",
                         lambda t, kt_i=kt_i, u=u:
                         t.transpose(ptr[:, u * 128: u * 128 + 64],
                                     vq[0:64, kt_i * 128:(kt_i + 1) * 128],
                                     K.ident_b[0:64, 0:64]),
                         reads=[(vq, ""), (K.ident_b, "")],
                         writes=[(ptr, "")])
                src4 = ptr.ap().rearrange("p (q c) -> p q c", c=128)[:, :, 0:64]
                dst4 = vt.ap().rearrange("p (t c) -> p t c", c=65)[
                    :, g4 * 4:(g4 + 1) * 4, 0:64]
                p.op("vector",
                     lambda v, src4=src4, dst4=dst4:
                     v.tensor_copy(dst4, src4),
                     reads=[(ptr, "")], writes=[(vt, "")])
            if debug_stop == "vt":
                return
            p.dma("sync", lambda e: e.dma_start(
                out=qt[0:64, :],
                in_=q_sb[r0:r0 + 64, c0 * T:(c0 + 1) * T]),
                sem=f"dq{par}", reads=[(q_sb, "")], writes=[(qt, "lo")])

        def pass_a(h):
            par = h % 2
            kt = kt_b[par]
            qt = qt_b[par]
            for qt_i in range(NQT):
                lq = qt[0:64, qt_i * 128: qt_i * 128 + 128]
                lc = cqs[:, qt_i * 128: qt_i * 128 + 128] if influence else None
                for kc in range(NKC):
                    ps = K.pb[kc % 2]
                    rk = kt[0:64, kc * 512:(kc + 1) * 512]
                    mtok = p.op("tensor", lambda t, lq=lq, rk=rk, ps=ps,
                                e=not influence:
                                t.matmul(ps[:, :512], lq, rk, start=True,
                                         stop=e),
                                reads=[(qt, "lo"), (kt, "")], writes=[(ps, "")])
                    if influence:
                        rc = ck[:, kc * 512:(kc + 1) * 512]
                        mtok = p.op("tensor", lambda t, lc=lc, rc=rc, ps=ps:
                                    t.matmul(ps[:, :512], lc, rc, start=False,
                                             stop=True),
                                    reads=[(cqs, ""), (ck, "")])
                    p.regions[_rk((ps, ""))] = [mtok, []]
                    if kc == 0:
                        p.op("vector", lambda v, ps=ps, qt_i=qt_i:
                             v.reduce_max(am[:, qt_i:qt_i + 1], ps[:, :512],
                                          axis=AX.X),
                             reads=[(ps, "")], writes=[(am, qt_i)])
                    else:
                        t1 = p.op("vector", lambda v, ps=ps:
                                  v.reduce_max(am[:, NQT:NQT + 1],
                                               ps[:, :512], axis=AX.X),
                                  reads=[(ps, "")], writes=[(am, "tmp")])
                        p.op("vector", lambda v, qt_i=qt_i:
                             v.tensor_max(am[:, qt_i:qt_i + 1],
                                          am[:, qt_i:qt_i + 1],
                                          am[:, NQT:NQT + 1]),
                             deps=[t1], reads=[(am, "tmp"), (am, qt_i)],
                             writes=[(am, qt_i)])

        def neg_m_to_qt(h):
            par = h % 2
            qt = qt_b[par]
            for j in range(NQB):
                for u in range(4):
                    qi = j * 4 + u
                    p.op("tensor",
                         lambda t, qi=qi, u=u:
                         t.transpose(psm[0:1, u * 128:(u + 1) * 128],
                                     am[:, qi:qi + 1], K.ident_f[:, :]),
                         reads=[(am, qi), (K.ident_f, "")], writes=[(psm, u)])
                p.op("vector",
                     lambda v, j=j:
                     v.tensor_scalar_mul(qt[64:65, j * 512:(j + 1) * 512],
                                         psm[0:1, :512], -1.0),
                     reads=[(psm, 0), (psm, 1), (psm, 2), (psm, 3)],
                     writes=[(qt, "bias")])

        def pass_b(h):
            par = h % 2
            kt, vt, qt = kt_b[par], vt_b[par], qt_b[par]
            r0, c0 = (h % 2) * 64, h // 2
            for j in range(NQB):
                pv = K.pb[2]
                rq = qt[0:65, j * 512:(j + 1) * 512]
                rc = cqs[:, j * 512:(j + 1) * 512] if influence else None
                pv_tok = None
                for kc in range(NKT):
                    ps = K.pb[3 + kc % 2]
                    lk = kt[0:65, kc * 128:(kc + 1) * 128]
                    mtok = p.op("tensor", lambda t, lk=lk, rq=rq, ps=ps,
                                e=not influence:
                                t.matmul(ps[:, :512], lk, rq, start=True,
                                         stop=e),
                                reads=[(kt, ""), (kt, "ones"), (qt, "lo"),
                                       (qt, "bias")],
                                writes=[(ps, "")])
                    if influence:
                        lc = ck[:, kc * 128:(kc + 1) * 128]
                        mtok = p.op("tensor", lambda t, lc=lc, rc=rc, ps=ps:
                                    t.matmul(ps[:, :512], lc, rc, start=False,
                                             stop=True),
                                    reads=[(ck, ""), (cqs, "")])
                    p.regions[_rk((ps, ""))] = [mtok, []]
                    pb_ = pbuf[kc % 4]
                    if masked:
                        mb = mbuf[kc % 2]
                        p.dma("sync", lambda e, j=j, kc=kc, mb=mb:
                              e.dma_start(out=mb[:], in_=cm_d[j, kc]),
                              sem=f"dm{kc % 2}", writes=[(mb, "")])
                    etok = p.op("scalar", lambda s, ps=ps, pb_=pb_:
                                s.activation(pb_[:, :512], ps[:, :512], AF.Exp),
                                reads=[(ps, "")], writes=[(pb_, "")])
                    if masked:
                        p.op("vector", lambda v, pb_=pb_, mb=mb:
                             v.tensor_mul(pb_[:, :512], pb_[:, :512],
                                          mb[:, :512]),
                             reads=[(pb_, ""), (mb, "")], writes=[(pb_, "")])
                    lv = vt[:, kc * 65:(kc + 1) * 65]
                    pv_tok = p.op("tensor",
                                  lambda t, lv=lv, pb_=pb_, pv=pv,
                                  s=(kc == 0), e=(kc == NKT - 1):
                                  t.matmul(pv[0:65, :512], lv, pb_[:, :512],
                                           start=s, stop=e),
                                  reads=[(vt, ""), (vt, "ones"), (pb_, "")],
                                  writes=[(pv, "")] if kc == 0 else [])
                p.regions[_rk((pv, ""))] = [pv_tok, []]
                t_r = p.op("vector", lambda v, pv=pv:
                           v.reciprocal(rsum[0:1, :512], pv[64:65, :512]),
                           reads=[(pv, "")], writes=[(rsum, "")])
                t_bc = K.matmul_acc(
                    pb5[0:64, :512],
                    [(K.ones_row_f[0:1, 0:64], rsum[0:1, :512],
                      [(rsum, ""), (K.ones_row_f, "")])], (pb5, ""))
                t_cp = p.op("vector", lambda v:
                            v.tensor_copy(rb[0:64, :512], pb5[0:64, :512]),
                            deps=[t_bc], reads=[(pb5, "")],
                            writes=[(rb, "")])
                dst = hb[r0:r0 + 64, c0 * T + j * 512: c0 * T + (j + 1) * 512]
                p.op("vector", lambda v, dst=dst, pv=pv:
                     v.tensor_mul(dst, pv[0:64, :512], rb[0:64, :512]),
                     deps=[t_cp], reads=[(pv, ""), (rb, "")],
                     writes=[(hb, "")])

        load_head(0)
        for h in range(NH):
            if h + 1 < NH:
                load_head(h + 1)
            if debug_stop == "loads":
                continue
            pass_a(h)
            if debug_stop == "passa":
                continue
            neg_m_to_qt(h)
            if debug_stop == "negm":
                continue
            pass_b(h)

        if debug_stop is not None and debug_stop != "wo":
            return
        def evac_addx(m, t0, n, psum):
            xb = xbv[m % 2]
            if t0 == 0:
                p.dma("sync", lambda e, m=m, xb=xb: e.dma_start(
                    out=xb[:, :T], in_=x_d[:, m * T:(m + 1) * T]),
                    sem=f"dxin{m % 2}", writes=[(xb, "")])
            dst = xb[:, t0:t0 + n]
            p.op("vector", lambda v, dst=dst, psum=psum, n=n:
                 v.tensor_add(dst, dst, psum[:, :n]),
                 reads=[(psum, ""), (xb, "")], writes=[(xb, "")])
            if t0 + n >= T:
                p.dma("sync", lambda e, m=m, xb=xb: e.dma_start(
                    out=xa_o[:, m * T:(m + 1) * T], in_=xb[:, :T]),
                    sem=f"dox{m % 2}", reads=[(xb, "")])
        K.emit_project_fm(None, hb, wo_d, 8, 8, T, evac_addx, wbufs, "w",
                          pbase=0, npb=2)
    p.barrier_all()


def build_moe0(T):
    K = KB(T=T)
    io = {}
    io["h"] = K.dram_in("h", [128, 8, T], BF16)
    io["xa"] = K.dram_in("xa", [128, 8 * T], F32)
    io["w1n"] = K.dram_in("w1n", [9, 16, 128, 1024], F32)
    io["w2n"] = K.dram_in("w2n", [9, 8, 128, 2048], F32)
    io["wr"] = K.dram_in("wr", [1, 8, 128, 128], F32)
    io["x1"] = K.dram_out("x1", [128, 8 * T], F32)
    with K.es:
        K.setup_common(npb=5, stats=False)
        emit_moe0(K, io)
        K.p.finalize()
        K.p.emit()
    return K


def emit_moe0(K, io):
    T = K.T
    p = K.p
    h_d, xa_d, w1n_d, w2n_d, wr_d, x1_o = (
        io["h"], io["xa"], io["w1n"], io["w2n"], io["wr"], io["x1"])
    NT = T // 128
    NC = T // 512
    with K.scope("mo"):
        ptr = K.ps("ptr", [128, 512], BF16)
        psm = K.ps("psm", [1, 512], F32)
        pg = K.ps("pg", [128, 8], F32)
        h_sb = K.sb("h_sb", [128, 8 * T], BF16)
        F_sb = K.sb("F_sb", [128, 8 * T], F32)
        w1T = K.sb("w1T", [128, 16 * 8 * 128], BF16)
        w2T = K.sb("w2T", [128, 8 * 16 * 128], BF16)
        h1 = K.sb("h1", [128, 16 * 512], BF16)
        wn = [K.sb("wn0", [128, 2048], F32)] * 2
        wq = [K.sb("wq0", [128, 2048], BF16)] * 2
        qs = [K.sb(f"qs{i}", [128, 4], F32) for i in range(2)]
        lg = K.sb("lg", [8, T], F32)
        ltm = K.sb("ltm", [128, 8], F32)
        etm = K.sb("etm", [128, 8], F32)
        gstat = K.sb("gstat", [128, 4], F32)
        gtm = K.sb("gtm", [128, 8 * NT], F32)
        grow = K.sb("grow", [1, 512], F32)
        gbc = K.sb("gbc", [128, 512], F32)
        ytmp = K.sb("ytmp", [128, 512], BF16)
        wrb = [wq[0]]

        tload = p.dma("sync", lambda e: e.dma_start(
            out=h_sb[:], in_=h_d.rearrange("p m t -> p (m t)")), sem="dh",
            writes=[(h_sb, "")])

        def evac_lg(m, t0, n, psum):
            p.op("vector", lambda v, t0=t0, n=n:
                 v.tensor_copy(lg[0:8, t0:t0 + n], psum[0:8, :n]),
                 reads=[(psum, "")], writes=[(lg, "")])
        K.emit_project_fm(None, h_sb, wr_d, 1, 8, T, evac_lg, wrb, "wr",
                          pbase=0, npb=2)
        for t in range(NT):
            ttok = p.op("tensor", lambda tt, t=t:
                        tt.transpose(pg[:, 0:8], lg[0:8, t * 128:(t + 1) * 128],
                                     K.ident_f[0:8, 0:8]),
                        reads=[(lg, ""), (K.ident_f, "")], writes=[(pg, "")])
            c1 = p.op("vector", lambda v: v.tensor_copy(ltm[:], pg[:, 0:8]),
                      deps=[ttok], reads=[(pg, "")], writes=[(ltm, "")])
            c2 = p.op("vector", lambda v:
                      v.reduce_max(gstat[:, 0:1], ltm[:], axis=AX.X,
                                   negate=True),
                      reads=[(ltm, "")], writes=[(gstat, "nml")])
            c3 = p.op("scalar", lambda s:
                      s.activation(etm[:], ltm[:], AF.Exp,
                                   bias=gstat[:, 0:1],
                                   accum_out=gstat[:, 1:2]),
                      reads=[(ltm, ""), (gstat, "nml")],
                      writes=[(etm, ""), (gstat, "se")])
            c4 = p.op("vector", lambda v:
                      v.reciprocal(gstat[:, 2:3], gstat[:, 1:2]),
                      reads=[(gstat, "se")], writes=[(gstat, "tp")])
            c5 = p.op("vector", lambda v:
                      v.tensor_scalar_mul(gstat[:, 3:4], gstat[:, 0:1], -1.0),
                      reads=[(gstat, "nml")], writes=[(gstat, "ml")])
            c6 = p.op("vector", lambda v, t=t:
                      v.tensor_single_scalar(gtm[:, t * 8:(t + 1) * 8], ltm[:],
                                             gstat[:, 3:4], ALU.is_equal),
                      reads=[(ltm, ""), (gstat, "ml")], writes=[(gtm, t)])
            p.op("vector", lambda v, t=t:
                 v.tensor_scalar_mul(gtm[:, t * 8:(t + 1) * 8],
                                     gtm[:, t * 8:(t + 1) * 8],
                                     gstat[:, 2:3]),
                 reads=[(gtm, t), (gstat, "tp")], writes=[(gtm, t)])

        def quant_chunk(src_ap, par, din, tag):
            w = wn[par]
            o = wq[par]
            s = qs[par]
            t0 = p.dma("gpsimd", lambda e: e.dma_start(out=w[:, :din],
                                                       in_=src_ap),
                       sem=f"dwn{par}", writes=[(w, "")])
            p.op("vector", lambda v:
                 v.tensor_reduce(s[:, 0:1], w[:, :din], axis=AX.X, op=ALU.add,
                                 apply_absolute_value=True),
                 reads=[(w, "")], writes=[(s, "s")])
            p.op("vector", lambda v:
                 v.tensor_scalar(s[:, 0:1], s[:, 0:1], 1.0 / din, 1e-5,
                                 ALU.mult, ALU.max),
                 reads=[(s, "s")], writes=[(s, "s")])
            p.op("vector", lambda v:
                 v.tensor_scalar_mul(s[:, 1:2], s[:, 0:1], 0.5),
                 reads=[(s, "s")], writes=[(s, "h")])
            p.op("vector", lambda v:
                 v.tensor_scalar_mul(s[:, 2:3], s[:, 0:1], -0.5),
                 reads=[(s, "s")], writes=[(s, "nh")])
            p.op("vector", lambda v:
                 v.tensor_single_scalar(o[:, :din], w[:, :din], s[:, 1:2],
                                        ALU.is_ge),
                 reads=[(w, ""), (s, "h")], writes=[(o, "")])
            p.op("vector", lambda v:
                 v.tensor_single_scalar(sgn[:, :din], w[:, :din], s[:, 2:3],
                                        ALU.is_le),
                 reads=[(w, ""), (s, "nh")], writes=[(sgn, "")])
            p.op("vector", lambda v:
                 v.tensor_sub(o[:, :din], o[:, :din], sgn[:, :din]),
                 reads=[(o, ""), (sgn, "")], writes=[(o, "")])
            p.op("vector", lambda v:
                 v.tensor_scalar_mul(o[:, :din], o[:, :din], s[:, 0:1]),
                 reads=[(o, ""), (s, "s")], writes=[(o, "")])
            return o

        sgn = K.sb("sgn", [128, 2048], BF16)

        def quant_matrix(nat_d, mat, nm, din, dstT):
            nk = din // 128
            for m in range(nm):
                par = m % 2
                o = quant_chunk(nat_d[mat, m], par, din, f"m{m}")
                for g4 in range(nk // 4):
                    for u in range(4):
                        k = g4 * 4 + u
                        p.op("tensor", lambda t, o=o, k=k, u=u:
                             t.transpose(ptr[:, u * 128:(u + 1) * 128],
                                         o[:, k * 128:(k + 1) * 128],
                                         K.ident_b[:, :]),
                             reads=[(o, ""), (K.ident_b, "")],
                             writes=[(ptr, "")])
                    dst4 = dstT[:, (m * nk + g4 * 4) * 128:
                                (m * nk + g4 * 4 + 4) * 128]
                    p.op("vector", lambda v, dst4=dst4:
                         v.tensor_copy(dst4, ptr[:, :512]),
                         reads=[(ptr, "")], writes=[(dstT, m)])

        def expert_pass(mat, gated):
            quant_matrix(w1n_d, mat, 16, 1024, w1T)
            quant_matrix(w2n_d, mat, 8, 2048, w2T)
            for tc in range(NC):
                t0 = tc * 512
                for m in range(16):
                    ps = K.pb[m % 2]
                    pairs = [(w1T[:, (m * 8 + k) * 128:(m * 8 + k + 1) * 128],
                              h_sb[:, k * T + t0:k * T + t0 + 512],
                              [(w1T, m), (h_sb, "")]) for k in range(8)]
                    K.matmul_acc(ps[:, :512], pairs, (ps, ""))
                    p.op("scalar", lambda s, ps=ps, m=m:
                         s.activation(h1[:, m * 512:(m + 1) * 512],
                                      ps[:, :512], AF.Silu),
                         reads=[(ps, "")], writes=[(h1, m)])
                if gated is not None:
                    e = gated
                    for u in range(4):
                        t = tc * 4 + u
                        p.op("tensor", lambda tt, t=t, u=u, e=e:
                             tt.transpose(psm[0:1, u * 128:(u + 1) * 128],
                                          gtm[:, t * 8 + e:t * 8 + e + 1],
                                          K.ident_f[:, :]),
                             reads=[(gtm, t), (K.ident_f, "")],
                             writes=[(psm, u)])
                    p.op("vector", lambda v:
                         v.tensor_copy(grow[0:1, :512], psm[0:1, :512]),
                         reads=[(psm, 0), (psm, 1), (psm, 2), (psm, 3)],
                         writes=[(grow, "")])
                    K.matmul_acc(K.pb[4][:, :512],
                                 [(K.ones_row_f[0:1, :], grow[0:1, :512],
                                   [(grow, ""), (K.ones_row_f, "")])],
                                 (K.pb[4], ""))
                    p.op("vector", lambda v:
                         v.tensor_copy(gbc[:, :512], K.pb[4][:, :512]),
                         reads=[(K.pb[4], "")], writes=[(gbc, "")])
                for m in range(8):
                    ps = K.pb[2 + m % 2]
                    pairs = [(w2T[:, (m * 16 + k) * 128:(m * 16 + k + 1) * 128],
                              h1[:, k * 512:(k + 1) * 512],
                              [(w2T, m), (h1, k)]) for k in range(16)]
                    K.matmul_acc(ps[:, :512], pairs, (ps, ""))
                    dst = F_sb[:, m * T + t0: m * T + t0 + 512]
                    if gated is None:
                        if mat == 0:
                            p.op("vector", lambda v, dst=dst, ps=ps:
                                 v.tensor_copy(dst, ps[:, :512]),
                                 reads=[(ps, "")], writes=[(F_sb, "")])
                        else:
                            p.op("vector", lambda v, dst=dst, ps=ps:
                                 v.tensor_add(dst, dst, ps[:, :512]),
                                 reads=[(ps, ""), (F_sb, "")],
                                 writes=[(F_sb, "")])
                    else:
                        t1 = p.op("vector", lambda v, ps=ps:
                                  v.tensor_mul(ytmp[:, :512], ps[:, :512],
                                               gbc[:, :512]),
                                  reads=[(ps, ""), (gbc, "")],
                                  writes=[(ytmp, "")])
                        p.op("vector", lambda v, dst=dst:
                             v.tensor_add(dst, dst, ytmp[:, :512]),
                             deps=[t1], reads=[(ytmp, ""), (F_sb, "")],
                             writes=[(F_sb, "")])

        expert_pass(0, None)
        for e in range(8):
            expert_pass(1 + e, e)

        xbuf = wn[0]
        for m in range(8):
            tl = p.dma("sync", lambda e, m=m: e.dma_start(
                out=xbuf[:, :T], in_=xa_d[:, m * T:(m + 1) * T]),
                sem="dxa", writes=[(xbuf, "")])
            ta = p.op("vector", lambda v, m=m:
                      v.tensor_add(F_sb[:, m * T:(m + 1) * T],
                                   F_sb[:, m * T:(m + 1) * T], xbuf[:, :T]),
                      deps=[tl], reads=[(xbuf, ""), (F_sb, "")],
                      writes=[(F_sb, "")])
            p.dma("sync", lambda e, m=m: e.dma_start(
                out=x1_o[:, m * T:(m + 1) * T],
                in_=F_sb[:, m * T:(m + 1) * T]),
                sem="dox", deps=[ta], reads=[(F_sb, "")])
    p.barrier_all()


def build_pre1(T):
    K = KB(T=T)
    io = {}
    io["x"] = K.dram_in("x", [128, 8 * T], F32)
    io["wq"] = K.dram_in("wq", [128, 8, 1024], F32)
    io["wk"] = K.dram_in("wk", [128, 8, 1024], F32)
    io["wv"] = K.dram_in("wv", [8, 8, 128, 128], F32)
    io["cosE"] = K.dram_in("cosE", [T, 1024], F32)
    io["sinE"] = K.dram_in("sinE", [T, 1024], F32)
    io["xn"] = K.dram_out("xn", [128, 8, T], BF16)
    io["q"] = K.dram_out("q", [128, 8, T], BF16)
    io["k"] = K.dram_out("k", [128, 8, T], BF16)
    io["v"] = K.dram_out("v", [128, 8, T], BF16)
    with K.es:
        K.setup_common(npb=5, stats=False)
        emit_pre1(K, io)
        K.p.finalize()
        K.p.emit()
    return K


def emit_pre1(K, io):
    T = K.T
    p = K.p
    x_d, wq_d, wk_d, wv_d, cos_d, sin_d = (
        io["x"], io["wq"], io["wk"], io["wv"], io["cosE"], io["sinE"])
    xn_o, q_o, k_o, v_o = io["xn"], io["q"], io["k"], io["v"]
    NT = T // 128
    xnf_o = io.get("xnf")
    with K.scope("p1"):
        K.pstat1 = K.ps("pstat1", [1, 512], F32)
        K.pstat2 = K.ps("pstat2", [1, 512], F32)
        ptr = K.ps("ptr", [128, 512], BF16)
        xt = K.sb("xt", [128, 8 * T], F32)
        xn = K.sb("xns", [128, 8 * T], BF16)
        sq = K.sb("sq", [128, 1024], F32)
        st = [K.sb(f"st{i}", [1, 512], F32) for i in range(3)]
        wbufs = [K.sb(f"wbuf{i}", [128, 8 * 128], BF16) for i in range(2)]
        wtm = [K.sb(f"wtm{i}", [128, 8 * 512], BF16) for i in range(2)]
        obufs = [K.sb(f"obuf{i}", [128, T], BF16) for i in range(2)]
        qtm = K.sb("qtm", [128, 1024], F32)
        qrm = [K.sb(f"qrm{i}", [128, 1024], BF16) for i in range(2)]
        ce = K.sb("ce", [128, 1024], F32)
        se = K.sb("se_t", [128, 1024], F32)
        rsc = [K.sb(f"rsc{i}", [128, 512], F32) for i in range(2)]
        qf = K.sb("qf", [128, 8 * T], BF16)

        p.dma("sync", lambda e: e.dma_start(out=xt[:], in_=x_d),
              sem="dx", writes=[(xt, "")])
        K.emit_ln_fm(xt, xt, 8, T, F32, sq_sb=sq, stat_sb=st,
                     ones_col_b=None, pbase=0)
        p.op("scalar", lambda s: s.activation(xn[:], xt[:], AF.Copy),
             reads=[(xt, "")], writes=[(xn, "")])
        p.dma("sync", lambda e: e.dma_start(out=xn_o.rearrange(
            "p m t -> p (m t)"), in_=xn[:]), sem="doxn", reads=[(xn, "")])
        if xnf_o is not None:
            p.dma("sync", lambda e: e.dma_start(out=xnf_o, in_=xt[:]),
                  sem="doxnf", reads=[(xt, "")])

        evac = K.mk_evac_store(obufs, v_o, T, func=AF.Copy, scale=1.0,
                               osem="dov")
        K.emit_project_fm(None, xn, wv_d, 8, 8, T, evac, wbufs, "wv",
                          pbase=2, npb=2)

        def qk_roped(w_d, out_d, scale, wsem):
            for t in range(NT):
                tct = p.dma("gpsimd", lambda e, t=t: e.dma_start(
                    out=ce[:], in_=cos_d[t * 128:(t + 1) * 128, :]),
                    sem="dce", writes=[(ce, "")])
                tst = p.dma("gpsimd", lambda e, t=t: e.dma_start(
                    out=se[:], in_=sin_d[t * 128:(t + 1) * 128, :]),
                    sem="dse", writes=[(se, "")])
                for half in range(2):
                    ps = K.pb[2 + half]
                    pairs = [(xn[:, kk * T + t * 128: kk * T + (t + 1) * 128],
                              wtm[half][:, kk * 512:(kk + 1) * 512],
                              [(xn, ""), (wtm[half], "")])
                             for kk in range(8)]
                    K.matmul_acc(ps[:, :512], pairs, (ps, ""))
                    p.op("scalar", lambda s, ps=ps, half=half:
                         s.activation(qtm[:, half * 512:(half + 1) * 512],
                                      ps[:, :512], AF.Copy, scale=scale),
                         reads=[(ps, "")], writes=[(qtm, half)])
                qr = qrm[t % 2]
                xe = qtm.ap().rearrange("p (j two) -> p j two", two=2)[:, :, 0:1]
                xo = qtm.ap().rearrange("p (j two) -> p j two", two=2)[:, :, 1:2]
                cee = ce.ap().rearrange("p (j two) -> p j two", two=2)[:, :, 0:1]
                soo = se.ap().rearrange("p (j two) -> p j two", two=2)[:, :, 0:1]
                oe = qr.ap().rearrange("p (j two) -> p j two", two=2)[:, :, 0:1]
                oo = qr.ap().rearrange("p (j two) -> p j two", two=2)[:, :, 1:2]
                r0 = rsc[0]
                r1 = rsc[1]
                r0a = r0[:, :512].rearrange("p (j o) -> p j o", o=1)
                r1a = r1[:, :512].rearrange("p (j o) -> p j o", o=1)
                t1 = p.op("vector", lambda v, r0a=r0a, xe=xe, cee=cee:
                          v.tensor_mul(r0a, xe, cee),
                          deps=[tct], reads=[(qtm, 0), (qtm, 1), (ce, "")],
                          writes=[(rsc[0], "")])
                t2 = p.op("vector", lambda v, r1a=r1a, xo=xo, soo=soo:
                          v.tensor_mul(r1a, xo, soo),
                          deps=[tst], reads=[(qtm, 0), (qtm, 1), (se, "")],
                          writes=[(rsc[1], "")])
                p.op("vector", lambda v, oe=oe, r0a=r0a, r1a=r1a:
                     v.tensor_sub(oe, r0a, r1a),
                     deps=[t1, t2], reads=[(rsc[0], ""), (rsc[1], "")],
                     writes=[(qr, "e")])
                t3 = p.op("vector", lambda v, r0a=r0a, xe=xe, soo=soo:
                          v.tensor_mul(r0a, xe, soo),
                          reads=[(qtm, 0), (qtm, 1), (se, ""), (rsc[0], "")],
                          writes=[(rsc[0], "")])
                t4 = p.op("vector", lambda v, r1a=r1a, xo=xo, cee=cee:
                          v.tensor_mul(r1a, xo, cee),
                          reads=[(qtm, 0), (qtm, 1), (ce, ""), (rsc[1], "")],
                          writes=[(rsc[1], "")])
                p.op("vector", lambda v, oo=oo, r0a=r0a, r1a=r1a:
                     v.tensor_add(oo, r0a, r1a),
                     deps=[t3, t4], reads=[(rsc[0], ""), (rsc[1], "")],
                     writes=[(qr, "o")])
                for g4 in range(2):
                    for u in range(4):
                        c = g4 * 4 + u
                        p.op("tensor", lambda tt, c=c, u=u, qr=qr:
                             tt.transpose(ptr[:, u * 128:(u + 1) * 128],
                                          qr[:, c * 128:(c + 1) * 128],
                                          K.ident_b[:, :]),
                             reads=[(qr, "e"), (qr, "o"), (K.ident_b, "")],
                             writes=[(ptr, "")])
                    src4 = ptr.ap().rearrange("p (q c) -> p q c", c=128)
                    dst4 = qf.ap().rearrange("p (c t) -> p c t", t=T)[
                        :, g4 * 4:(g4 + 1) * 4, t * 128:(t + 1) * 128]
                    p.op("vector", lambda v, src4=src4, dst4=dst4:
                         v.tensor_copy(dst4, src4),
                         reads=[(ptr, "")], writes=[(qf, "")])
            p.dma("sync", lambda e: e.dma_start(out=out_d.rearrange(
                "p m t -> p (m t)"), in_=qf[:]), sem="doq", reads=[(qf, "")])

        def load_w_half2(w_d, half, wsem):
            src = w_d[:, :, half * 512:(half + 1) * 512]
            dst = wtm[half].ap().rearrange("p (k c) -> p k c", c=512)
            return p.dma("gpsimd", lambda e: e.dma_start(out=dst, in_=src),
                         sem=wsem, writes=[(wtm[half], "")])

        for (w_d, out_d, scale, nm) in ((wq_d, q_o, 0.125, "wq2"),
                                        (wk_d, k_o, 1.0, "wk2")):
            load_w_half2(w_d, 0, nm + "a")
            load_w_half2(w_d, 1, nm + "b")
            qk_roped(w_d, out_d, scale, nm)
    p.barrier_all()


def build_loc_attn(T, debug_dump_step=None):
    HLO = 257
    TE = T + HLO
    NS = T // 512
    K = KB(T=T)
    io = {}
    io["y"] = K.dram_in("y", [128, 8 * TE], F32)
    io["wqkv"] = K.dram_in("wqkv", [128, 8, 3072], F32)
    io["wo"] = K.dram_in("wo", [8, 8, 128, 128], F32)
    io["qa"] = K.dram_in("qa", [TE, 1024], F32)
    io["qb"] = K.dram_in("qb", [TE, 1024], F32)
    io["ka"] = K.dram_in("ka", [TE, 1024], F32)
    io["kb"] = K.dram_in("kb", [TE, 1024], F32)
    io["qscl"] = K.dram_in("qscl", [128, 1024], F32)
    io["kscl"] = K.dram_in("kscl", [128, 1024], F32)
    io["maskA"] = K.dram_in("maskA", [NS * 4, 128, 384], BF16)
    io["maskB"] = K.dram_in("maskB", [NS * 6, 128, 512], BF16)
    io["yo"] = K.dram_out("yo", [128, 8 * T], F32)
    with K.es:
        K.setup_common(npb=5, stats=False)
        emit_loc_attn(K, io, scope="la", debug_dump_step=debug_dump_step)
        K.p.finalize()
        K.p.emit()
    return K


def emit_loc_attn(K, io, scope, debug_dump_step=None):
    T = K.T
    p = K.p
    HLO = 257
    TE = T + HLO
    NS = T // 512
    y_d, wqkv_d, wo_d = io["y"], io["wqkv"], io["wo"]
    qa_d, qb_d, ka_d, kb_d = io["qa"], io["qb"], io["ka"], io["kb"]
    ma_d, mb_d, yo_o = io["maskA"], io["maskB"], io["yo"]
    with K.scope(scope):
        qscl = K.sb("qscl_t", [128, 1024], BF16)
        kscl = K.sb("kscl_t", [128, 1024], BF16)
        p.dma("gpsimd", lambda e: e.dma_start(out=qscl[:], in_=io["qscl"]),
              sem="dqscl", writes=[(qscl, "")])
        p.dma("gpsimd", lambda e: e.dma_start(out=kscl[:], in_=io["kscl"]),
              sem="dkscl", writes=[(kscl, "")])
        pb5 = K.ps("pb5", [128, 512], F32)
        ptr = K.ps("ptr", [128, 512], BF16)
        psm = K.ps("psm", [1, 512], F32)
        y_sb = K.sb("y_sb", [128, 8 * TE], F32)
        wtm = [K.sb(f"wtm{i}", [128, 8 * 512], BF16) for i in range(2)]
        xta = K.sb("xta", [128, 6 * 1024], BF16)
        qkt = K.sb("qkt", [128, 6 * 2048], BF16)
        sqv = K.sb("sqv", [128, 1024], F32)
        nrm = K.sb("nrm", [128, 32], F32)
        tabs = [K.sb(f"tab{i}", [128, 1024], F32) for i in range(2)]
        qfl = K.sb("qfl", [128, 8 * 768], BF16)
        kfl = K.sb("kfl", [128, 8 * 768], BF16)
        vta = K.sb("vta", [128, 6 * 16 * 65], BF16)
        hbl = K.sb("hbl", [128, 8 * 512], BF16)
        qkr = hbl
        ktl = K.sb("ktl", [65, 768], BF16)
        qtl = K.sb("qtl", [65, 512], BF16)
        maska = K.sb("maska", [128, 4 * 384], BF16)
        maskb = [K.sb(f"maskb{i}", [128, 512], BF16) for i in range(2)]
        pbuf = [K.sb(f"pbuf{i}", [128, 512], BF16) for i in range(2)]
        am = K.sb("am_l", [128, 4], F32)
        thirty = K.sb("thirty", [128, 1], F32)
        rsum = K.sb("rsum_l", [1, 512], F32)
        rb = K.sb("rb_l", [64, 512], F32)
        oby = [K.sb(f"oby{i}", [128, 512], F32) for i in range(2)]
        wbufs = [K.sb(f"wbuf{i}", [128, 8 * 128], BF16) for i in range(2)]

        p.dma("sync", lambda e: e.dma_start(out=y_sb[:], in_=y_d),
              sem="dy", writes=[(y_sb, "")])
        p.op("gpsimd", lambda g: g.memset(ktl[64:65, :], 1.0),
             writes=[(ktl, "ones")])
        p.op("gpsimd", lambda g: g.memset(thirty[:], 30.0),
             writes=[(thirty, "")])
        vap = vta.ap().rearrange("p (t c) -> p t c", c=65)
        p.op("gpsimd", lambda g: g.memset(vap[:, :, 64:65], 1.0),
             writes=[(vta, "ones")])

        for s in range(NS):
            ps0 = 1 + s * 512
            for ti in range(6):
                c0 = ps0 + ti * 128
                for kk in range(8):
                    off = c0 if kk < 4 else c0 - 1
                    p.op("scalar", lambda sc, kk=kk, off=off, ti=ti:
                         sc.activation(xta[:, ti * 1024 + kk * 128:
                                           ti * 1024 + (kk + 1) * 128],
                                       y_sb[:, kk * TE + off:
                                            kk * TE + off + 128],
                                       AF.Copy),
                         reads=[(y_sb, "")], writes=[(xta, (ti, kk))])
            for g in range(6):
                p.dma("gpsimd", lambda e, g=g: e.dma_start(
                    out=wtm[g % 2].ap().rearrange(
                        "p (k c) -> p k c", c=512),
                    in_=wqkv_d[:, :, g * 512:(g + 1) * 512]),
                    sem=f"dwg{g % 2}", writes=[(wtm[g % 2], "")])
                for ti in range(6):
                    ps_ = K.pb[ti % 2 + (0 if g % 2 == 0 else 2)]
                    pairs = [(xta[:, ti * 1024 + kk * 128:
                                  ti * 1024 + (kk + 1) * 128],
                              wtm[g % 2][:, kk * 512:(kk + 1) * 512],
                              [(xta, (ti, kk)), (wtm[g % 2], "")])
                             for kk in range(8)]
                    K.matmul_acc(ps_[:, :512], pairs, (ps_, ""))
                    if g < 4:
                        p.op("scalar", lambda sc, ps_=ps_, g=g, ti=ti:
                             sc.activation(qkt[:, ti * 2048 + g * 512:
                                               ti * 2048 + (g + 1) * 512],
                                           ps_[:, :512], AF.Copy),
                             reads=[(ps_, "")], writes=[(qkt, (ti, g))])
                    else:
                        dst = vta.ap().rearrange("p (t h c) -> p t h c",
                                                 h=16, c=65)[
                            :, ti, (g - 4) * 8:(g - 4) * 8 + 8, 0:64]
                        src = ps_[:, :512].rearrange("p (h c) -> p h c", c=64)
                        p.op("scalar", lambda sc, dst=dst, src=src, ps_=ps_:
                             sc.activation(dst, src, AF.Copy),
                             reads=[(ps_, "")], writes=[(vta, ti)])
            for ti in range(6):
                c0 = ps0 + ti * 128
                qk_reads = [(qkt, (ti, g)) for g in range(4)]
                for half in range(2):
                    p.op("scalar", lambda sc, ti=ti, half=half:
                         sc.activation(sqv[:],
                                       qkt[:, ti * 2048 + half * 1024:
                                           ti * 2048 + (half + 1) * 1024],
                                       AF.Square),
                         reads=qk_reads, writes=[(sqv, "")])
                    p.op("vector", lambda v, half=half: v.tensor_reduce(
                        nrm[:, half * 16:(half + 1) * 16].rearrange(
                            "p (h o) -> p h o", o=1),
                        sqv[:].rearrange("p (h d) -> p h d", d=64),
                        axis=AX.X, op=ALU.add),
                        reads=[(sqv, "")], writes=[(nrm, half)])
                    p.op("scalar", lambda sc, half=half: sc.activation(
                        nrm[:, half * 16:(half + 1) * 16],
                        nrm[:, half * 16:(half + 1) * 16], AF.Sqrt),
                        reads=[(nrm, half)], writes=[(nrm, half)])
                    p.op("vector", lambda v, half=half: v.tensor_scalar_max(
                        nrm[:, half * 16:(half + 1) * 16],
                        nrm[:, half * 16:(half + 1) * 16], 1e-12),
                        reads=[(nrm, half)], writes=[(nrm, half)])
                    p.op("vector", lambda v, half=half: v.reciprocal(
                        nrm[:, half * 16:(half + 1) * 16],
                        nrm[:, half * 16:(half + 1) * 16]),
                        reads=[(nrm, half)], writes=[(nrm, half)])
                for qk in range(2):
                    scl_t = qscl if qk == 0 else kscl
                    p.op("vector", lambda v, ti=ti, qk=qk, scl_t=scl_t:
                         v.tensor_mul(
                             qkt[:, ti * 2048 + qk * 1024:
                                 ti * 2048 + (qk + 1) * 1024],
                             qkt[:, ti * 2048 + qk * 1024:
                                 ti * 2048 + (qk + 1) * 1024],
                             scl_t[:]),
                         reads=[(qkt, (ti, 2 * qk)), (qkt, (ti, 2 * qk + 1)),
                                (scl_t, "")],
                         writes=[(qkt, (ti, 2 * qk)), (qkt, (ti, 2 * qk + 1))])
                for qk in range(2):
                    A, B = tabs[0], tabs[1]
                    tda = (qa_d, qb_d) if qk == 0 else (ka_d, kb_d)
                    ta_tok = p.dma("sync", lambda e, td=tda[0], c0=c0:
                                   e.dma_start(out=A[:],
                                               in_=td[c0:c0 + 128, :]),
                                   sem="dta", writes=[(tabs[0], "")])
                    tb_tok = p.dma("sync", lambda e, td=tda[1], c0=c0:
                                   e.dma_start(out=B[:],
                                               in_=td[c0:c0 + 128, :]),
                                   sem="dtb", writes=[(tabs[1], "")])
                    base = ti * 2048 + qk * 1024
                    src3 = qkt.ap()[:, base:base + 1024].rearrange(
                        "p (h v d) -> p h v d", v=2, d=32)
                    out3 = qkr.ap()[:, qk * 1024:(qk + 1) * 1024].rearrange(
                        "p (h v d) -> p h v d", v=2, d=32)
                    A3 = A.ap().rearrange("p (h v d) -> p h v d", v=2, d=32)
                    B3 = B.ap().rearrange("p (h v d) -> p h v d", v=2, d=32)
                    r03 = sqv.ap()[:, 0:512].rearrange(
                        "p (h d) -> p h d", d=32)
                    r13 = sqv.ap()[:, 512:1024].rearrange(
                        "p (h d) -> p h d", d=32)
                    q1, q2 = src3[:, :, 0], src3[:, :, 1]
                    t1 = p.op("vector", lambda v, q1=q1, A3=A3, r03=r03:
                              v.tensor_mul(r03, q1, A3[:, :, 0]),
                              deps=[ta_tok], reads=qk_reads + [(tabs[0], "")],
                              writes=[(sqv, "")])
                    t2 = p.op("vector", lambda v, q2=q2, B3=B3, r13=r13:
                              v.tensor_mul(r13, q2, B3[:, :, 0]),
                              deps=[tb_tok], reads=qk_reads + [(tabs[1], "")],
                              writes=[(sqv, "")])
                    o1 = p.op("vector", lambda v, out3=out3, r03=r03, r13=r13:
                              v.tensor_sub(out3[:, :, 0], r03, r13),
                              deps=[t1, t2],
                              reads=[(sqv, ""), (sqv, "")],
                              writes=[(hbl, "")])
                    t3 = p.op("vector", lambda v, q2=q2, A3=A3, r03=r03:
                              v.tensor_mul(r03, q2, A3[:, :, 1]),
                              reads=qk_reads + [(tabs[0], ""), (sqv, "")],
                              writes=[(sqv, "")])
                    t4 = p.op("vector", lambda v, q1=q1, B3=B3, r13=r13:
                              v.tensor_mul(r13, q1, B3[:, :, 1]),
                              reads=qk_reads + [(tabs[1], ""), (sqv, "")],
                              writes=[(sqv, "")])
                    o2 = p.op("vector", lambda v, out3=out3, r03=r03, r13=r13:
                              v.tensor_add(out3[:, :, 1], r03, r13),
                              deps=[t3, t4],
                              reads=[(sqv, ""), (sqv, "")],
                              writes=[(hbl, "")])
                    for h in range(16):
                        sl = slice(qk * 1024 + h * 64, qk * 1024 + (h + 1) * 64)
                        p.op("vector", lambda v, h=h, qk=qk, sl=sl:
                             v.tensor_scalar_mul(
                                 qkr[:, sl], qkr[:, sl],
                                 nrm[:, qk * 16 + h: qk * 16 + h + 1]),
                             deps=[o1, o2],
                             reads=[(hbl, ""), (hbl, ""),
                                    (nrm, qk)],
                             writes=[(hbl, ""), (hbl, "")])
                for g4 in range(4):
                    dst = qfl if g4 < 2 else kfl
                    for u in range(4):
                        c = g4 * 4 + u
                        p.op("tensor", lambda tt, c=c, u=u:
                             tt.transpose(ptr[:, u * 128:(u + 1) * 128],
                                          qkr[:, c * 128:(c + 1) * 128],
                                          K.ident_b[:, :]),
                             reads=[(hbl, ""), (hbl, ""), (hbl, ""), (hbl, ""),
                                    (K.ident_b, "")],
                             writes=[(ptr, "")])
                    src4 = ptr.ap().rearrange("p (q c) -> p q c", c=128)
                    dst4 = dst.ap().rearrange("p (c t) -> p c t", t=768)[
                        :, (g4 % 2) * 4:(g4 % 2) * 4 + 4,
                        ti * 128:(ti + 1) * 128]
                    p.op("vector", lambda v, src4=src4, dst4=dst4:
                         v.tensor_copy(dst4, src4),
                         reads=[(ptr, "")], writes=[(dst, "")])

            p.dma("sync", lambda e, s=s: e.dma_start(
                out=maska[:].rearrange("p (q c) -> p q c", c=384),
                in_=ma_d[s * 4:(s + 1) * 4].rearrange("q p c -> p q c")),
                sem="dma_a", writes=[(maska, "")])
            if debug_dump_step == s:
                qkt_o = K.dram_out("qkt_dbg", [128, 6 * 2048], BF16)
                p.dma("sync", lambda e: e.dma_start(out=qkt_o, in_=qkt[:]),
                      sem="ddbg0", reads=[(qkt, (ti_, g_)) for ti_ in range(6)
                                         for g_ in range(4)])
                qfl_o = K.dram_out("qfl_dbg", [128, 8 * 768], BF16)
                kfl_o = K.dram_out("kfl_dbg", [128, 8 * 768], BF16)
                vta_o = K.dram_out("vta_dbg", [128, 6 * 16 * 65], BF16)
                p.dma("sync", lambda e: e.dma_start(out=qfl_o, in_=qfl[:]),
                      sem="ddbg1", reads=[(qfl, "")])
                p.dma("sync", lambda e: e.dma_start(out=kfl_o, in_=kfl[:]),
                      sem="ddbg2", reads=[(kfl, "")])
                p.dma("sync", lambda e: e.dma_start(out=vta_o, in_=vta[:]),
                      sem="ddbg3", reads=[(vta, ""), (vta, "ones")] +
                      [(vta, z) for z in range(6)])

            for h in range(16):
                r0 = (h % 2) * 64
                c0h = h // 2
                p.dma("sync", lambda e, r0=r0, c0h=c0h: e.dma_start(
                    out=ktl[0:64, :],
                    in_=kfl[r0:r0 + 64, c0h * 768:(c0h + 1) * 768]),
                    sem="dktl", reads=[(kfl, "")], writes=[(ktl, "")])
                for qt in range(4):
                    ps_ = K.pb[qt % 2]
                    lq = qfl[r0:r0 + 64,
                             c0h * 768 + 256 + qt * 128:
                             c0h * 768 + 256 + (qt + 1) * 128]
                    rk = kfl[r0:r0 + 64,
                             c0h * 768 + qt * 128: c0h * 768 + qt * 128 + 384]
                    mtok = p.op("tensor", lambda t, lq=lq, rk=rk, ps_=ps_:
                                t.matmul(ps_[:, :384], lq, rk, start=True,
                                         stop=True),
                                reads=[(qfl, ""), (kfl, "")],
                                writes=[(ps_, "")])
                    p.regions[_rk((ps_, ""))] = [mtok, []]
                    ta = p.op("vector", lambda v, ps_=ps_, qt=qt:
                              v.tensor_add(sqv[:, :384], ps_[:, :384],
                                           maska[:, qt * 384:(qt + 1) * 384]),
                              reads=[(ps_, ""), (maska, ""), (sqv, "")],
                              writes=[(sqv, "")])
                    p.op("vector", lambda v, qt=qt:
                         v.reduce_max(am[:, qt:qt + 1], sqv[:, :384],
                                      axis=AX.X),
                         deps=[ta], reads=[(sqv, "")], writes=[(am, qt)])
                p.dma("sync", lambda e, r0=r0, c0h=c0h: e.dma_start(
                    out=qtl[0:64, :],
                    in_=qfl[r0:r0 + 64, c0h * 768 + 256: c0h * 768 + 768]),
                    sem="dqtl", reads=[(qfl, "")], writes=[(qtl, "lo")])
                for u in range(4):
                    p.op("tensor", lambda t, u=u:
                         t.transpose(psm[0:1, u * 128:(u + 1) * 128],
                                     am[:, u:u + 1], K.ident_f[:, :]),
                         reads=[(am, u), (K.ident_f, "")], writes=[(psm, u)])
                p.op("vector", lambda v:
                     v.tensor_scalar_mul(qtl[64:65, 0:512], psm[0:1, :512],
                                         -1.0),
                     reads=[(psm, 0), (psm, 1), (psm, 2), (psm, 3)],
                     writes=[(qtl, "bias")])
                pv = K.pb[2]
                pv_tok = None
                for kc in range(6):
                    ps_ = K.pb[3 + kc % 2]
                    lk = ktl[0:65, kc * 128:(kc + 1) * 128]
                    mtok = p.op("tensor", lambda t, lk=lk, ps_=ps_:
                                t.matmul(ps_[:, :512], lk, qtl[0:65, :512],
                                         start=True, stop=True),
                                reads=[(ktl, ""), (ktl, "ones"), (qtl, "lo"),
                                       (qtl, "bias")],
                                writes=[(ps_, "")])
                    p.regions[_rk((ps_, ""))] = [mtok, []]
                    mb = maskb[kc % 2]
                    p.dma("sync", lambda e, s=s, kc=kc, mb=mb: e.dma_start(
                        out=mb[:], in_=mb_d[s * 6 + kc]),
                        sem=f"dmb{kc % 2}", writes=[(mb, "")])
                    pb_ = pbuf[kc % 2]
                    p.op("scalar", lambda sc, ps_=ps_:
                         sc.activation(ps_[:, :512], ps_[:, :512], AF.Relu,
                                       bias=thirty[:, 0:1], scale=-1.0),
                         reads=[(ps_, ""), (thirty, "")], writes=[(ps_, "")])
                    p.op("scalar", lambda sc, ps_=ps_, pb_=pb_:
                         sc.activation(pb_[:, :512], ps_[:, :512], AF.Exp,
                                       bias=thirty[:, 0:1], scale=-1.0),
                         reads=[(ps_, ""), (thirty, "")], writes=[(pb_, "")])
                    p.op("vector", lambda v, pb_=pb_, mb=mb:
                         v.tensor_mul(pb_[:, :512], pb_[:, :512],
                                      mb[:, :512]),
                         reads=[(pb_, ""), (mb, "")], writes=[(pb_, "")])
                    lv = vta[:, (kc * 16 + h) * 65:(kc * 16 + h) * 65 + 65]
                    pv_tok = p.op("tensor",
                                  lambda t, lv=lv, pb_=pb_, pv=pv,
                                  st=(kc == 0), en=(kc == 5):
                                  t.matmul(pv[0:65, :512], lv, pb_[:, :512],
                                           start=st, stop=en),
                                  reads=[(vta, kc), (vta, "ones"), (pb_, "")],
                                  writes=[(pv, "")] if kc == 0 else [])
                p.regions[_rk((pv, ""))] = [pv_tok, []]
                t_r = p.op("vector", lambda v, pv=pv:
                           v.reciprocal(rsum[0:1, :512], pv[64:65, :512]),
                           reads=[(pv, "")], writes=[(rsum, "")])
                t_bc = K.matmul_acc(
                    pb5[0:64, :512],
                    [(K.ones_row_f[0:1, 0:64], rsum[0:1, :512],
                      [(rsum, ""), (K.ones_row_f, "")])], (pb5, ""))
                t_cp = p.op("vector", lambda v:
                            v.tensor_copy(rb[0:64, :512], pb5[0:64, :512]),
                            deps=[t_bc], reads=[(pb5, "")],
                            writes=[(rb, "")])
                dst = hbl[r0:r0 + 64, c0h * 512:(c0h + 1) * 512]
                p.op("vector", lambda v, dst=dst, pv=pv:
                     v.tensor_mul(dst, pv[0:64, :512], rb[0:64, :512]),
                     deps=[t_cp], reads=[(pv, ""), (rb, "")],
                     writes=[(hbl, "")])

            for m in range(8):
                K.wstream_load(wbufs[m % 2], wo_d, m, 8, f"dwo{m % 2}")
                ps_ = K.pb[m % 2]
                pairs = [(wbufs[m % 2][:, k * 128:(k + 1) * 128],
                          hbl[:, k * 512:(k + 1) * 512],
                          [(wbufs[m % 2], ""), (hbl, "")]) for k in range(8)]
                K.matmul_acc(ps_[:, :512], pairs, (ps_, ""))
                ob = oby[m % 2]
                ysl = y_sb[:, m * TE + HLO + s * 512:
                           m * TE + HLO + (s + 1) * 512]
                ta = p.op("vector", lambda v, ob=ob, ps_=ps_, ysl=ysl:
                          v.tensor_add(ob[:, :512], ps_[:, :512], ysl),
                          reads=[(ps_, ""), (y_sb, "")], writes=[(ob, "")])
                p.dma("sync", lambda e, m=m, s=s, ob=ob: e.dma_start(
                    out=yo_o[:, m * T + s * 512: m * T + (s + 1) * 512],
                    in_=ob[:, :512]),
                    sem=f"doy{m % 2}", deps=[ta], reads=[(ob, "")])
    p.barrier_all()



def host_loc_tables(cols_glob, q_scale=None, k_scale=None):
    DIM_HEAD, WINDOW, N = 64, 256, 8192
    half = np.arange(0, DIM_HEAD, 2, dtype=np.float64)
    inv_freq = 1.0 / 10000.0 ** (half / DIM_HEAD)
    t = cols_glob.astype(np.float64)
    fr = np.outer(t, inv_freq)
    freqs = np.concatenate([fr, fr], -1)
    base_scale = (half + 0.4 * DIM_HEAD) / (1.4 * DIM_HEAD)
    power = (t - N // 2) / (WINDOW // 2)
    scale = base_scale[None, :] ** power[:, None]
    scale = np.concatenate([scale, scale], -1)
    cos, sin = np.cos(freqs), np.sin(freqs)
    qa = cos * scale
    qb = sin * scale
    ka = cos / scale
    kb = sin / scale
    tile16 = lambda a: np.tile(a, (1, 16)).astype(np.float32)
    return tile16(qa), tile16(qb), tile16(ka), tile16(kb)



def host_loc_masks(T, t_off, batch_start):
    NS = T // 512
    ma = np.zeros((NS * 4, 128, 384), np.float32)
    mb = np.zeros((NS * 6, 128, 512), np.float32)
    for s in range(NS):
        for qt in range(4):
            q = t_off + s * 512 + qt * 128 + np.arange(128)[:, None]
            k = t_off - 256 + s * 512 + qt * 128 + np.arange(384)[None, :]
            ok = (k >= q - 256) & (k <= q) & (k >= batch_start)
            ma[s * 4 + qt] = np.where(ok, 0.0, -3e38)
        for kc in range(6):
            k = t_off - 256 + s * 512 + kc * 128 + np.arange(128)[:, None]
            q = t_off + s * 512 + np.arange(512)[None, :]
            ok = (k >= q - 256) & (k <= q) & (k >= batch_start)
            mb[s * 6 + kc] = np.where(ok, 1.0, 0.0)
    return ma, mb


def build_loc_ffn(T):
    TE = T + 1
    K = KB(T=T)
    io = {}
    io["y"] = K.dram_in("y", [128, 8 * TE], F32)
    io["gb"] = K.dram_in("gb", [128, 16], F32)
    io["w1a"] = K.dram_in("w1a", [22, 8, 128, 128], F32)
    io["w1g"] = K.dram_in("w1g", [22, 8, 128, 128], F32)
    io["b1a"] = K.dram_in("b1a", [128, 22], F32)
    io["b1g"] = K.dram_in("b1g", [128, 22], F32)
    io["w2"] = K.dram_in("w2", [8, 22, 128, 128], F32)
    io["b2"] = K.dram_in("b2", [128, 8], F32)
    io["yo"] = K.dram_out("yo", [128, 8 * T], F32)
    with K.es:
        K.setup_common(npb=5, stats=False)
        emit_loc_ffn(K, io, scope="lf")
        K.p.finalize()
        K.p.emit()
    return K


def emit_loc_ffn(K, io, scope):
    T = K.T
    p = K.p
    TE = T + 1
    NC = T // 512
    y_d, gb_d = io["y"], io["gb"]
    w1a_d, w1g_d, b1a_d, b1g_d = io["w1a"], io["w1g"], io["b1a"], io["b1g"]
    w2_d, b2_d, yo_o = io["w2"], io["b2"], io["yo"]
    with K.scope(scope):
        K.pstat1 = K.ps("pstat1", [1, 512], F32)
        K.pstat2 = K.ps("pstat2", [1, 512], F32)
        y_sb = K.sb("y_sb", [128, 8 * TE], F32)
        xnb = K.sb("xnb", [128, 8 * 512], BF16)
        ab = K.sb("ab", [128, 22 * 512], BF16)
        gel = K.sb("gel", [128, 512], BF16)
        oby = [K.sb(f"oby{i}", [128, 512], F32) for i in range(2)]
        sq = K.sb("sq", [128, 1024], F32)
        st = [K.sb(f"st{i}", [1, 512], F32) for i in range(3)]
        gb = K.sb("gbs", [128, 16], F32)
        b1a = K.sb("b1as", [128, 22], F32)
        b1g = K.sb("b1gs", [128, 22], F32)
        b2 = K.sb("b2s", [128, 8], F32)
        wb = [K.sb(f"wb{i}", [128, 22 * 128], BF16) for i in range(2)]

        p.dma("sync", lambda e: e.dma_start(out=y_sb[:], in_=y_d),
              sem="dy", writes=[(y_sb, "")])
        for ci, (t_, d_) in enumerate(((gb, gb_d), (b1a, b1a_d),
                                       (b1g, b1g_d), (b2, b2_d))):
            p.dma("sync", lambda e, t_=t_, d_=d_: e.dma_start(out=t_[:],
                                                              in_=d_),
                  sem=f"dconst{ci}", writes=[(t_, "")])

        offs = [1, 1, 1, 1, 0, 0, 0, 0]
        for tc in range(NC):
            t0 = tc * 512
            K.emit_ln_fm(y_sb, xnb, 8, 512, F32, gamma_beta=(gb, 0),
                         sq_sb=sq, stat_sb=st, pbase=0,
                         offs=[o + 1 + t0 - 1 for o in offs], x_stride=TE,
                         out_stride=512, out_off=0)
            for m in range(22):
                K.wstream_load(wb[m % 2], w1a_d, m, 8, f"dwa{m % 2}")
                ps_ = K.pb[2 + m % 2]
                pairs = [(wb[m % 2][:, k * 128:(k + 1) * 128],
                          xnb[:, k * 512:(k + 1) * 512],
                          [(wb[m % 2], ""), (xnb, "")]) for k in range(8)]
                K.matmul_acc(ps_[:, :512], pairs, (ps_, ""))
                p.op("scalar", lambda s, ps_=ps_, m=m:
                     s.activation(ab[:, m * 512:(m + 1) * 512], ps_[:, :512],
                                  AF.Identity, bias=b1a[:, m:m + 1]),
                     reads=[(ps_, ""), (b1a, "")], writes=[(ab, m)])
            for m in range(22):
                K.wstream_load(wb[m % 2], w1g_d, m, 8, f"dwa{m % 2}")
                ps_ = K.pb[2 + m % 2]
                pairs = [(wb[m % 2][:, k * 128:(k + 1) * 128],
                          xnb[:, k * 512:(k + 1) * 512],
                          [(wb[m % 2], ""), (xnb, "")]) for k in range(8)]
                K.matmul_acc(ps_[:, :512], pairs, (ps_, ""))
                tg = p.op("scalar", lambda s, ps_=ps_, m=m:
                          s.activation(gel[:, :512], ps_[:, :512],
                                       AF.Gelu, bias=b1g[:, m:m + 1]),
                          reads=[(ps_, ""), (b1g, "")], writes=[(gel, "")])
                p.op("vector", lambda v, m=m:
                     v.tensor_mul(ab[:, m * 512:(m + 1) * 512],
                                  ab[:, m * 512:(m + 1) * 512], gel[:, :512]),
                     deps=[tg], reads=[(ab, m), (gel, "")], writes=[(ab, m)])
            for m in range(8):
                K.wstream_load(wb[m % 2], w2_d, m, 22, f"dwa{m % 2}")
                ps_ = K.pb[2 + m % 2]
                pairs = [(wb[m % 2][:, k * 128:(k + 1) * 128],
                          ab[:, k * 512:(k + 1) * 512],
                          [(wb[m % 2], ""), (ab, k)]) for k in range(22)]
                K.matmul_acc(ps_[:, :512], pairs, (ps_, ""))
                ob = oby[m % 2]
                ysl = y_sb[:, m * TE + 1 + t0: m * TE + 1 + t0 + 512]
                ta = p.op("vector", lambda v, ob=ob, ps_=ps_, ysl=ysl, m=m:
                          v.scalar_tensor_tensor(ob[:, :512], ps_[:, :512],
                                                 b2[:, m:m + 1], ysl,
                                                 ALU.add, ALU.add),
                          reads=[(ps_, ""), (b2, ""), (y_sb, "")],
                          writes=[(ob, "")])
                p.dma("sync", lambda e, m=m, tc=tc, ob=ob: e.dma_start(
                    out=yo_o[:, m * T + tc * 512: m * T + (tc + 1) * 512],
                    in_=ob[:, :512]),
                    sem=f"doy{m % 2}", deps=[ta], reads=[(ob, "")])
    p.barrier_all()


def build_final(T):
    K = KB(T=T)
    io = {}
    io["a"] = K.dram_in("a", [128, 8 * T], F32)
    io["b"] = K.dram_in("b", [128, 8 * T], F32)
    io["o"] = K.dram_out("o", [128, 8 * T], F32)
    with K.es:
        emit_final(K, io)
        K.p.finalize()
        K.p.emit()
    return K


def emit_final(K, io):
    T = K.T
    p = K.p
    a_d, b_d, o_d = io["a"], io["b"], io["o"]
    with K.scope("fz"):
        at = K.sb("at", [128, 8 * T], F32)
        bt = K.sb("bt", [128, 8 * T], F32)
        p.dma("sync", lambda e: e.dma_start(out=at[:], in_=a_d), sem="da",
              writes=[(at, "")])
        p.dma("sync", lambda e: e.dma_start(out=bt[:], in_=b_d), sem="db",
              writes=[(bt, "")])
        for m in range(8):
            ta = p.op("vector", lambda v, m=m: v.tensor_add(
                at[:, m * T:(m + 1) * T], at[:, m * T:(m + 1) * T],
                bt[:, m * T:(m + 1) * T]),
                reads=[(at, ""), (bt, "")], writes=[(at, "")])
            p.dma("sync", lambda e, m=m: e.dma_start(
                out=o_d[:, m * T:(m + 1) * T], in_=at[:, m * T:(m + 1) * T]),
                sem=f"do{m % 2}", deps=[ta], reads=[(at, "")])
    p.barrier_all()



NCORES = 8
TLOC = 2048
SFULL = 8192

_build_cache = {}


def _get(name, builder, *args):
    if name not in _build_cache:
        _build_cache[name] = builder(*args)
    return _build_cache[name]


def _run(K, in_maps):
    res = run_bass_kernel_spmd(K.nc, in_maps, core_ids=list(range(NCORES)))
    return res.results


def kernel(**inputs):
    if FUSED:
        return kernel_fused(inputs)
    return kernel_multi(**inputs)


def kernel_multi(L0_attn_X, L0_attn_Wq, L0_attn_Wk, L0_attn_Wv, L0_attn_Wo,
           L0_attn_We, L0_ffn_shared_w1, L0_ffn_shared_w2, L0_ffn_expert_w1,
           L0_ffn_expert_w2, L0_ffn_router_w, L0_ffn_ln_gamma, L0_ffn_ln_beta,
           L1_attn_Wq, L1_attn_Wk, L1_attn_Wv, L1_attn_Wo, L1_ffn_Wqkv,
           L1_ffn_q_scale, L1_ffn_k_scale, L1_ffn_Wo, L1_ffn_ln_g,
           L1_ffn_ln_b, L1_ffn_W1, L1_ffn_b1, L1_ffn_W2, L1_ffn_b2):
    bf16 = ml_dtypes.bfloat16
    f32 = np.float32
    T, S = TLOC, SFULL
    X = np.asarray(L0_attn_X, f32)

    def gb_cols(g, b):
        out = np.zeros((128, 16), f32)
        out[:, 0:8] = np.asarray(g, f32).reshape(8, 128).T
        out[:, 8:16] = np.asarray(b, f32).reshape(8, 128).T
        return out

    x_sh = []
    for c in range(NCORES):
        b, t0 = c // 4, (c % 4) * T
        x_sh.append(host_fm(X[b, t0:t0 + T]).reshape(128, 8 * T))

    K1 = _get("pre0", build_pre0, T)
    wq0 = host_tile_w(np.asarray(L0_attn_Wq, f32).T)
    wk0 = host_tile_w(np.asarray(L0_attn_Wk, f32).T)
    wv0 = host_tile_w(np.asarray(L0_attn_Wv, f32).T)
    we0 = host_tile_w(np.asarray(L0_attn_We, f32).T)
    gb0 = gb_cols(L0_ffn_ln_gamma, L0_ffn_ln_beta)
    r1 = _run(K1, [dict(x=x_sh[c], wq=wq0, wk=wk0, wv=wv0, we=we0, gb=gb0)
                   for c in range(NCORES)])

    K2 = _get("attn0", build_attn0, T, S)
    wo0 = host_tile_w(np.asarray(L0_attn_Wo, f32).T)
    kf = [np.concatenate([r1[g * 4 + i]["k"] for i in range(4)], axis=2)
          for g in range(2)]
    vf = [np.concatenate([r1[g * 4 + i]["v"] for i in range(4)], axis=2)
          for g in range(2)]
    cf = [np.concatenate([r1[g * 4 + i]["ctx"] for i in range(4)], axis=2)
          for g in range(2)]
    r2 = _run(K2, [dict(q=r1[c]["q"], kf=kf[c // 4], vf=vf[c // 4],
                        cf=cf[c // 4], cq=r1[c]["ctx"], wo=wo0, x=x_sh[c])
                   for c in range(NCORES)])

    K3 = _get("moe0", build_moe0, T)
    w1n = np.concatenate([np.asarray(L0_ffn_shared_w1, f32)[None],
                          np.asarray(L0_ffn_expert_w1, f32)]
                         ).reshape(9, 16, 128, 1024)
    w2n = np.concatenate([np.asarray(L0_ffn_shared_w2, f32)[None],
                          np.asarray(L0_ffn_expert_w2, f32)]
                         ).reshape(9, 8, 128, 2048)
    wr = host_tile_w(np.asarray(L0_ffn_router_w, f32).T)
    r3 = _run(K3, [dict(h=r1[c]["h"], xa=r2[c]["xa"], w1n=w1n, w2n=w2n, wr=wr)
                   for c in range(NCORES)])
    x1 = [r3[c]["x1"] for c in range(NCORES)]

    K4 = _get("pre1", build_pre1, T)
    wq1 = host_rhs(np.asarray(L1_attn_Wq, f32).T)
    wk1 = host_rhs(np.asarray(L1_attn_Wk, f32).T)
    wv1 = host_tile_w(np.asarray(L1_attn_Wv, f32).T)
    DK = 64
    inv = 1.0 / 10000.0 ** (np.arange(0, DK, 2, dtype=np.float64) / DK)
    tt = np.arange(S, dtype=np.float64)
    fr = np.outer(tt, inv)
    cosS, sinS = np.cos(fr), np.sin(fr)
    idx = (np.arange(1024) % 64) // 2
    in4 = []
    for c in range(NCORES):
        t0 = (c % 4) * T
        in4.append(dict(x=x1[c], wq=wq1, wk=wk1, wv=wv1,
                        cosE=cosS[t0:t0 + T][:, idx].astype(f32),
                        sinE=sinS[t0:t0 + T][:, idx].astype(f32)))
    r4 = _run(K4, in4)

    K5 = _get("attn1", build_attn1, T, S)
    wo1 = host_tile_w(np.asarray(L1_attn_Wo, f32).T)
    kf1 = [np.concatenate([r4[g * 4 + i]["k"] for i in range(4)], axis=2)
           for g in range(2)]
    vf1 = [np.concatenate([r4[g * 4 + i]["v"] for i in range(4)], axis=2)
           for g in range(2)]
    in5 = []
    for c in range(NCORES):
        t0 = (c % 4) * T
        kpos = np.arange(S).reshape(64, 128)[:, :, None]
        qpos = (t0 + np.arange(T).reshape(4, 512))[:, None, None, :]
        cm = (kpos[None] <= qpos).astype(bf16)
        in5.append(dict(q=r4[c]["q"], kf=kf1[c // 4], vf=vf1[c // 4],
                        cmask=cm, wo=wo1, x=x1[c]))
    r5 = _run(K5, in5)
    xacc = [r5[c]["xa"] for c in range(NCORES)]

    KA = _get("loc_attn", build_loc_attn, T)
    KF = _get("loc_ffn", build_loc_ffn, T)
    y = [np.ascontiguousarray(
        np.asarray(r4[c]["xn"], bf16).astype(f32).reshape(128, 8 * T))
        for c in range(NCORES)]
    HLO = 257
    qsl = np.asarray(L1_ffn_q_scale, f32)
    ksl = np.asarray(L1_ffn_k_scale, f32)
    tabs_c = []
    masks_c = []
    for c in range(NCORES):
        t0 = (c % 4) * T
        cols_glob = np.arange(t0 - HLO, t0 + T)
        tabs_c.append(host_loc_tables(cols_glob, None, None))
        ma, mb = host_loc_masks(T, t0, 0)
        masks_c.append((ma.astype(bf16), mb.astype(bf16)))
    tile16 = lambda a: np.ascontiguousarray(
        np.tile(np.tile(a.astype(f32), 16)[None, :], (128, 1)))
    for i in range(6):
        wqkv = host_rhs(np.asarray(L1_ffn_Wqkv, f32)[i])
        woL = host_tile_w(np.asarray(L1_ffn_Wo, f32)[i])
        qscl = tile16(8.0 * qsl[i])
        kscl = tile16(ksl[i])
        inA = []
        for c in range(NCORES):
            yv = y[c].reshape(128, 8, T)
            if c % 4 == 0:
                halo = np.zeros((128, 8, HLO), f32)
            else:
                halo = y[c - 1].reshape(128, 8, T)[:, :, T - HLO:]
            y_ext = np.concatenate([halo, yv], axis=2).reshape(128, 8 * (T + HLO))
            qa, qb, ka, kb = tabs_c[c]
            ma, mb = masks_c[c]
            inA.append(dict(y=y_ext, wqkv=wqkv, wo=woL,
                            qa=qa, qb=qb, ka=ka, kb=kb,
                            qscl=qscl, kscl=kscl,
                            maskA=ma, maskB=mb))
        rA = _run(KA, inA)
        y = [np.ascontiguousarray(np.asarray(rA[c]["yo"], f32))
             for c in range(NCORES)]

        gbL = gb_cols(np.asarray(L1_ffn_ln_g, f32)[i],
                      np.asarray(L1_ffn_ln_b, f32)[i])
        w1a = host_tile_w(np.asarray(L1_ffn_W1, f32)[i][:, :2730])
        w1g = host_tile_w(np.asarray(L1_ffn_W1, f32)[i][:, 2730:])
        b1a = host_bias_cols(np.asarray(L1_ffn_b1, f32)[i][:2730])
        b1g = host_bias_cols(np.asarray(L1_ffn_b1, f32)[i][2730:])
        w2L = host_tile_w(np.asarray(L1_ffn_W2, f32)[i])
        b2L = host_bias_cols(np.asarray(L1_ffn_b2, f32)[i])
        inF = []
        for c in range(NCORES):
            yv = y[c].reshape(128, 8, T)
            if c % 4 == 0:
                halo = np.zeros((128, 8, 1), f32)
            else:
                halo = y[c - 1].reshape(128, 8, T)[:, :, T - 1:]
            y_ext = np.concatenate([halo, yv], axis=2).reshape(128, 8 * (T + 1))
            inF.append(dict(y=y_ext, gb=gbL, w1a=w1a, w1g=w1g, b1a=b1a,
                            b1g=b1g, w2=w2L, b2=b2L))
        rF = _run(KF, inF)
        y = [np.ascontiguousarray(np.asarray(rF[c]["yo"], f32))
             for c in range(NCORES)]

    KZ = _get("final", build_final, T)
    rZ = _run(KZ, [dict(a=xacc[c], b=y[c]) for c in range(NCORES)])

    out = np.zeros((2, S, 1024), f32)
    for c in range(NCORES):
        b, t0 = c // 4, (c % 4) * T
        out[b, t0:t0 + T] = host_unfm(
            np.asarray(rZ[c]["o"], f32).reshape(128, 8, T))
    return out



def build_fused(T, S):
    K = KB(T=T)
    nc = K.nc
    p = K.p
    HLO = 257
    TE = T + HLO

    def scr(name, shape, dtype):
        return nc.dram_tensor(name, list(shape), dtype, kind="Internal").ap()

    ein = {}

    def din(name, shape, dtype=F32):
        ein[name] = (shape, dtype)
        return K.dram_in(name, list(shape), dtype)

    x_d = din("z_x", [128, 8 * T])
    out_d = K.dram_out("z_out", [128, 8 * T], F32)
    io0 = dict(x=x_d,
               wq=din("p0_wq", [8, 8, 128, 128]),
               wk=din("p0_wk", [8, 8, 128, 128]),
               wv=din("p0_wv", [8, 8, 128, 128]),
               we=din("p0_we", [1, 8, 128, 128]),
               gb=din("p0_gb", [128, 16]),
               q=scr("s_q0", [128, 8, T], BF16),
               k=scr("s_k0", [128, 8, T], BF16),
               v=scr("s_v0", [128, 8, T], BF16),
               ctx=scr("s_c0", [128, 1, T], BF16),
               h=scr("s_h0", [128, 8, T], BF16))
    k0g = scr("s_k0g", [512, 8 * T], BF16)
    v0g = scr("s_v0g", [512, 8 * T], BF16)
    c0g = scr("s_c0g", [512, T], BF16)
    xa_s = scr("s_xa", [128, 8 * T], F32)
    io2 = dict(q=io0["q"], kf=k0g, vf=v0g, cf=c0g, cq=io0["ctx"],
               wo=din("a0_wo", [8, 8, 128, 128]), x=x_d, xa=xa_s)
    x1_s = scr("s_x1", [128, 8 * T], F32)
    io3 = dict(h=io0["h"], xa=xa_s,
               w1n=din("mo_w1n", [9, 16, 128, 1024]),
               w2n=din("mo_w2n", [9, 8, 128, 2048]),
               wr=din("mo_wr", [1, 8, 128, 128]),
               x1=x1_s)
    xnf_s = scr("s_xnf", [128, 8 * T], F32)
    io4 = dict(x=x1_s,
               wq=din("p1_wq", [128, 8, 1024]),
               wk=din("p1_wk", [128, 8, 1024]),
               wv=din("p1_wv", [8, 8, 128, 128]),
               cosE=din("p1_cos", [T, 1024]),
               sinE=din("p1_sin", [T, 1024]),
               xn=scr("s_xn1", [128, 8, T], BF16),
               q=scr("s_q1", [128, 8, T], BF16),
               k=scr("s_k1", [128, 8, T], BF16),
               v=scr("s_v1", [128, 8, T], BF16),
               xnf=xnf_s)
    k1g = scr("s_k1g", [512, 8 * T], BF16)
    v1g = scr("s_v1g", [512, 8 * T], BF16)
    xacc_s = scr("s_xacc", [128, 8 * T], F32)
    io5 = dict(q=io4["q"], kf=k1g, vf=v1g,
               cmask=din("a1_cm", [T // 512, S // 128, 128, 512], BF16),
               wo=din("a1_wo", [8, 8, 128, 128]), x=x1_s, xa=xacc_s)
    lt = dict(qa=din("lt_qa", [TE, 1024]), qb=din("lt_qb", [TE, 1024]),
              ka=din("lt_ka", [TE, 1024]), kb=din("lt_kb", [TE, 1024]),
              maskA=din("lt_ma", [(T // 512) * 4, 128, 384], BF16),
              maskB=din("lt_mb", [(T // 512) * 6, 128, 512], BF16))
    hsel_d = din("lt_hsel", [128, 4])
    locA = []
    locF = []
    for i in range(6):
        locA.append(dict(wqkv=din(f"l{i}_wqkv", [128, 8, 3072]),
                         wo=din(f"l{i}_wo", [8, 8, 128, 128]),
                         qscl=din(f"l{i}_qscl", [128, 1024]),
                         kscl=din(f"l{i}_kscl", [128, 1024])))
        locF.append(dict(gb=din(f"l{i}_gb", [128, 16]),
                         w1a=din(f"l{i}_w1a", [22, 8, 128, 128]),
                         w1g=din(f"l{i}_w1g", [22, 8, 128, 128]),
                         b1a=din(f"l{i}_b1a", [128, 22]),
                         b1g=din(f"l{i}_b1g", [128, 22]),
                         w2=din(f"l{i}_w2", [8, 22, 128, 128]),
                         b2=din(f"l{i}_b2", [128, 8])))
    y_a = scr("s_ya", [128, 8 * T], F32)
    y_b = scr("s_yb", [128, 8 * T], F32)
    yxA = scr("s_yxA", [128, 8 * TE], F32)
    yxF = scr("s_yxF", [128, 8 * (T + 1)], F32)
    ccinE = scr("s_ccinE", [128, 8 * HLO], F32)
    ccoutE = scr("s_ccoutE", [512, 8 * HLO], F32)
    ccinF = scr("s_ccinF", [128, 8], F32)
    ccoutF = scr("s_ccoutF", [512, 8], F32)
    groups = [[0, 1, 2, 3], [4, 5, 6, 7]]

    def cc_gather(cin2d, cout, nm):
        p.op("gpsimd", lambda g: g.collective_compute(
            "AllGather", ALU.bypass, replica_groups=groups,
            ins=[cin2d], outs=[cout]), tag=nm)
        p.barrier_all()

    def halo_exchange(y_cur, y_ext, H, scope_nm):
        cin = ccinE if H > 1 else ccinF
        cout = ccoutE if H > 1 else ccoutF
        with K.scope(scope_nm):
            hbuf = K.sb("hbuf", [128, 8 * H], F32)
            hacc = K.sb("hacc", [128, 8 * H], F32)
            hsel = K.sb("hsel", [128, 4], F32)
            p.dma("sync", lambda e: e.dma_start(out=hsel[:], in_=hsel_d),
                  sem="dhsel", writes=[(hsel, "")])
            p.dma("sync", lambda e: e.dma_start(
                out=y_ext.rearrange("p (c t) -> p c t", t=T + H)[:, :, H:],
                in_=y_cur.rearrange("p (c t) -> p c t", t=T)), sem="dyx")
            with K.nc.allow_non_contiguous_dma(reason="tiny halo edge"):
                p.dma("sync", lambda e: e.dma_start(
                    out=cin.rearrange("p (c t) -> p c t", t=H),
                    in_=y_cur.rearrange("p (c t) -> p c t", t=T)[:, :, T - H:],
                    allow_slow_non_contiguous=(H == 1)),
                    sem="dyE")
            p.barrier_all()
            p.op("gpsimd", lambda g: g.collective_compute(
                "AllGather", ALU.bypass, replica_groups=groups,
                ins=[cin], outs=[cout]))
            p.barrier_all()
            tz = p.op("gpsimd", lambda g: g.memset(hacc[:, :8 * H], 0.0),
                      writes=[(hacc, "")])
            for r in range(4):
                tl = p.dma("sync", lambda e, r=r: e.dma_start(
                    out=hbuf[:, :8 * H],
                    in_=cout[r * 128:(r + 1) * 128, :]),
                    sem="dhb", writes=[(hbuf, "")])
                p.op("vector", lambda v, r=r:
                     v.scalar_tensor_tensor(hacc[:, :8 * H], hbuf[:, :8 * H],
                                            hsel[:, r:r + 1], hacc[:, :8 * H],
                                            ALU.mult, ALU.add),
                     deps=[tl], reads=[(hbuf, ""), (hsel, ""), (hacc, "")],
                     writes=[(hacc, "")])
            with K.nc.allow_non_contiguous_dma(reason="tiny halo cols"):
                td = p.dma("sync", lambda e: e.dma_start(
                    out=y_ext.rearrange("p (c t) -> p c t", t=T + H)[:, :, 0:H],
                    in_=hacc[:].rearrange("p (c t) -> p c t", t=H),
                    allow_slow_non_contiguous=(H == 1)),
                    sem="dyH", reads=[(hacc, "")])
        p.barrier_all()

    with K.es:
        K.setup_common(npb=5, stats=False)
        emit_pre0(K, io0)
        cc_gather(io0["k"].rearrange("p m t -> p (m t)"), k0g, "cck0")
        cc_gather(io0["v"].rearrange("p m t -> p (m t)"), v0g, "ccv0")
        cc_gather(io0["ctx"].rearrange("p m t -> p (m t)"), c0g, "ccc0")
        emit_attn(K, io2, S, influence=True, masked=False, scope="a0",
                  gathered=True)
        emit_moe0(K, io3)
        emit_pre1(K, io4)
        cc_gather(io4["k"].rearrange("p m t -> p (m t)"), k1g, "cck1")
        cc_gather(io4["v"].rearrange("p m t -> p (m t)"), v1g, "ccv1")
        emit_attn(K, io5, S, influence=False, masked=True, scope="a1",
                  gathered=True)
        y_cur, y_nxt = xnf_s, y_a
        for i in range(6):
            halo_exchange(y_cur, yxA, HLO, f"hxA{i}")
            ioA = dict(y=yxA, yo=y_nxt, **lt, **locA[i])
            emit_loc_attn(K, ioA, scope=f"la{i}")
            y_cur, y_nxt = y_nxt, (y_b if y_nxt is y_a else y_a)
            halo_exchange(y_cur, yxF, 1, f"hxF{i}")
            ioF = dict(y=yxF, yo=y_nxt, **locF[i])
            emit_loc_ffn(K, ioF, scope=f"lf{i}")
            y_cur, y_nxt = y_nxt, (y_b if y_nxt is y_a else y_a)
        emit_final(K, dict(a=xacc_s, b=y_cur, o=out_d))
        K.p.finalize()
        K.p.emit()
    return K


FUSED = True


def kernel_fused(ins):
    bf16 = ml_dtypes.bfloat16
    f32 = np.float32
    T, S, HLO = TLOC, SFULL, 257
    X = np.asarray(ins["L0_attn_X"], f32)

    def gb_cols(g, b):
        out = np.zeros((128, 16), f32)
        out[:, 0:8] = np.asarray(g, f32).reshape(8, 128).T
        out[:, 8:16] = np.asarray(b, f32).reshape(8, 128).T
        return out

    KF = _get("fused", build_fused, T, S)

    shared = {
        "p0_wq": host_tile_w(np.asarray(ins["L0_attn_Wq"], f32).T),
        "p0_wk": host_tile_w(np.asarray(ins["L0_attn_Wk"], f32).T),
        "p0_wv": host_tile_w(np.asarray(ins["L0_attn_Wv"], f32).T),
        "p0_we": host_tile_w(np.asarray(ins["L0_attn_We"], f32).T),
        "p0_gb": gb_cols(ins["L0_ffn_ln_gamma"], ins["L0_ffn_ln_beta"]),
        "a0_wo": host_tile_w(np.asarray(ins["L0_attn_Wo"], f32).T),
        "mo_w1n": np.concatenate(
            [np.asarray(ins["L0_ffn_shared_w1"], f32)[None],
             np.asarray(ins["L0_ffn_expert_w1"], f32)]).reshape(
                 9, 16, 128, 1024),
        "mo_w2n": np.concatenate(
            [np.asarray(ins["L0_ffn_shared_w2"], f32)[None],
             np.asarray(ins["L0_ffn_expert_w2"], f32)]).reshape(
                 9, 8, 128, 2048),
        "mo_wr": host_tile_w(np.asarray(ins["L0_ffn_router_w"], f32).T),
        "p1_wq": host_rhs(np.asarray(ins["L1_attn_Wq"], f32).T),
        "p1_wk": host_rhs(np.asarray(ins["L1_attn_Wk"], f32).T),
        "p1_wv": host_tile_w(np.asarray(ins["L1_attn_Wv"], f32).T),
        "a1_wo": host_tile_w(np.asarray(ins["L1_attn_Wo"], f32).T),
    }
    tile16 = lambda a: np.ascontiguousarray(
        np.tile(np.tile(a.astype(f32), 16)[None, :], (128, 1)))
    qsl = np.asarray(ins["L1_ffn_q_scale"], f32)
    ksl = np.asarray(ins["L1_ffn_k_scale"], f32)
    for i in range(6):
        shared[f"l{i}_wqkv"] = host_rhs(np.asarray(ins["L1_ffn_Wqkv"], f32)[i])
        shared[f"l{i}_wo"] = host_tile_w(np.asarray(ins["L1_ffn_Wo"], f32)[i])
        shared[f"l{i}_qscl"] = tile16(8.0 * qsl[i])
        shared[f"l{i}_kscl"] = tile16(ksl[i])
        shared[f"l{i}_gb"] = gb_cols(np.asarray(ins["L1_ffn_ln_g"], f32)[i],
                                     np.asarray(ins["L1_ffn_ln_b"], f32)[i])
        shared[f"l{i}_w1a"] = host_tile_w(
            np.asarray(ins["L1_ffn_W1"], f32)[i][:, :2730])
        shared[f"l{i}_w1g"] = host_tile_w(
            np.asarray(ins["L1_ffn_W1"], f32)[i][:, 2730:])
        shared[f"l{i}_b1a"] = host_bias_cols(
            np.asarray(ins["L1_ffn_b1"], f32)[i][:2730])
        shared[f"l{i}_b1g"] = host_bias_cols(
            np.asarray(ins["L1_ffn_b1"], f32)[i][2730:])
        shared[f"l{i}_w2"] = host_tile_w(np.asarray(ins["L1_ffn_W2"], f32)[i])
        shared[f"l{i}_b2"] = host_bias_cols(np.asarray(ins["L1_ffn_b2"], f32)[i])

    DK = 64
    inv = 1.0 / 10000.0 ** (np.arange(0, DK, 2, dtype=np.float64) / DK)
    tt = np.arange(S, dtype=np.float64)
    fr = np.outer(tt, inv)
    cosS, sinS = np.cos(fr), np.sin(fr)
    idx = (np.arange(1024) % 64) // 2

    in_maps = []
    for c in range(NCORES):
        b, t0 = c // 4, (c % 4) * T
        im = dict(shared)
        im["z_x"] = host_fm(X[b, t0:t0 + T]).reshape(128, 8 * T)
        im["p1_cos"] = cosS[t0:t0 + T][:, idx].astype(f32)
        im["p1_sin"] = sinS[t0:t0 + T][:, idx].astype(f32)
        kpos = np.arange(S).reshape(64, 128)[:, :, None]
        qpos = (t0 + np.arange(T).reshape(4, 512))[:, None, None, :]
        im["a1_cm"] = (kpos[None] <= qpos).astype(bf16)
        cols_glob = np.arange(t0 - HLO, t0 + T)
        qa, qb, ka, kb = host_loc_tables(cols_glob)
        im["lt_qa"], im["lt_qb"], im["lt_ka"], im["lt_kb"] = qa, qb, ka, kb
        ma, mb = host_loc_masks(T, t0, 0)
        im["lt_ma"] = ma.astype(bf16)
        im["lt_mb"] = mb.astype(bf16)
        hs = np.zeros((128, 4), f32)
        if c % 4 > 0:
            hs[:, (c % 4) - 1] = 1.0
        im["lt_hsel"] = hs
        in_maps.append(im)

    res = run_bass_kernel_spmd(KF.nc, in_maps, core_ids=list(range(NCORES)))
    out = np.zeros((2, S, 1024), f32)
    for c in range(NCORES):
        b, t0 = c // 4, (c % 4) * T
        out[b, t0:t0 + T] = host_unfm(
            np.asarray(res.results[c]["z_out"], f32).reshape(128, 8, T))
    return out
